# version 01
# baseline (speedup 1.0000x reference)
import contextlib
import math

import numpy as np
import ml_dtypes

import concourse.bass as bass
import concourse.mybir as mybir
from concourse.bass_utils import run_bass_kernel_spmd

F32 = mybir.dt.float32
BF16 = mybir.dt.bfloat16
AF = mybir.ActivationFunctionType
ALU = mybir.AluOpType
AX = mybir.AxisListType

MAGIC = float(np.float32(1.5 * 2 ** 23))
D = 2048
NCH = 16
TALL = 2048
TOWN = 1024
HPC = 8
ISQ = 1.0 / math.sqrt(128.0)
KCH = 6
KTOK = [0, 128, 1536, 1664, 1792, 1920]

ENGS = ("sync", "gpsimd", "tensor", "vector", "scalar")

DBG = []
_CACHE = {}


class P(object):
    def __init__(self, nc, ctx):
        self.nc = nc
        self.ops = {e: [] for e in ENGS}
        self.sems = {}
        self.ctx = ctx

    def sem(self, name):
        self.sems[name] = [self.ctx.enter_context(self.nc.semaphore(name)), 0]

    def v(self, name):
        return self.sems[name][1]

    def h(self, name):
        return self.sems[name][0]

    def wait(self, eng, sname, val=None):
        h = self.h(sname)
        v = self.v(sname) if val is None else val
        if v > 0:
            self.ops[eng].append(lambda e, h=h, v=v: e.wait_ge(h, v))

    def ins(self, eng, meth, *args, inc=None, incn=1, **kw):
        pair = None
        if inc is not None:
            self.sems[inc][1] += incn
            pair = (self.h(inc), incn)

        def f(e, meth=meth, args=args, kw=kw, pair=pair):
            i = getattr(e, meth)(*args, **kw)
            if pair is not None:
                i.then_inc(*pair)

        self.ops[eng].append(f)


def _build():
    nc = bass.Bass("TRN2", target_bir_lowering=False, debug=False)

    def dram(n, s, d, k=None):
        t = nc.dram_tensor(n, s, d, kind=k) if k else nc.dram_tensor(n, s, d)
        return t.ap()

    xT = dram("xT", [D, TALL], F32, "ExternalInput")
    wqkv = dram("wqkv", [HPC, NCH, 128, 256], F32, "ExternalInput")
    wv0 = dram("wv0", [NCH, 128, 1024], F32, "ExternalInput")
    bqkv = dram("bqkv", [128, 24], F32, "ExternalInput")
    wo0 = dram("wo0", [NCH, HPC, 128, 128], F32, "ExternalInput")
    bo0 = dram("bo0", [128, 16], F32, "ExternalInput")
    w0t = dram("w0t", [NCH, NCH, 128, 128], F32, "ExternalInput")
    g0 = dram("g0", [128, 16], F32, "ExternalInput")
    wqk1 = dram("wqk1", [HPC, NCH, 128, 256], F32, "ExternalInput")
    wv1 = dram("wv1", [NCH, 128, 1024], F32, "ExternalInput")
    bqk1 = dram("bqk1", [128, 16], F32, "ExternalInput")
    bv1 = dram("bv1", [128, 8], F32, "ExternalInput")
    wo1 = dram("wo1", [NCH, HPC, 128, 128], F32, "ExternalInput")
    bo1 = dram("bo1", [128, 16], F32, "ExternalInput")
    wg1 = dram("wg1", [16, NCH, 128, 512], F32, "ExternalInput")
    wu1 = dram("wu1", [16, NCH, 128, 512], F32, "ExternalInput")
    wd1 = dram("wd1", [NCH, 64, 128, 128], F32, "ExternalInput")
    msel = dram("msel", [128, 2], F32, "ExternalInput")

    out_own = dram("out_own", [D, TOWN], F32, "ExternalOutput")

    xpre_d = dram("xpre_d", [D, TALL], BF16)
    gmax_i = dram("gmax_i", [1, 16], F32)
    gmax_o = dram("gmax_o", [8, 16], F32)
    cb_d = dram("cb_d", [D, TALL], BF16)
    ar_d = dram("ar_d", [D, TALL], BF16)
    x1_d = dram("x1_d", [D, TALL], BF16)
    x1o_d = dram("x1o_d", [D, TOWN], F32)
    xn2_d = dram("xn2_d", [D, TALL], BF16)
    p_d = dram("p_d", [8192, TOWN], BF16)
    wdc_d = dram("wdc_d", [NCH, 64, 128, 128], BF16)

    dbg = {}
    for name in DBG:
        shp = {"xn": [D, TALL], "xpre": [D, TALL], "gmax": [128, 4],
               "vt": [128, 16384], "q0": [128, TALL], "k0": [128, TALL],
               "oh": [128, 16384], "cb": [D, TALL], "arx": [D, TALL],
               "x1": [D, TALL], "xn2": [D, TALL], "iq": [D, TOWN],
               "k1": [128, 1024], "q1": [128, TALL], "sin": [64, TALL],
               "cos": [64, TALL], "rmax": [128, 16], "wsc": [128, 4],
               "i1": [D, TOWN], "oh1": [128, 16384], "mh": [128, 4],
               "gmi": [1, 16], "gmo": [8, 16], "lmax": [128, 8],
               "cb1": [D, TALL], "ar1": [D, TALL], "pd": [1024, TOWN],
               "wq0": [128, 1024], "fs": [128, 512], "osc": [128, 4],
               }[name]
        dbg[name] = dram("dbg_" + name, shp, BF16, "ExternalOutput")

    ctx = contextlib.ExitStack()
    with ctx:
        ARB = ctx.enter_context(nc.sbuf_tensor("ARB", [128, 86528], BF16))
        ARF = ctx.enter_context(nc.sbuf_tensor("ARF", [128, 8960], F32))
        PS = ctx.enter_context(nc.psum_tensor("PS", [128, 4096], F32))

        p = P(nc, ctx)
        for s in ("LX", "LW", "TE", "VE", "SE", "GE", "STO", "CC"):
            p.sem(s)

        def bv(kb, ncol, part=128, frac=0):
            c = kb * 512 + frac
            return ARB[0:part, c:c + ncol]

        def fv(kb, ncol, part=128, frac=0):
            c = kb * 256 + frac
            return ARF[0:part, c:c + ncol]

        def bank(b, w=512, part=128):
            return PS[0:part, b * 512:b * 512 + w]

        XN = lambda dc: bv(4 * dc, 2048)
        CC0 = lambda oc: bv(4 * oc, 2048)
        XBT = lambda dc: bv(64 + dc, 512)
        SQ0 = bv(80, 512)
        SQ1 = bv(81, 512)
        VT = lambda tc: bv(64 + 2 * tc, 1024)
        W0IQ = lambda ic: bv(64 + 2 * ic, 1024)
        XR = W0IQ
        PQ = lambda oc: bv(64 + oc, 512)
        VSL = lambda dc: bv(96 + 2 * dc, 1024)
        OH = lambda h: bv(96 + 4 * h, 2048)
        GSL = bv(96, 8192)
        USL = bv(112, 8192)
        WDS = lambda pr: bv(128 + 8 * pr, 4096)
        QH = lambda pr: bv(128 + 4 * pr, 2048)
        KH = lambda pr: bv(136 + 4 * pr, 2048)
        WOS = lambda pr: bv(128 + 2 * pr, 1024)
        AB = bv(144, 2048)
        ABW = bv(144, 1536)
        MKE = bv(148, 896)
        MKD = lambda d_: bv(148, 512, frac=384 - 128 * d_)
        ONESB = bv(150, 128)
        P64 = bv(150, 128, frac=128)
        STG = lambda i: bv(151 + i, 512)
        QKS = lambda pr: bv(153 + 8 * pr, 4096)
        RSC = lambda i: bv(153 + 2 * i, 1024)
        VBB = bv(153, 1024)

        LNT = [fv(0, 512), fv(2, 512), fv(4, 512)]
        SM = fv(6, 256)
        sm = lambda i: SM[:, i:i + 1]
        BIAS = fv(7, 256)
        WFA = lambda pr: fv(8, 2048, frac=pr * 2048)
        SINF = fv(8, 2048)
        COSF = fv(16, 2048)
        IDN = fv(24, 128)
        ONESF = fv(24, 128, frac=128)
        TRS = fv(25, 512)
        BCQ = lambda i: fv(27 + 2 * i, 512)
        XO1 = fv(27, 512)
        XO2 = fv(29, 512)

        B_QKV, B_BO0, B_G0, B_QK1, B_BV1, B_BO1, B_MS = 0, 24, 40, 56, 72, 80, 96
        mlo = BIAS[:, B_MS:B_MS + 1]
        mhi = BIAS[:, B_MS + 1:B_MS + 2]

        def dma(q, out, in_, sem_="LX", **kw):
            p.ins(q, "dma_start", inc=sem_, incn=16, out=out, in_=in_, **kw)

        def dma_tiles(q, dst2d, src3d, n, w, sem_="LW"):
            dma(q, dst2d, src3d.rearrange("n p w -> p n w"), sem_)

        def mm(lhsT, rhs, out, st, sp, inc=None):
            p.ins("tensor", "matmul", out, lhsT, rhs, start=st, stop=sp,
                  inc=inc)

        def vgate():
            p.ins("vector", "nop", inc="VE")
            p.wait("vector", "VE")

        p.ins("gpsimd", "memset", ONESB, 1.0)
        p.ins("gpsimd", "memset", ONESF, 1.0)
        p.ins("gpsimd", "memset", IDN, 0.0)
        p.ins("gpsimd", "affine_select", out=IDN, in_=IDN,
              compare_op=ALU.not_equal, fill=1.0, base=0,
              pattern=[[-1, 128]], channel_multiplier=1)
        p.ins("gpsimd", "memset", MKE, 1.0)
        p.ins("gpsimd", "affine_select", out=MKE, in_=MKE,
              compare_op=ALU.is_ge, fill=0.0, base=-384,
              pattern=[[1, 896]], channel_multiplier=-1)
        p.ins("gpsimd", "memset", P64, 0.0)
        p.ins("gpsimd", "affine_select", out=P64, in_=P64,
              compare_op=ALU.not_equal, fill=1.0, base=64,
              pattern=[[1, 128]], channel_multiplier=-1)
        p.ins("gpsimd", "affine_select", out=P64, in_=P64,
              compare_op=ALU.not_equal, fill=1.0, base=-64,
              pattern=[[1, 128]], channel_multiplier=-1)
        p.ins("gpsimd", "memset", sm(20), 1.0)
        p.ins("gpsimd", "affine_select", out=sm(20), in_=sm(20),
              compare_op=ALU.is_ge, fill=0.0, base=76,
              pattern=[[0, 1]], channel_multiplier=-1, inc="GE")

        dma("sync", BIAS[:, B_QKV:B_QKV + 24], bqkv[:, :])
        dma("sync", BIAS[:, B_BO0:B_BO0 + 16], bo0[:, :])
        dma("sync", BIAS[:, B_G0:B_G0 + 16], g0[:, :])
        dma("sync", BIAS[:, B_QK1:B_QK1 + 16], bqk1[:, :])
        dma("sync", BIAS[:, B_BV1:B_BV1 + 8], bv1[:, :])
        dma("sync", BIAS[:, B_BO1:B_BO1 + 16], bo1[:, :])
        dma("sync", BIAS[:, B_MS:B_MS + 2], msel[:, :])
        lx_bias = p.v("LX")
        p.wait("vector", "LX", lx_bias)
        p.wait("vector", "GE")
        for i, val in ((10, 1e-5), (11, 1e-6), (12, 1e-8), (13, 0.0),
                       (15, 0.0)):
            p.ins("vector", "memset", sm(i), val)
        p.ins("vector", "memset", SM[:, 24:28], 0.0, inc="VE")
        p.wait("tensor", "VE")
        p.wait("tensor", "GE")
        p.wait("scalar", "VE")
        p.wait("gpsimd", "VE")

        def tap_chunks(name, view, w=TALL):
            if name in dbg:
                for e in ("VE", "SE", "TE"):
                    p.wait("gpsimd", e)
                for dc in range(NCH):
                    dma("gpsimd", dbg[name][128 * dc:128 * dc + 128, 0:w],
                        view(dc), "STO")

        def tap_one(name, src, rows=128):
            if name in dbg:
                for e in ("VE", "SE", "TE"):
                    p.wait("gpsimd", e)
                dma("gpsimd", dbg[name][0:rows, 0:src.shape[1]], src, "STO")

        def bcast8(src8, dst):
            p.wait("vector", "TE")
            p.wait("vector", "SE")
            p.ins("vector", "memset", LNT[0][0:8, :], 0.0)
            p.ins("vector", "memset", LNT[1][0:8, :], 0.0, inc="VE")
            p.wait("sync", "VE")
            for j in range(8):
                t = LNT[0] if j < 4 else LNT[1]
                dma("sync", t[j:j + 1, (j % 4) * 128:(j % 4) * 128 + 128],
                    src8[:, j:j + 1], "LX")
            lxg = p.v("LX")
            p.wait("tensor", "LX", lxg)
            p.wait("tensor", "VE")
            mm(ONESF[0:8, :], LNT[0][0:8, :], bank(4), True, True)
            mm(ONESF[0:8, :], LNT[1][0:8, :], bank(5), True, True, inc="TE")
            p.wait("vector", "TE")
            p.ins("vector", "tensor_copy", dst[:, 0:512], bank(4))
            p.ins("vector", "tensor_copy", dst[:, 512:1024], bank(5), inc="VE")
            vgate()

        def pbcast(srccol, dstslot):
            p.wait("tensor", "VE")
            mm(ONESF[0:1, :], TRS[0:1, srccol:srccol + 1], bank(4, 1),
               True, True, inc="TE")
            p.wait("vector", "TE")
            p.ins("vector", "tensor_copy", sm(dstslot), bank(4, 1),
                  inc="VE")
            vgate()

        def layernorm(load, xdst, eps_slot, post=None):
            for t0 in (0, 512, 1024, 1536):
                p.wait("gpsimd", "VE")
                p.wait("sync", "VE")
                for dc in range(NCH):
                    load(dc, t0, XBT(dc))
                lx0, lw0 = p.v("LX"), p.v("LW")
                for e in ("tensor", "vector"):
                    p.wait(e, "LX", lx0)
                    p.wait(e, "LW", lw0)
                p.wait("tensor", "VE")
                for dc in range(NCH):
                    mm(ONESB, XBT(dc), bank(0), dc == 0, dc == NCH - 1,
                       inc="TE" if dc == NCH - 1 else None)
                te_marks = []
                for dc in range(NCH):
                    if dc >= 2:
                        p.wait("vector", "TE", te_marks[dc - 2])
                    sq = SQ0 if dc % 2 == 0 else SQ1
                    p.ins("vector", "tensor_mul", sq, XBT(dc), XBT(dc),
                          inc="VE")
                    p.wait("tensor", "VE")
                    mm(ONESB, sq, bank(1), dc == 0, dc == NCH - 1, inc="TE")
                    te_marks.append(p.v("TE"))
                p.wait("vector", "TE")
                p.ins("vector", "tensor_scalar", LNT[0], bank(0), 1.0 / D, 0.0,
                      ALU.mult, ALU.add)
                p.ins("vector", "tensor_scalar", LNT[1], bank(1), 1.0 / D, 0.0,
                      ALU.mult, ALU.add, inc="VE")
                vgate()
                p.ins("vector", "tensor_mul", LNT[2], LNT[0], LNT[0])
                p.ins("vector", "tensor_sub", LNT[1], LNT[1], LNT[2], inc="VE")
                p.wait("scalar", "VE")
                p.ins("scalar", "activation", LNT[2], LNT[1], AF.Sqrt,
                      bias=sm(eps_slot), scale=1.0, inc="SE")
                p.wait("vector", "SE")
                p.ins("vector", "reciprocal", LNT[1], LNT[2], inc="VE")
                p.wait("vector", "VE")
                p.ins("vector", "tensor_mul", LNT[2], LNT[0], LNT[1], inc="VE")
                for dc in range(NCH):
                    sq = SQ0 if dc % 2 == 0 else SQ1
                    p.ins("vector", "tensor_mul", sq, XBT(dc), LNT[1])
                    p.ins("vector", "tensor_sub", xdst(dc, t0), sq, LNT[2],
                          inc="VE")
                if post is not None:
                    post(t0)

        def load_x(dc, t0, dst):
            if dc == 0:
                dma_tiles("gpsimd", bv(64, 8192),
                          xT[:, t0:t0 + 512].rearrange("(n p) w -> n p w",
                                                       p=128), 16, 512, "LW")

        def l0_post(t0):
            ti = t0 // 512
            p.wait("tensor", "VE")
            sto_marks = []
            for dc in range(NCH):
                sq = SQ0 if dc % 2 == 0 else SQ1
                p.ins("vector", "tensor_mul", sq,
                      XN(dc)[:, t0:t0 + 512], XN(dc)[:, t0:t0 + 512],
                      inc="VE")
                p.wait("tensor", "VE")
                mm(ONESB, sq, bank(2), dc == 0, dc == NCH - 1, inc="TE")
            p.wait("vector", "TE")
            p.ins("vector", "tensor_scalar", LNT[0], bank(2), 1.0 / D, 0.0,
                  ALU.mult, ALU.add, inc="VE")
            vgate()
            p.wait("scalar", "VE")
            p.ins("scalar", "activation", LNT[1], LNT[0], AF.Sqrt,
                  bias=sm(11), scale=1.0, inc="SE")
            p.wait("vector", "SE")
            p.ins("vector", "reciprocal", LNT[0], LNT[1], inc="VE")
            p.wait("vector", "VE")
            for dc in range(NCH):
                i = dc % 2
                if dc >= 2:
                    p.wait("vector", "STO", sto_marks[dc - 2])
                p.ins("vector", "scalar_tensor_tensor", STG(i),
                      XN(dc)[:, t0:t0 + 512],
                      BIAS[:, B_G0 + dc:B_G0 + dc + 1], LNT[0],
                      ALU.mult, ALU.mult, inc="VE")
                p.ins("vector", "reduce_max", SM[:, 64 + 16 * ti + dc:
                                                 65 + 16 * ti + dc],
                      STG(i), axis=AX.X, apply_absolute_value=True,
                      inc="VE")
                p.wait("sync", "VE")
                dma("sync", xpre_d[128 * dc:128 * dc + 128, t0:t0 + 512],
                    STG(i), "STO")
                sto_marks.append(p.v("STO"))

        layernorm(load_x, lambda dc, t0: XN(dc)[:, t0:t0 + 512], 10,
                  post=l0_post)
        tap_chunks("xn", XN)

        p.ins("vector", "reduce_max", sm(28), SM[:, 64:96], axis=AX.X,
              inc="VE")
        p.ins("vector", "reduce_max", sm(29), SM[:, 96:128], axis=AX.X,
              inc="VE")
        vgate()
        p.ins("vector", "tensor_scalar", sm(30), sm(28), mlo, 0.0, ALU.mult,
              ALU.add)
        p.ins("vector", "tensor_scalar", sm(31), sm(29), mhi, 0.0, ALU.mult,
              ALU.add, inc="VE")
        vgate()
        p.ins("vector", "tensor_max", sm(28), sm(30), sm(31), inc="VE")
        vgate()
        p.ins("vector", "tensor_copy", TRS[:, 0:1], sm(28), inc="VE")
        p.wait("tensor", "VE")
        p.ins("tensor", "transpose", bank(4, 128), TRS[:, 0:128], IDN,
              inc="TE")
        p.wait("vector", "TE")
        p.ins("vector", "reduce_max", TRS[0:1, 130:131], bank(4, 128)[0:1, :],
              axis=AX.X, inc="VE")
        vgate()
        for j in range(16):
            p.ins("vector", "tensor_copy", TRS[0:1, 131 + j:132 + j],
                  TRS[0:1, 130:131])
        p.ins("vector", "nop", inc="VE")
        p.wait("sync", "VE")
        dma("sync", gmax_i[:, :], TRS[0:1, 131:147], "STO")
        if "lmax" in dbg:
            p.wait("gpsimd", "VE")
            dma("gpsimd", dbg["lmax"][:, 0:4], SM[:, 24:28], "STO")
            dma("gpsimd", dbg["lmax"][:, 4:5], sm(28), "STO",
                allow_slow_non_contiguous=True)
        if "gmi" in dbg:
            p.wait("gpsimd", "STO")
            dma("gpsimd", dbg["gmi"][:, :], gmax_i[:, :], "STO")
        p.wait("gpsimd", "STO")
        p.ins("gpsimd", "collective_compute", "AllGather", ALU.bypass,
              replica_groups=[list(range(8))],
              ins=[gmax_i.opt()], outs=[gmax_o.opt()], inc="CC")

        dma_tiles("gpsimd", bv(96, 16384), wv0[:, :, :], 16, 1024, "LW")
        lw_v = p.v("LW")
        bcast8(BIAS[:, B_QKV + 16:B_QKV + 24], VBB)
        p.wait("tensor", "LW", lw_v)
        for tc in range(NCH):
            for half in range(2):
                bk = bank(half)
                for dc in range(NCH):
                    mm(XN(dc)[:, 128 * tc:128 * tc + 128],
                       VSL(dc)[:, 512 * half:512 * half + 512],
                       bk, dc == 0, dc == NCH - 1,
                       inc="TE" if dc == NCH - 1 else None)
            p.wait("vector", "TE")
            p.ins("vector", "tensor_add", VT(tc)[:, 0:512], bank(0),
                  VBB[:, 0:512])
            p.ins("vector", "tensor_add", VT(tc)[:, 512:1024], bank(1),
                  VBB[:, 512:1024], inc="VE")
            p.wait("tensor", "VE")
        if "vt" in dbg:
            p.wait("gpsimd", "VE")
            for tc in range(NCH):
                dma("gpsimd", dbg["vt"][:, 1024 * tc:1024 * tc + 1024],
                    VT(tc), "STO")

        te_head_done = {}

        def qk_head(h, pr, wsrc, bcol_q, bcol_k, l1=False):
            if h >= 2:
                p.wait("gpsimd", "TE", te_head_done[h - 2])
            else:
                p.wait("gpsimd", "VE")
                p.wait("gpsimd", "TE")
            dma_tiles("gpsimd", QKS(pr), wsrc[h, :, :, :], 16, 256, "LW")
            lw0 = p.v("LW")
            p.wait("tensor", "LW", lw0)
            for tt in range(4):
                bk = bank(4 + (tt % 2))
                for dc in range(NCH):
                    mm(QKS(pr)[:, 256 * dc:256 * dc + 128],
                       XN(dc)[:, 512 * tt:512 * tt + 512],
                       bk, dc == 0, dc == NCH - 1,
                       inc="TE" if dc == NCH - 1 else None)
                p.wait("scalar", "TE")
                p.ins("scalar", "activation", QH(pr)[:, 512 * tt:512 * tt + 512],
                      bk, AF.Identity, bias=BIAS[:, bcol_q:bcol_q + 1],
                      scale=1.0, inc="SE")
                p.wait("tensor", "SE")
            kranges = (((0, 512), 0), ((512, 512), 512), ((1024, 512), 1024),
                       ((1536, 512), 1536)) if not l1 else \
                      (((0, 256), 0), ((1536, 512), 256))
            for (src0, w), dcol in kranges:
                bk2 = bank(6, w)
                for dc in range(NCH):
                    mm(QKS(pr)[:, 256 * dc + 128:256 * dc + 256],
                       XN(dc)[:, src0:src0 + w],
                       bk2, dc == 0, dc == NCH - 1,
                       inc="TE" if dc == NCH - 1 else None)
                p.wait("scalar", "TE")
                p.ins("scalar", "activation", KH(pr)[:, dcol:dcol + w],
                      bk2, AF.Identity, bias=BIAS[:, bcol_k:bcol_k + 1],
                      scale=1.0, inc="SE")
                p.wait("tensor", "SE")

        def mhat(pr, ktiles):
            for which, tiles in ((0, ((0, 512), (512, 512), (1024, 512),
                                      (1536, 512))), (1, ktiles)):
                src = QH(pr) if which == 0 else KH(pr)
                for i, (c0, w) in enumerate(tiles):
                    p.wait("vector", "SE")
                    p.wait("vector", "TE")
                    p.ins("vector", "tensor_mul", STG(0)[:, 0:w],
                          src[:, c0:c0 + w], src[:, c0:c0 + w], inc="VE")
                    p.wait("tensor", "VE")
                    mm(ONESB, STG(0)[:, 0:w], bank(6, w), True, True,
                       inc="TE")
                    p.wait("vector", "TE")
                    p.ins("vector", "reduce_max",
                          SM[:, 32 + 4 * which + i:33 + 4 * which + i],
                          bank(6, w), axis=AX.X, inc="VE")
                    vgate()
            nk = len(ktiles)
            p.ins("vector", "reduce_max", sm(0), SM[:, 32:36], axis=AX.X)
            p.ins("vector", "reduce_max", sm(1), SM[:, 36:36 + nk], axis=AX.X,
                  inc="VE")
            p.wait("scalar", "VE")
            p.ins("scalar", "activation", sm(0), sm(0), AF.Sqrt, bias=sm(13),
                  scale=1.0)
            p.ins("scalar", "activation", sm(1), sm(1), AF.Sqrt, bias=sm(13),
                  scale=1.0, inc="SE")
            p.wait("vector", "SE")
            p.ins("vector", "scalar_tensor_tensor", sm(2), sm(0), -ISQ,
                  sm(1), ALU.mult, ALU.mult, inc="VE")
            vgate()
            p.wait("scalar", "VE")

        def l0_attn(h, pr):
            for qt in range(4):
                depth = 4 * (qt + 1)
                for g in range((depth + 3) // 4):
                    c0, c1 = 4 * g, min(4 * g + 4, depth)
                    for c in range(c0, c1):
                        mm(KH(pr)[:, 128 * c:128 * c + 128],
                           QH(pr)[:, 512 * qt:512 * qt + 512],
                           bank(c - c0), True, True,
                           inc="TE" if c == c1 - 1 else None)
                    p.wait("scalar", "TE")
                    p.wait("scalar", "VE")
                    w = 512 * (c1 - c0)
                    p.ins("scalar", "activation", AB[:, 0:w], PS[0:128, 0:w],
                          AF.Exp, bias=sm(2), scale=ISQ, inc="SE")
                    p.wait("vector", "SE")
                    for c in range(c0, c1):
                        dd = c - 4 * qt
                        if 0 <= dd <= 3:
                            off = 512 * (c - c0)
                            p.ins("vector", "tensor_mul", AB[:, off:off + 512],
                                  AB[:, off:off + 512], MKD(dd))
                    p.ins("vector", "nop", inc="VE")
                    p.wait("tensor", "VE")
                    for c in range(c0, c1):
                        off = 512 * (c - c0)
                        first = (g == 0 and c == c0)
                        last = (c == depth - 1)
                        mm(ONESB, AB[:, off:off + 512], bank(6), first, last)
                        mm(VT(c)[:, 128 * h:128 * h + 128],
                           AB[:, off:off + 512], bank(7), first, last,
                           inc="TE" if c == c1 - 1 else None)
                p.wait("vector", "TE")
                p.ins("vector", "reciprocal", LNT[0], bank(6), inc="VE")
                vgate()
                p.ins("vector", "tensor_mul",
                      OH(h)[:, 512 * qt:512 * qt + 512], bank(7), LNT[0],
                      inc="VE")
                p.wait("tensor", "VE")

        for h in range(HPC):
            pr = h % 2
            qk_head(h, pr, wqkv, B_QKV + h, B_QKV + 8 + h, l1=False)
            mhat(pr, ((0, 512), (512, 512), (1024, 512), (1536, 512)))
            l0_attn(h, pr)
            te_head_done[h] = p.v("TE")
        if "q0" in dbg:
            tap_one("q0", QH(1))
        if "k0" in dbg:
            tap_one("k0", KH(1))
        if "oh" in dbg:
            p.wait("gpsimd", "VE")
            for h in range(HPC):
                dma("gpsimd", dbg["oh"][:, 2048 * h:2048 * h + 2048], OH(h),
                    "STO")

        def wo_pass(wsrc):
            p.wait("gpsimd", "TE")
            p.wait("gpsimd", "SE")
            marks = {}
            for oc in range(NCH):
                pr = oc % 2
                if oc >= 2:
                    p.wait("gpsimd", "TE", marks[oc - 2])
                dma_tiles("gpsimd", WOS(pr), wsrc[oc, :, :, :], 8, 128,
                          "LW")
                lw0 = p.v("LW")
                p.wait("tensor", "LW", lw0)
                for tt in range(4):
                    bk = bank(tt % 2)
                    for hc in range(HPC):
                        mm(WOS(pr)[:, 128 * hc:128 * hc + 128],
                           OH(hc)[:, 512 * tt:512 * tt + 512],
                           bk, hc == 0, hc == HPC - 1,
                           inc="TE" if hc == HPC - 1 else None)
                    p.wait("vector", "TE")
                    p.ins("vector", "tensor_copy",
                          CC0(oc)[:, 512 * tt:512 * tt + 512], bk, inc="VE")
                    p.wait("tensor", "VE")
                marks[oc] = p.v("TE")

        wo_pass(wo0)

        p.ins("vector", "memset", sm(3), 0.0, inc="VE")
        for oc in range(NCH):
            pr = oc % 2
            dma_tiles("sync", WFA(pr),
                      w0t[oc, :, :, :], 16, 128, "LX")
            lx0 = p.v("LX")
            p.wait("vector", "LX", lx0)
            p.ins("vector", "reduce_sum", SM[:, 48 + oc:49 + oc], WFA(pr),
                  axis=AX.X, apply_absolute_value=True, inc="VE")
            p.wait("sync", "VE")
        p.wait("vector", "VE")
        p.ins("vector", "reduce_sum", sm(3), SM[:, 48:64], axis=AX.X,
              inc="VE")
        vgate()
        p.ins("vector", "tensor_copy", TRS[:, 0:1], sm(3), inc="VE")
        p.wait("tensor", "VE")
        p.ins("tensor", "transpose", bank(4, 128), TRS[:, 0:128], IDN,
              inc="TE")
        p.wait("vector", "TE")
        p.ins("vector", "reduce_sum", TRS[0:1, 130:131], bank(4, 128)[0:1, :],
              axis=AX.X, inc="VE")
        vgate()
        p.ins("vector", "tensor_scalar", TRS[0:1, 130:131], TRS[0:1, 130:131],
              1.0 / (D * D), 1e-5, ALU.mult, ALU.max, inc="VE")
        pbcast(130, 5)
        p.ins("vector", "tensor_scalar", sm(6), sm(5), 0.5, 0.0, ALU.mult,
              ALU.add)
        p.ins("vector", "tensor_scalar", sm(7), sm(5), -0.5, 0.0, ALU.mult,
              ALU.add, inc="VE")
        vgate()
        p.wait("sync", "CC")
        if "gmo" in dbg:
            p.wait("gpsimd", "CC")
            dma("gpsimd", dbg["gmo"][:, :], gmax_o[:, :], "STO")
        dma("sync", TRS[0:1, 0:128], gmax_o[:, :], "LX")
        p.wait("vector", "LX")
        p.ins("vector", "reduce_max", TRS[0:1, 130:131], TRS[0:1, 0:128],
              axis=AX.X, inc="VE")
        vgate()
        p.ins("vector", "tensor_scalar", TRS[0:1, 130:131], TRS[0:1, 130:131],
              1e-5, 0.0, ALU.max, ALU.add, inc="VE")
        pbcast(130, 8)
        if "gmax" in dbg:
            p.wait("gpsimd", "VE")
            dma("gpsimd", dbg["gmax"][:, 0:1], sm(8), "STO",
                allow_slow_non_contiguous=True)
        if "wsc" in dbg:
            p.wait("gpsimd", "VE")
            dma("gpsimd", dbg["wsc"][:, 0:1], sm(5), "STO",
                allow_slow_non_contiguous=True)
        p.ins("vector", "reciprocal", sm(9), sm(8), inc="VE")
        p.wait("vector", "VE")
        p.ins("vector", "tensor_scalar", sm(9), sm(9), 127.0, 0.0, ALU.mult,
              ALU.add, inc="VE")
        vgate()
        p.ins("vector", "scalar_tensor_tensor", sm(14), sm(8), 1.0 / 127.0,
              sm(5), ALU.mult, ALU.mult, inc="VE")
        vgate()

        for ic in range(NCH):
            p.wait("sync", "VE")
            dma("sync", RSC(0), xpre_d[128 * ic:128 * ic + 128, 0:1024], "LX")
            dma("sync", RSC(1), xpre_d[128 * ic:128 * ic + 128, 1024:2048],
                "LX")
            lx0 = p.v("LX")
            p.wait("vector", "LX", lx0)
            for tt in range(2):
                p.ins("vector", "tensor_scalar", LNT[0],
                      RSC(0)[:, 512 * tt:512 * tt + 512], mlo, 0.0,
                      ALU.mult, ALU.add)
                p.ins("vector", "scalar_tensor_tensor", LNT[0],
                      RSC(1)[:, 512 * tt:512 * tt + 512], mhi, LNT[0],
                      ALU.mult, ALU.add)
                p.ins("vector", "tensor_scalar", LNT[0], LNT[0], sm(9), MAGIC,
                      ALU.mult, ALU.add)
                p.ins("vector", "tensor_scalar",
                      W0IQ(ic)[:, 512 * tt:512 * tt + 512], LNT[0], MAGIC,
                      0.0, ALU.subtract, ALU.add, inc="VE")
        if "iq" in dbg:
            p.wait("gpsimd", "VE")
            for ic in range(NCH):
                dma("gpsimd", dbg["iq"][128 * ic:128 * ic + 128, :],
                    W0IQ(ic), "STO")

        ve_marks = {}
        te_marks_f = {}
        for oc in range(NCH):
            pr = oc % 2
            if oc >= 2:
                p.wait("sync", "VE", ve_marks[oc - 2])
            dma_tiles("sync", WFA(pr),
                      w0t[oc, :, :, :], 16, 128, "LX")
            lx0 = p.v("LX")
            p.wait("vector", "LX", lx0)
            if oc >= 1:
                p.wait("vector", "TE", te_marks_f[oc - 1])
            p.ins("vector", "tensor_scalar", RSC(0), WFA(pr)[:, 0:1024],
                  sm(6), 0.0, ALU.is_gt, ALU.add)
            p.ins("vector", "tensor_scalar", RSC(1), WFA(pr)[:, 0:1024],
                  sm(7), 0.0, ALU.is_lt, ALU.add)
            p.ins("vector", "tensor_sub", RSC(2), RSC(0), RSC(1))
            p.ins("vector", "tensor_scalar", RSC(0), WFA(pr)[:, 1024:2048],
                  sm(6), 0.0, ALU.is_gt, ALU.add)
            p.ins("vector", "tensor_scalar", RSC(1), WFA(pr)[:, 1024:2048],
                  sm(7), 0.0, ALU.is_lt, ALU.add)
            p.ins("vector", "tensor_sub", RSC(3), RSC(0), RSC(1), inc="VE")
            ve_marks[oc] = p.v("VE")
            p.wait("tensor", "VE")
            for tt in range(2):
                bk = bank(4 + tt)
                for ic in range(NCH):
                    wqt = RSC(2) if ic < 8 else RSC(3)
                    mm(wqt[:, 128 * (ic % 8):128 * (ic % 8) + 128],
                       W0IQ(ic)[:, 512 * tt:512 * tt + 512],
                       bk, ic == 0, ic == NCH - 1,
                       inc="TE" if ic == NCH - 1 else None)
            te_marks_f[oc] = p.v("TE")
            if oc == 0 and "wq0" in dbg:
                p.wait("gpsimd", "TE")
                dma("gpsimd", dbg["wq0"][:, :], RSC(2), "STO")
            p.wait("vector", "TE")
            for tt in range(2):
                p.ins("vector", "tensor_scalar", LNT[tt], bank(4 + tt), sm(14),
                      0.0, ALU.mult, ALU.add, inc="VE")
            if oc == 0 and "fs" in dbg:
                p.wait("gpsimd", "VE")
                dma("gpsimd", dbg["fs"][:, :], LNT[0], "STO")
            if oc == 0 and "osc" in dbg:
                p.wait("gpsimd", "VE")
                dma("gpsimd", dbg["osc"][:, 0:1], sm(14), "STO",
                    allow_slow_non_contiguous=True)
            vgate()
            for tt in range(2):
                lo = CC0(oc)[:, 512 * tt:512 * tt + 512]
                hi = CC0(oc)[:, 1024 + 512 * tt:1024 + 512 * tt + 512]
                p.ins("vector", "scalar_tensor_tensor", lo, LNT[tt], mlo, lo,
                      ALU.mult, ALU.add)
                p.ins("vector", "scalar_tensor_tensor", hi, LNT[tt], mhi, hi,
                      ALU.mult, ALU.add)
            p.ins("vector", "nop", inc="VE")
            p.wait("tensor", "VE")
        tap_chunks("cb", CC0)

        p.wait("sync", "VE")
        for oc in range(NCH):
            dma("sync", cb_d[128 * oc:128 * oc + 128, :], CC0(oc), "STO")
        p.wait("gpsimd", "STO")
        p.ins("gpsimd", "collective_compute", "AllReduce", ALU.add,
              replica_groups=[[0, 1], [2, 3], [4, 5], [6, 7]],
              ins=[cb_d.opt()], outs=[ar_d.opt()], inc="CC")

        p.wait("sync", "CC")
        for oc in range(NCH):
            for ht in range(2):
                dma("sync", RSC(0)[:, 0:512], ar_d[128 * oc:128 * oc + 128,
                                         512 * ht:512 * ht + 512], "LX")
                dma("sync", RSC(1)[:, 0:512], ar_d[128 * oc:128 * oc + 128,
                                         1024 + 512 * ht:1536 + 512 * ht],
                    "LX")
                dma("sync", XO1, xT[128 * oc:128 * oc + 128,
                                    512 * ht:512 * ht + 512], "LX")
                dma("sync", XO2, xT[128 * oc:128 * oc + 128,
                                    1024 + 512 * ht:1536 + 512 * ht], "LX")
                lx0 = p.v("LX")
                p.wait("vector", "LX", lx0)
                if oc > 0 or ht > 0:
                    p.wait("vector", "STO")
                p.ins("vector", "scalar_tensor_tensor", LNT[0],
                      RSC(0)[:, 0:512], BIAS[:, B_BO0 + oc:B_BO0 + oc + 1],
                      XO1, ALU.add, ALU.add)
                p.ins("vector", "scalar_tensor_tensor", LNT[1],
                      RSC(1)[:, 0:512], BIAS[:, B_BO0 + oc:B_BO0 + oc + 1],
                      XO2, ALU.add, ALU.add, inc="VE")
                p.ins("vector", "tensor_copy", STG(0), LNT[0])
                p.ins("vector", "tensor_copy", STG(1), LNT[1], inc="VE")
                p.ins("vector", "tensor_scalar", LNT[2], LNT[0], mlo, 0.0,
                      ALU.mult, ALU.add)
                p.ins("vector", "scalar_tensor_tensor", LNT[2], LNT[1], mhi,
                      LNT[2], ALU.mult, ALU.add, inc="VE")
                p.wait("sync", "VE")
                dma("sync", x1_d[128 * oc:128 * oc + 128,
                                 512 * ht:512 * ht + 512], STG(0), "STO")
                dma("sync", x1_d[128 * oc:128 * oc + 128,
                                 1024 + 512 * ht:1536 + 512 * ht], STG(1),
                    "STO")
                dma("sync", x1o_d[128 * oc:128 * oc + 128,
                                  512 * ht:512 * ht + 512], LNT[2], "STO")
        if "x1" in dbg:
            p.wait("gpsimd", "STO")
            for oc in range(NCH):
                dma("gpsimd", dbg["x1"][128 * oc:128 * oc + 128, :],
                    x1_d[128 * oc:128 * oc + 128, :], "STO")
        if "arx" in dbg:
            p.wait("gpsimd", "STO")
            for oc in range(NCH):
                dma("gpsimd", dbg["arx"][128 * oc:128 * oc + 128, :],
                    ar_d[128 * oc:128 * oc + 128, :], "STO")

        def load_x1(dc, t0, dst):
            if dc == 0:
                dma_tiles("sync", bv(64, 8192),
                          x1_d[:, t0:t0 + 512].rearrange("(n p) w -> n p w",
                                                         p=128), 16, 512, "LX")

        def l1_post(t0):
            p.wait("sync", "VE")
            for dc in range(NCH):
                dma("sync", xn2_d[128 * dc:128 * dc + 128, t0:t0 + 512],
                    XN(dc)[:, t0:t0 + 512], "STO")

        layernorm(load_x1, lambda dc, t0: XN(dc)[:, t0:t0 + 512], 10,
                  post=l1_post)
        tap_chunks("xn2", XN)

        p.wait("gpsimd", "VE")
        p.ins("gpsimd", "iota", TRS[0:64, 0:1], pattern=[[0, 1]], base=0,
              channel_multiplier=1, allow_small_or_imprecise_dtypes=True,
              inc="GE")
        p.wait("scalar", "GE")
        p.ins("scalar", "activation", sm(21)[0:64, :], TRS[0:64, 0:1], AF.Exp,
              bias=sm(15)[0:64, :], scale=-math.log(10000.0) / 64.0,
              inc="SE")
        p.wait("gpsimd", "SE")
        p.ins("gpsimd", "iota", COSF[0:64, :], pattern=[[1, 2048]], base=0,
              channel_multiplier=0, allow_small_or_imprecise_dtypes=True,
              inc="GE")
        p.wait("vector", "GE")
        p.wait("vector", "SE")
        p.ins("vector", "tensor_scalar", SINF[0:64, :], COSF[0:64, :],
              sm(21)[0:64, :], 1.0 / (2 * math.pi), ALU.mult, ALU.mult,
              inc="VE")
        p.wait("vector", "VE")
        p.ins("vector", "tensor_scalar", COSF[0:64, :], SINF[0:64, :], 0.25,
              0.0, ALU.add, ALU.add, inc="VE")
        p.wait("vector", "VE")
        SCBOTH = fv(8, 4096)
        for seg in range(8):
            s0 = 512 * seg
            p.ins("vector", "tensor_scalar", TRS[0:64, 0:512],
                  SCBOTH[0:64, s0:s0 + 512], MAGIC, MAGIC, ALU.add,
                  ALU.subtract, inc="VE")
            p.wait("vector", "VE")
            p.ins("vector", "tensor_sub", SCBOTH[0:64, s0:s0 + 512],
                  SCBOTH[0:64, s0:s0 + 512], TRS[0:64, 0:512], inc="VE")
        p.wait("scalar", "VE")
        p.ins("scalar", "activation", SCBOTH[0:64, :], SCBOTH[0:64, :],
              AF.Sin, bias=sm(15)[0:64, :], scale=2 * math.pi, inc="SE")
        if "sin" in dbg:
            p.wait("gpsimd", "SE")
            dma("gpsimd", dbg["sin"][:, :], SINF[0:64, :], "STO")
        if "cos" in dbg:
            p.wait("gpsimd", "SE")
            dma("gpsimd", dbg["cos"][:, :], COSF[0:64, :], "STO")
        p.wait("sync", "SE")
        dma("sync", SINF[64:128, :], SINF[0:64, :], "LX")
        dma("sync", COSF[64:128, :], COSF[0:64, :], "LX")
        lx_rt = p.v("LX")
        p.wait("vector", "LX", lx_rt)
        p.ins("vector", "tensor_scalar", SINF[0:64, :], SINF[0:64, :], -1.0,
              0.0, ALU.mult, ALU.add, inc="VE")
        p.wait("tensor", "VE")

        dma_tiles("gpsimd", bv(96, 16384), wv1[:, :, :], 16, 1024, "LW")
        lw_v1 = p.v("LW")
        bcast8(BIAS[:, B_BV1:B_BV1 + 8], VBB)
        p.wait("tensor", "LW", lw_v1)
        for kc in range(KCH):
            t0 = KTOK[kc]
            for half in range(2):
                bk = bank(half)
                for dc in range(NCH):
                    mm(XN(dc)[:, t0:t0 + 128],
                       VSL(dc)[:, 512 * half:512 * half + 512],
                       bk, dc == 0, dc == NCH - 1,
                       inc="TE" if dc == NCH - 1 else None)
            p.wait("vector", "TE")
            p.ins("vector", "tensor_add", VT(kc)[:, 0:512], bank(0),
                  VBB[:, 0:512])
            p.ins("vector", "tensor_add", VT(kc)[:, 512:1024], bank(1),
                  VBB[:, 512:1024], inc="VE")
            p.wait("tensor", "VE")

        def rope(dst, ncol, pos0):
            p.wait("tensor", "VE")
            p.wait("tensor", "SE")
            nmm = (ncol + 511) // 512
            for s in range(nmm):
                c0 = s * 512
                w = min(512, ncol - c0)
                mm(P64, dst[:, c0:c0 + w], bank(s, w), True, True,
                   inc="TE" if s == nmm - 1 else None)
            p.wait("vector", "TE")
            ca = COSF[:, pos0:pos0 + ncol]
            sa = SINF[:, pos0:pos0 + ncol]
            p.ins("vector", "tensor_mul", AB[:, 0:ncol], dst[:, 0:ncol], ca)
            p.ins("vector", "tensor_mul", dst[:, 0:ncol],
                  PS[0:128, 0:ncol], sa)
            p.ins("vector", "tensor_add", dst[:, 0:ncol], dst[:, 0:ncol],
                  AB[:, 0:ncol], inc="VE")
            p.wait("tensor", "VE")

        def l1_attn(h, pr):
            for qt in range(4):
                for grp in range(2):
                    cs = range(3 * grp, 3 * grp + 3)
                    for i, c in enumerate(cs):
                        mm(KH(pr)[:, 128 * c:128 * c + 128],
                           QH(pr)[:, 512 * qt:512 * qt + 512],
                           bank(i), True, True,
                           inc="TE" if i == 2 else None)
                    p.wait("scalar", "TE")
                    p.wait("scalar", "VE")
                    p.ins("scalar", "activation", ABW, PS[0:128, 0:1536],
                          AF.Exp, bias=sm(2), scale=ISQ, inc="SE")
                    p.wait("vector", "SE")
                    if grp == 0:
                        p.ins("vector", "tensor_scalar", ABW[:, 512:1024],
                              ABW[:, 512:1024], sm(20), 0.0, ALU.mult,
                              ALU.add)
                    p.ins("vector", "nop", inc="VE")
                    p.wait("tensor", "VE")
                    for i, c in enumerate(cs):
                        off = 512 * i
                        first = (grp == 0 and i == 0)
                        last = (grp == 1 and i == 2)
                        mm(ONESB, ABW[:, off:off + 512], bank(6), first,
                           last)
                        mm(VT(c)[:, 128 * h:128 * h + 128],
                           ABW[:, off:off + 512], bank(7), first, last,
                           inc="TE" if i == 2 else None)
                p.wait("vector", "TE")
                p.ins("vector", "reciprocal", LNT[0], bank(6), inc="VE")
                vgate()
                p.ins("vector", "tensor_mul",
                      OH(h)[:, 512 * qt:512 * qt + 512], bank(7), LNT[0],
                      inc="VE")
                p.wait("tensor", "VE")

        te_head_done = {}
        for h in range(HPC):
            pr = h % 2
            qk_head(h, pr, wqk1, B_QK1 + h, B_QK1 + 8 + h, l1=True)
            p.wait("vector", "SE")
            rope(QH(pr), 2048, 0)
            rope(KH(pr)[:, 0:256], 256, 0)
            rope(KH(pr)[:, 256:768], 512, 1536)
            mhat(pr, ((0, 512), (256, 512)))
            l1_attn(h, pr)
            te_head_done[h] = p.v("TE")
        if "q1" in dbg:
            tap_one("q1", QH(1))
        if "k1" in dbg:
            tap_one("k1", KH(1)[:, 0:1024])
        if "oh1" in dbg:
            p.wait("gpsimd", "VE")
            for h in range(HPC):
                dma("gpsimd", dbg["oh1"][:, 2048 * h:2048 * h + 2048], OH(h),
                    "STO")

        for t0 in (0, 512):
            for dc in range(NCH):
                p.wait("sync", "VE")
                dma("sync", RSC(0)[:, 0:512], xn2_d[128 * dc:128 * dc + 128,
                                          t0:t0 + 512], "LX")
                dma("sync", RSC(1)[:, 0:512], xn2_d[128 * dc:128 * dc + 128,
                                          1024 + t0:1024 + t0 + 512], "LX")
                lx0 = p.v("LX")
                p.wait("vector", "LX", lx0)
                p.ins("vector", "tensor_scalar", STG(0), RSC(0)[:, 0:512],
                      mlo, 0.0, ALU.mult, ALU.add)
                p.ins("vector", "scalar_tensor_tensor",
                      XR(dc)[:, t0:t0 + 512], RSC(1)[:, 0:512], mhi, STG(0),
                      ALU.mult, ALU.add, inc="VE")
            p.wait("tensor", "VE")
            for dc in range(NCH):
                mm(ONESB, XR(dc)[:, t0:t0 + 512], bank(0), dc == 0,
                   dc == NCH - 1, inc="TE" if dc == NCH - 1 else None)
            te_marks = []
            for dc in range(NCH):
                if dc >= 2:
                    p.wait("vector", "TE", te_marks[dc - 2])
                sq = STG(0) if dc % 2 == 0 else STG(1)
                p.ins("vector", "tensor_mul", sq, XR(dc)[:, t0:t0 + 512],
                      XR(dc)[:, t0:t0 + 512], inc="VE")
                p.wait("tensor", "VE")
                mm(ONESB, sq, bank(1), dc == 0, dc == NCH - 1, inc="TE")
                te_marks.append(p.v("TE"))
            p.wait("vector", "TE")
            p.ins("vector", "tensor_scalar", LNT[0], bank(0), 1.0 / D, 0.0,
                  ALU.mult, ALU.add)
            p.ins("vector", "tensor_scalar", LNT[1], bank(1), 1.0 / D, 0.0,
                  ALU.mult, ALU.add, inc="VE")
            vgate()
            p.ins("vector", "tensor_mul", LNT[2], LNT[0], LNT[0])
            p.ins("vector", "tensor_sub", LNT[1], LNT[1], LNT[2], inc="VE")
            p.wait("scalar", "VE")
            p.ins("scalar", "activation", LNT[2], LNT[1], AF.Sqrt,
                  bias=sm(12), scale=1.0, inc="SE")
            p.wait("vector", "SE")
            p.ins("vector", "reciprocal", LNT[1], LNT[2], inc="VE")
            p.wait("vector", "VE")
            p.ins("vector", "tensor_mul", LNT[2], LNT[0], LNT[1], inc="VE")
            for dc in range(NCH):
                sq = STG(0) if dc % 2 == 0 else STG(1)
                p.ins("vector", "tensor_mul", sq, XR(dc)[:, t0:t0 + 512],
                      LNT[1])
                p.ins("vector", "tensor_sub", XR(dc)[:, t0:t0 + 512], sq,
                      LNT[2], inc="VE")

        p.ins("vector", "memset", SM[:, 40:48], 0.0, inc="VE")
        for dc in range(NCH):
            for tch in range(8):
                p.wait("vector", "TE")
                p.ins("vector", "tensor_copy", TRS[:, 0:128],
                      XR(dc)[:, 128 * tch:128 * tch + 128], inc="VE")
                p.wait("tensor", "VE")
                p.ins("tensor", "transpose", bank(4, 128), TRS[:, 0:128], IDN,
                      inc="TE")
                p.wait("vector", "TE")
                p.ins("vector", "reduce_max", sm(28), bank(4, 128), axis=AX.X,
                      apply_absolute_value=True, inc="VE")
                vgate()
                p.ins("vector", "tensor_max", SM[:, 40 + tch:41 + tch],
                      SM[:, 40 + tch:41 + tch], sm(28), inc="VE")
        if "rmax" in dbg:
            p.wait("gpsimd", "VE")
            dma("gpsimd", dbg["rmax"][:, 0:8], SM[:, 40:48], "STO")
        bcast8(SM[:, 40:48], VBB)
        p.ins("vector", "tensor_copy", BCQ(0), VBB[:, 0:512])
        p.ins("vector", "tensor_copy", BCQ(1), VBB[:, 512:1024], inc="VE")
        p.wait("vector", "VE")
        p.ins("vector", "reciprocal", BCQ(2), BCQ(0))
        p.ins("vector", "reciprocal", BCQ(3), BCQ(1), inc="VE")
        p.wait("vector", "VE")
        for dc in range(NCH):
            for tt in range(2):
                p.ins("vector", "scalar_tensor_tensor", LNT[0],
                      XR(dc)[:, 512 * tt:512 * tt + 512], 127.0, BCQ(2 + tt),
                      ALU.mult, ALU.mult)
                p.ins("vector", "tensor_scalar", LNT[0], LNT[0], MAGIC, MAGIC,
                      ALU.add, ALU.subtract)
                p.ins("vector", "tensor_copy",
                      XR(dc)[:, 512 * tt:512 * tt + 512], LNT[0], inc="VE")
        p.ins("vector", "tensor_scalar", BCQ(0), BCQ(0), 2.0 / 127.0, 0.0,
              ALU.mult, ALU.add)
        p.ins("vector", "tensor_scalar", BCQ(1), BCQ(1), 2.0 / 127.0, 0.0,
              ALU.mult, ALU.add, inc="VE")
        if "i1" in dbg:
            p.wait("gpsimd", "VE")
            for dc in range(NCH):
                dma("gpsimd", dbg["i1"][128 * dc:128 * dc + 128, :], XR(dc),
                    "STO")

        wo_pass(wo1)

        te_marks_g = {}
        p.wait("gpsimd", "TE")
        for osb in range(16):
            if osb >= 1:
                p.wait("gpsimd", "TE", te_marks_g[osb - 1])
            dma_tiles("gpsimd", GSL, wg1[osb, :, :, :], 16, 512, "LW")
            dma_tiles("gpsimd", USL, wu1[osb, :, :, :], 16, 512, "LW")
            lwg = p.v("LW")
            p.wait("vector", "LW", lwg)
            if osb >= 1:
                p.wait("vector", "TE", te_marks_g[osb - 1])
            p.ins("vector", "tensor_scalar", GSL, GSL, 0.0, 0.5, ALU.is_gt,
                  ALU.subtract)
            p.ins("vector", "tensor_scalar", USL, USL, 0.0, 0.5, ALU.is_gt,
                  ALU.subtract, inc="VE")
            p.wait("tensor", "VE")
            p.wait("tensor", "LW", lwg)
            for oc4 in range(4):
                oc = 4 * osb + oc4
                for tt in range(2):
                    bkg, bku = bank(4), bank(5)
                    for ic in range(NCH):
                        mm(GSL[:, 512 * ic + 128 * oc4:512 * ic + 128 * oc4 + 128],
                           XR(ic)[:, 512 * tt:512 * tt + 512],
                           bkg, ic == 0, ic == NCH - 1)
                    for ic in range(NCH):
                        mm(USL[:, 512 * ic + 128 * oc4:512 * ic + 128 * oc4 + 128],
                           XR(ic)[:, 512 * tt:512 * tt + 512],
                           bku, ic == 0, ic == NCH - 1,
                           inc="TE" if ic == NCH - 1 else None)
                    p.wait("vector", "TE")
                    if not (osb == 0 and oc4 == 0 and tt == 0):
                        p.wait("vector", "STO")
                    p.ins("vector", "tensor_mul", RSC(2)[:, 0:512], bkg,
                          BCQ(tt), inc="VE")
                    p.wait("scalar", "VE")
                    p.ins("scalar", "activation", RSC(2)[:, 0:512],
                          RSC(2)[:, 0:512], AF.Silu, bias=sm(15), scale=1.0,
                          inc="SE")
                    p.ins("vector", "tensor_mul", RSC(3)[:, 0:512], bku,
                          BCQ(tt), inc="VE")
                    p.wait("vector", "SE")
                    p.ins("vector", "tensor_mul", RSC(2)[:, 0:512],
                          RSC(2)[:, 0:512], RSC(3)[:, 0:512], inc="VE")
                    p.wait("sync", "VE")
                    dma("sync", p_d[128 * oc:128 * oc + 128,
                                    512 * tt:512 * tt + 512],
                        RSC(2)[:, 0:512], "STO")
                    p.wait("tensor", "VE")
            te_marks_g[osb] = p.v("TE")

        if "pd" in dbg:
            p.wait("gpsimd", "STO")
            for oc in range(8):
                dma("gpsimd", dbg["pd"][128 * oc:128 * oc + 128, :],
                    p_d[128 * oc:128 * oc + 128, :], "STO")

        mm_done = {}
        for th in range(2):
            p.wait("sync", "TE")
            dma_tiles("sync", bv(64, 32768),
                      p_d[:, 512 * th:512 * th + 512]
                      .rearrange("(n p) w -> n p w", p=128), 64, 512, "LX")
            lxp = p.v("LX")
            p.wait("tensor", "LX", lxp)
            for d2c in range(NCH):
                for half in range(2):
                    pr = half
                    if d2c >= 1:
                        p.wait("gpsimd", "TE", mm_done[(th, d2c - 1)])
                    elif th == 1 and half == 0:
                        p.wait("gpsimd", "TE", mm_done[(0, NCH - 1)])
                        p.wait("gpsimd", "STO")
                    if th == 0:
                        dma_tiles("gpsimd", WDS(pr),
                                  wd1[d2c, 32 * half:32 * half + 32, :, :],
                                  32, 128, "LW")
                        lw0 = p.v("LW")
                        p.wait("vector", "LW", lw0)
                        if d2c >= 1:
                            p.wait("vector", "TE", mm_done[(th, d2c - 1)])
                        p.ins("vector", "tensor_scalar", WDS(pr), WDS(pr),
                              0.0, 0.5, ALU.is_gt, ALU.subtract, inc="VE")
                        p.wait("sync", "VE")
                        dma("sync",
                            wdc_d[d2c, 32 * half:32 * half + 32, :, :]
                            .rearrange("n p w -> p n w"),
                            WDS(pr), "STO")
                    else:
                        dma_tiles("gpsimd", WDS(pr),
                                  wdc_d[d2c, 32 * half:32 * half + 32, :, :],
                                  32, 128, "LW")
                lw1, ve1 = p.v("LW"), p.v("VE")
                p.wait("tensor", "LW", lw1)
                p.wait("tensor", "VE", ve1)
                bk = bank(6 + (d2c % 2))
                for oc in range(64):
                    mm(WDS(oc // 32)[:, 128 * (oc % 32):128 * (oc % 32) + 128],
                       PQ(oc), bk, oc == 0, oc == 63,
                       inc="TE" if oc == 63 else None)
                mm_done[(th, d2c)] = p.v("TE")
                p.wait("vector", "TE")
                p.ins("vector", "tensor_scalar", LNT[0], bk, 2.0, 0.0,
                      ALU.mult, ALU.add, inc="VE")
                vgate()
                lo = CC0(d2c)[:, 512 * th:512 * th + 512]
                hi = CC0(d2c)[:, 1024 + 512 * th:1536 + 512 * th]
                p.ins("vector", "scalar_tensor_tensor", lo, LNT[0], mlo, lo,
                      ALU.mult, ALU.add)
                p.ins("vector", "scalar_tensor_tensor", hi, LNT[0], mhi, hi,
                      ALU.mult, ALU.add)
                p.ins("vector", "nop", inc="VE")

        tap_chunks("cb1", CC0)
        p.wait("sync", "VE")
        for oc in range(NCH):
            dma("sync", cb_d[128 * oc:128 * oc + 128, :], CC0(oc), "STO")
        p.wait("gpsimd", "STO")
        p.ins("gpsimd", "collective_compute", "AllReduce", ALU.add,
              replica_groups=[[0, 1], [2, 3], [4, 5], [6, 7]],
              ins=[cb_d.opt()], outs=[ar_d.opt()], inc="CC")

        p.wait("sync", "CC")
        if "ar1" in dbg:
            p.wait("gpsimd", "CC")
            for oc in range(NCH):
                dma("gpsimd", dbg["ar1"][128 * oc:128 * oc + 128, :],
                    ar_d[128 * oc:128 * oc + 128, :], "STO")
        for oc in range(NCH):
            for ht in range(2):
                dma("sync", RSC(0)[:, 0:512], ar_d[128 * oc:128 * oc + 128,
                                         512 * ht:512 * ht + 512], "LX")
                dma("sync", RSC(1)[:, 0:512], ar_d[128 * oc:128 * oc + 128,
                                         1024 + 512 * ht:1536 + 512 * ht],
                    "LX")
                dma("sync", XO1, x1o_d[128 * oc:128 * oc + 128,
                                       512 * ht:512 * ht + 512], "LX")
                lx0 = p.v("LX")
                p.wait("vector", "LX", lx0)
                if oc > 0 or ht > 0:
                    p.wait("vector", "STO")
                p.ins("vector", "tensor_scalar", LNT[0], RSC(0)[:, 0:512],
                      mlo, 0.0, ALU.mult, ALU.add)
                p.ins("vector", "scalar_tensor_tensor", LNT[0],
                      RSC(1)[:, 0:512], mhi, LNT[0], ALU.mult, ALU.add)
                p.ins("vector", "scalar_tensor_tensor", LNT[1], LNT[0],
                      BIAS[:, B_BO1 + oc:B_BO1 + oc + 1], XO1,
                      ALU.add, ALU.add, inc="VE")
                p.wait("sync", "VE")
                dma("sync", out_own[128 * oc:128 * oc + 128,
                                    512 * ht:512 * ht + 512], LNT[1], "STO")

        p.wait("sync", "STO")
        p.wait("sync", "LX")
        p.wait("sync", "LW")

        block = ctx.enter_context(nc.Block())

        @block.sync
        def _(e):
            for f in p.ops["sync"]:
                f(e)

        @block.gpsimd
        def _(e):
            for f in p.ops["gpsimd"]:
                f(e)

        @block.tensor
        def _(e):
            for f in p.ops["tensor"]:
                f(e)

        @block.vector
        def _(e):
            for f in p.ops["vector"]:
                f(e)

        @block.scalar
        def _(e):
            for f in p.ops["scalar"]:
                f(e)

    return nc


def _rope_perm():
    pm = np.zeros(128, np.int64)
    pm[0:64] = np.arange(0, 128, 2)
    pm[64:128] = np.arange(1, 128, 2)
    return pm


def _shard(inputs):
    f32 = np.float32
    x = np.asarray(inputs["L0_attn_x"], f32)
    Wqkv = np.asarray(inputs["L0_attn_Wqkv"], f32)
    bqkv = np.asarray(inputs["L0_attn_bqkv"], f32)
    Wo0 = np.asarray(inputs["L0_attn_Wo"], f32)
    bo0 = np.asarray(inputs["L0_attn_bo"], f32)
    W0 = np.asarray(inputs["L0_ffn_weight"], f32)
    g0 = np.asarray(inputs["L0_ffn_norm_weight"], f32)
    Wq1 = np.asarray(inputs["L1_attn_Wq"], f32)
    bq1 = np.asarray(inputs["L1_attn_bq"], f32)
    Wk1 = np.asarray(inputs["L1_attn_Wk"], f32)
    bk1 = np.asarray(inputs["L1_attn_bk"], f32)
    Wv1 = np.asarray(inputs["L1_attn_Wv"], f32)
    bv1 = np.asarray(inputs["L1_attn_bv"], f32)
    Wo1 = np.asarray(inputs["L1_attn_Wo"], f32)
    bo1 = np.asarray(inputs["L1_attn_bo"], f32)
    Wg = np.asarray(inputs["L1_ffn_W_g"], f32)
    Wu = np.asarray(inputs["L1_ffn_W_u"], f32)
    Wd = np.asarray(inputs["L1_ffn_W_d"], f32)

    pm = _rope_perm()
    W0T = np.ascontiguousarray(W0.T)
    w0t = np.ascontiguousarray(
        W0T.reshape(NCH, 128, NCH, 128).transpose(2, 0, 1, 3))
    wg1 = np.ascontiguousarray(
        Wg.T.reshape(NCH, 128, 16, 512).transpose(2, 0, 1, 3))
    wu1 = np.ascontiguousarray(
        Wu.T.reshape(NCH, 128, 16, 512).transpose(2, 0, 1, 3))
    wd1 = np.ascontiguousarray(
        Wd.T.reshape(64, 128, NCH, 128).transpose(2, 0, 1, 3))
    Wq1T, Wk1T, Wv1T, Wo1T = (np.ascontiguousarray(w.T)
                              for w in (Wq1, Wk1, Wv1, Wo1))

    shards = []
    for c in range(8):
        pair, m = divmod(c, 2)
        heads = list(range(8 * m, 8 * m + 8))
        wq_cols = np.stack([Wqkv[:, 128 * h:128 * h + 128] for h in heads])
        wk_cols = np.stack([Wqkv[:, 2048 + 128 * h:2048 + 128 * h + 128]
                            for h in heads])
        wqkv_h = np.ascontiguousarray(
            np.concatenate([wq_cols, wk_cols], axis=2)
            .reshape(HPC, NCH, 128, 256))
        vcols = Wqkv[:, 4096 + 1024 * m:4096 + 1024 * m + 1024]
        wv0_t = np.ascontiguousarray(vcols.reshape(NCH, 128, 1024))
        bq = bqkv[1024 * m:1024 * m + 1024]
        bk = bqkv[2048 + 1024 * m:2048 + 1024 * m + 1024]
        bvv = bqkv[4096 + 1024 * m:4096 + 1024 * m + 1024]
        bqkv_t = np.concatenate([bq.reshape(8, 128).T, bk.reshape(8, 128).T,
                                 bvv.reshape(8, 128).T], axis=1)
        wo0_m = Wo0[1024 * m:1024 * m + 1024, :]
        wo0_t = np.ascontiguousarray(
            wo0_m.reshape(HPC, 128, NCH, 128).transpose(2, 0, 1, 3))
        wq1l, wk1l, bq1l, bk1l = [], [], [], []
        for h in heads:
            cols = 128 * h + pm
            wq1l.append(Wq1T[:, cols])
            wk1l.append(Wk1T[:, cols])
            bq1l.append(bq1[cols])
            bk1l.append(bk1[cols])
        wqk1_t = np.ascontiguousarray(
            np.concatenate([np.stack(wq1l), np.stack(wk1l)], axis=2)
            .reshape(HPC, NCH, 128, 256))
        bqk1_t = np.concatenate(
            [np.stack(bq1l).reshape(8, 128).T,
             np.stack(bk1l).reshape(8, 128).T], axis=1)
        wv1_m = Wv1T[:, 1024 * m:1024 * m + 1024]
        wv1_t = np.ascontiguousarray(wv1_m.reshape(NCH, 128, 1024))
        bv1_t = np.ascontiguousarray(
            bv1[1024 * m:1024 * m + 1024].reshape(8, 128).T)
        wo1_m = Wo1T[1024 * m:1024 * m + 1024, :]
        wo1_t = np.ascontiguousarray(
            wo1_m.reshape(HPC, 128, NCH, 128).transpose(2, 0, 1, 3))
        msel_t = np.zeros((128, 2), f32)
        msel_t[:, m] = 1.0
        shards.append({
            "xT": np.ascontiguousarray(x[pair].T),
            "wqkv": wqkv_h, "wv0": wv0_t,
            "bqkv": np.ascontiguousarray(bqkv_t),
            "wo0": wo0_t, "bo0": np.ascontiguousarray(bo0.reshape(16, 128).T),
            "w0t": w0t, "g0": np.ascontiguousarray(g0.reshape(16, 128).T),
            "wqk1": wqk1_t, "wv1": wv1_t,
            "bqk1": np.ascontiguousarray(bqk1_t), "bv1": bv1_t,
            "wo1": wo1_t, "bo1": np.ascontiguousarray(bo1.reshape(16, 128).T),
            "wg1": wg1, "wu1": wu1, "wd1": wd1,
            "msel": msel_t,
        })
    return shards


def kernel(**inputs):
    if "nc" not in _CACHE:
        _CACHE["nc"] = _build()
    nc = _CACHE["nc"]
    shards = _shard(inputs)
    res = run_bass_kernel_spmd(nc, shards, core_ids=list(range(8)))
    out = np.zeros((4, TALL, D), np.float32)
    for c in range(8):
        pair, m = divmod(c, 2)
        out[pair, 1024 * m:1024 * m + 1024, :] = res.results[c]["out_own"].T
    if DBG:
        kernel.dbg = [{k: v for k, v in r.items() if k.startswith("dbg_")}
                      for r in res.results]
    return out


# version 04
# speedup vs baseline: 1.0816x; 1.0816x over previous
import contextlib
import math

import numpy as np
import ml_dtypes

import concourse.bass as bass
import concourse.mybir as mybir
from concourse.bass_utils import run_bass_kernel_spmd

F32 = mybir.dt.float32
BF16 = mybir.dt.bfloat16
AF = mybir.ActivationFunctionType
ALU = mybir.AluOpType
AX = mybir.AxisListType

MAGIC = float(np.float32(1.5 * 2 ** 23))
D = 2048
NCH = 16
TALL = 2048
TOWN = 1024
HPC = 8
ISQ = 1.0 / math.sqrt(128.0)
KCH = 6
KTOK = [0, 128, 1536, 1664, 1792, 1920]

ENGS = ("sync", "gpsimd", "tensor", "vector", "scalar")

DBG = []
_CACHE = {}


class P(object):
    def __init__(self, nc, ctx):
        self.nc = nc
        self.ops = {e: [] for e in ENGS}
        self.sems = {}
        self.ctx = ctx

    def sem(self, name):
        self.sems[name] = [self.ctx.enter_context(self.nc.semaphore(name)), 0]

    def v(self, name):
        return self.sems[name][1]

    def h(self, name):
        return self.sems[name][0]

    def wait(self, eng, sname, val=None):
        h = self.h(sname)
        v = self.v(sname) if val is None else val
        if v > 0:
            self.ops[eng].append(lambda e, h=h, v=v: e.wait_ge(h, v))

    def ins(self, eng, meth, *args, inc=None, incn=1, **kw):
        pair = None
        if inc is not None:
            self.sems[inc][1] += incn
            pair = (self.h(inc), incn)

        def f(e, meth=meth, args=args, kw=kw, pair=pair):
            i = getattr(e, meth)(*args, **kw)
            if pair is not None:
                i.then_inc(*pair)

        self.ops[eng].append(f)


def _build():
    nc = bass.Bass("TRN2", target_bir_lowering=False, debug=False)

    def dram(n, s, d, k=None):
        t = nc.dram_tensor(n, s, d, kind=k) if k else nc.dram_tensor(n, s, d)
        return t.ap()

    xT = dram("xT", [D, TALL], F32, "ExternalInput")
    wqkv = dram("wqkv", [HPC, NCH, 128, 256], F32, "ExternalInput")
    wv0 = dram("wv0", [NCH, 128, 1024], F32, "ExternalInput")
    bqkv = dram("bqkv", [128, 24], F32, "ExternalInput")
    wo0 = dram("wo0", [NCH, HPC, 128, 128], F32, "ExternalInput")
    bo0 = dram("bo0", [128, 16], F32, "ExternalInput")
    w0t = dram("w0t", [NCH, NCH, 128, 128], F32, "ExternalInput")
    g0 = dram("g0", [128, 16], F32, "ExternalInput")
    wqk1 = dram("wqk1", [HPC, NCH, 128, 256], F32, "ExternalInput")
    wv1 = dram("wv1", [NCH, 128, 1024], F32, "ExternalInput")
    bqk1 = dram("bqk1", [128, 16], F32, "ExternalInput")
    bv1 = dram("bv1", [128, 8], F32, "ExternalInput")
    wo1 = dram("wo1", [NCH, HPC, 128, 128], F32, "ExternalInput")
    bo1 = dram("bo1", [128, 16], F32, "ExternalInput")
    wg1 = dram("wg1", [16, NCH, 128, 512], F32, "ExternalInput")
    wu1 = dram("wu1", [16, NCH, 128, 512], F32, "ExternalInput")
    wd1 = dram("wd1", [NCH, 64, 128, 128], F32, "ExternalInput")
    msel = dram("msel", [128, 2], F32, "ExternalInput")

    out_own = dram("out_own", [D, TOWN], F32, "ExternalOutput")

    xpre_d = dram("xpre_d", [D, TALL], BF16)
    gmax_i = dram("gmax_i", [1, 16], F32)
    gmax_o = dram("gmax_o", [8, 16], F32)
    cb_d = dram("cb_d", [D, TALL], BF16)
    ar_d = dram("ar_d", [D, TALL], BF16)
    x1_d = dram("x1_d", [D, TALL], BF16)
    x1o_d = dram("x1o_d", [D, TOWN], F32)
    xn2_d = dram("xn2_d", [D, TALL], BF16)
    p_d = dram("p_d", [8192, TOWN], BF16)
    wdc_d = dram("wdc_d", [NCH, 64, 128, 128], BF16)

    dbg = {}
    for name in DBG:
        shp = {"xn": [D, TALL], "xpre": [D, TALL], "gmax": [128, 4],
               "vt": [128, 16384], "q0": [128, TALL], "k0": [128, TALL],
               "oh": [128, 16384], "cb": [D, TALL], "arx": [D, TALL],
               "x1": [D, TALL], "xn2": [D, TALL], "iq": [D, TOWN],
               "k1": [128, 1024], "q1": [128, TALL], "sin": [64, TALL],
               "cos": [64, TALL], "rmax": [128, 16], "wsc": [128, 4],
               "i1": [D, TOWN], "oh1": [128, 16384], "mh": [128, 4],
               "gmi": [1, 16], "gmo": [8, 16], "lmax": [128, 8],
               "cb1": [D, TALL], "ar1": [D, TALL], "pd": [1024, TOWN],
               "wq0": [128, 1024], "fs": [128, 512], "osc": [128, 4],
               }[name]
        dbg[name] = dram("dbg_" + name, shp, BF16, "ExternalOutput")

    ctx = contextlib.ExitStack()
    with ctx:
        ARB = ctx.enter_context(nc.sbuf_tensor("ARB", [128, 86528], BF16))
        ARF = ctx.enter_context(nc.sbuf_tensor("ARF", [128, 8960], F32))
        PS = ctx.enter_context(nc.psum_tensor("PS", [128, 4096], F32))

        p = P(nc, ctx)
        for s in ("LX", "LW", "TE", "VE", "SE", "GE", "STO", "CC"):
            p.sem(s)

        def bv(kb, ncol, part=128, frac=0):
            c = kb * 512 + frac
            return ARB[0:part, c:c + ncol]

        def fv(kb, ncol, part=128, frac=0):
            c = kb * 256 + frac
            return ARF[0:part, c:c + ncol]

        def bank(b, w=512, part=128):
            return PS[0:part, b * 512:b * 512 + w]

        XN = lambda dc: bv(4 * dc, 2048)
        CC0 = lambda oc: bv(4 * oc, 2048)
        XBT = lambda dc: bv(64 + dc, 512)
        SQ0 = bv(80, 512)
        SQ1 = bv(81, 512)
        VT = lambda tc: bv(64 + 2 * tc, 1024)
        W0IQ = lambda ic: bv(64 + 2 * ic, 1024)
        XR = W0IQ
        PQ = lambda oc: bv(64 + oc, 512)
        VSL = lambda dc: bv(96 + 2 * dc, 1024)
        OH = lambda h: bv(96 + 4 * h, 2048)
        GSL = lambda pr: bv(96 + 32 * pr, 8192)
        USL = lambda pr: bv(112 + 32 * pr, 8192)
        GUT = bv(161, 512)
        UUT = bv(163, 512)
        WDS = lambda pr: bv(128 + 8 * pr, 4096)
        WDS2 = lambda pr, par: bv(128 + 16 * par + 8 * pr, 4096)
        QH = lambda pr: bv(128 + 4 * pr, 2048)
        KH = lambda pr: bv(136 + 4 * pr, 2048)
        WOS = lambda pr: bv(128 + 2 * pr, 1024)
        AB = bv(144, 2048)
        ABW = bv(144, 1536)
        MKE = bv(148, 896)
        MKD = lambda d_: bv(148, 512, frac=384 - 128 * d_)
        ONESB = bv(150, 128)
        P64 = bv(150, 128, frac=128)
        STG = lambda i: bv(151 + i, 512)
        QKS = lambda pr: bv(153 + 8 * pr, 4096)
        RSC = lambda i: bv(153 + 2 * i, 1024)
        VBB = bv(153, 1024)

        LNT = [fv(0, 512), fv(2, 512), fv(4, 512)]
        SM = fv(6, 256)
        sm = lambda i: SM[:, i:i + 1]
        BIAS = fv(7, 256)
        WFA = lambda pr: fv(8, 2048, frac=pr * 2048)
        SINF = fv(8, 2048)
        COSF = fv(16, 2048)
        IDN = fv(24, 128)
        ONESF = fv(24, 128, frac=128)
        TRS = fv(25, 512)
        BCQ = lambda i: fv(27 + 2 * i, 512)
        XO1 = fv(27, 512)
        XO2 = fv(29, 512)

        B_QKV, B_BO0, B_G0, B_QK1, B_BV1, B_BO1, B_MS = 0, 24, 40, 56, 72, 80, 96
        mlo = BIAS[:, B_MS:B_MS + 1]
        mhi = BIAS[:, B_MS + 1:B_MS + 2]

        def dma(q, out, in_, sem_="LX", **kw):
            p.ins(q, "dma_start", inc=sem_, incn=16, out=out, in_=in_, **kw)

        def dma_tiles(q, dst2d, src3d, n, w, sem_="LW"):
            dma(q, dst2d, src3d.rearrange("n p w -> p n w"), sem_)

        def mm(lhsT, rhs, out, st, sp, inc=None):
            p.ins("tensor", "matmul", out, lhsT, rhs, start=st, stop=sp,
                  inc=inc)

        def vgate():
            p.ins("vector", "nop", inc="VE")
            p.wait("vector", "VE")

        p.ins("gpsimd", "memset", ONESB, 1.0)
        p.ins("gpsimd", "memset", ONESF, 1.0)
        p.ins("gpsimd", "memset", IDN, 0.0)
        p.ins("gpsimd", "affine_select", out=IDN, in_=IDN,
              compare_op=ALU.not_equal, fill=1.0, base=0,
              pattern=[[-1, 128]], channel_multiplier=1)
        p.ins("gpsimd", "memset", MKE, 1.0)
        p.ins("gpsimd", "affine_select", out=MKE, in_=MKE,
              compare_op=ALU.is_ge, fill=0.0, base=-384,
              pattern=[[1, 896]], channel_multiplier=-1)
        p.ins("gpsimd", "memset", P64, 0.0)
        p.ins("gpsimd", "affine_select", out=P64, in_=P64,
              compare_op=ALU.not_equal, fill=1.0, base=64,
              pattern=[[1, 128]], channel_multiplier=-1)
        p.ins("gpsimd", "affine_select", out=P64, in_=P64,
              compare_op=ALU.not_equal, fill=1.0, base=-64,
              pattern=[[1, 128]], channel_multiplier=-1)
        p.ins("gpsimd", "memset", sm(20), 1.0)
        p.ins("gpsimd", "affine_select", out=sm(20), in_=sm(20),
              compare_op=ALU.is_ge, fill=0.0, base=76,
              pattern=[[0, 1]], channel_multiplier=-1, inc="GE")

        dma("sync", BIAS[:, B_QKV:B_QKV + 24], bqkv[:, :])
        dma("sync", BIAS[:, B_BO0:B_BO0 + 16], bo0[:, :])
        dma("sync", BIAS[:, B_G0:B_G0 + 16], g0[:, :])
        dma("sync", BIAS[:, B_QK1:B_QK1 + 16], bqk1[:, :])
        dma("sync", BIAS[:, B_BV1:B_BV1 + 8], bv1[:, :])
        dma("sync", BIAS[:, B_BO1:B_BO1 + 16], bo1[:, :])
        dma("sync", BIAS[:, B_MS:B_MS + 2], msel[:, :])
        lx_bias = p.v("LX")
        p.wait("vector", "LX", lx_bias)
        p.wait("vector", "GE")
        for i, val in ((10, 1e-5), (11, 1e-6), (12, 1e-8), (13, 0.0),
                       (15, 0.0)):
            p.ins("vector", "memset", sm(i), val)
        p.ins("vector", "memset", SM[:, 24:28], 0.0, inc="VE")
        p.wait("tensor", "VE")
        p.wait("tensor", "GE")
        p.wait("scalar", "VE")
        p.wait("gpsimd", "VE")

        def tap_chunks(name, view, w=TALL):
            if name in dbg:
                for e in ("VE", "SE", "TE"):
                    p.wait("gpsimd", e)
                for dc in range(NCH):
                    dma("gpsimd", dbg[name][128 * dc:128 * dc + 128, 0:w],
                        view(dc), "STO")

        def tap_one(name, src, rows=128):
            if name in dbg:
                for e in ("VE", "SE", "TE"):
                    p.wait("gpsimd", e)
                dma("gpsimd", dbg[name][0:rows, 0:src.shape[1]], src, "STO")

        def bcast8(src8, dst):
            p.wait("vector", "TE")
            p.wait("vector", "SE")
            p.ins("vector", "memset", LNT[0][0:8, :], 0.0)
            p.ins("vector", "memset", LNT[1][0:8, :], 0.0, inc="VE")
            p.wait("sync", "VE")
            for j in range(8):
                t = LNT[0] if j < 4 else LNT[1]
                dma("sync", t[j:j + 1, (j % 4) * 128:(j % 4) * 128 + 128],
                    src8[:, j:j + 1], "LX")
            lxg = p.v("LX")
            p.wait("tensor", "LX", lxg)
            p.wait("tensor", "VE")
            mm(ONESF[0:8, :], LNT[0][0:8, :], bank(4), True, True)
            mm(ONESF[0:8, :], LNT[1][0:8, :], bank(5), True, True, inc="TE")
            p.wait("vector", "TE")
            p.ins("vector", "tensor_copy", dst[:, 0:512], bank(4))
            p.ins("vector", "tensor_copy", dst[:, 512:1024], bank(5), inc="VE")
            vgate()

        def pbcast(srccol, dstslot):
            p.wait("tensor", "VE")
            mm(ONESF[0:1, :], TRS[0:1, srccol:srccol + 1], bank(4, 1),
               True, True, inc="TE")
            p.wait("vector", "TE")
            p.ins("vector", "tensor_copy", sm(dstslot), bank(4, 1),
                  inc="VE")
            vgate()

        def layernorm(load, xdst, eps_slot, post=None):
            for t0 in (0, 512, 1024, 1536):
                p.wait("gpsimd", "VE")
                p.wait("sync", "VE")
                for dc in range(NCH):
                    load(dc, t0, XBT(dc))
                lx0, lw0 = p.v("LX"), p.v("LW")
                for e in ("tensor", "vector"):
                    p.wait(e, "LX", lx0)
                    p.wait(e, "LW", lw0)
                p.wait("tensor", "VE")
                for dc in range(NCH):
                    mm(ONESB, XBT(dc), bank(0), dc == 0, dc == NCH - 1,
                       inc="TE" if dc == NCH - 1 else None)
                te_marks = []
                for dc in range(NCH):
                    if dc >= 2:
                        p.wait("vector", "TE", te_marks[dc - 2])
                    sq = SQ0 if dc % 2 == 0 else SQ1
                    p.ins("vector", "tensor_mul", sq, XBT(dc), XBT(dc),
                          inc="VE")
                    p.wait("tensor", "VE")
                    mm(ONESB, sq, bank(1), dc == 0, dc == NCH - 1, inc="TE")
                    te_marks.append(p.v("TE"))
                p.wait("vector", "TE")
                p.ins("vector", "tensor_scalar", LNT[0], bank(0), 1.0 / D, 0.0,
                      ALU.mult, ALU.add)
                p.ins("vector", "tensor_scalar", LNT[1], bank(1), 1.0 / D, 0.0,
                      ALU.mult, ALU.add, inc="VE")
                vgate()
                p.ins("vector", "tensor_mul", LNT[2], LNT[0], LNT[0])
                p.ins("vector", "tensor_sub", LNT[1], LNT[1], LNT[2], inc="VE")
                p.wait("scalar", "VE")
                p.ins("scalar", "activation", LNT[2], LNT[1], AF.Sqrt,
                      bias=sm(eps_slot), scale=1.0, inc="SE")
                p.wait("vector", "SE")
                p.ins("vector", "reciprocal", LNT[1], LNT[2], inc="VE")
                p.wait("vector", "VE")
                p.ins("vector", "tensor_mul", LNT[2], LNT[0], LNT[1], inc="VE")
                for dc in range(NCH):
                    sq = SQ0 if dc % 2 == 0 else SQ1
                    p.ins("vector", "tensor_mul", sq, XBT(dc), LNT[1])
                    p.ins("vector", "tensor_sub", xdst(dc, t0), sq, LNT[2],
                          inc="VE")
                if post is not None:
                    post(t0)

        def load_x(dc, t0, dst):
            if dc == 0:
                dma_tiles("gpsimd", bv(64, 8192),
                          xT[:, t0:t0 + 512].rearrange("(n p) w -> n p w",
                                                       p=128), 16, 512, "LW")

        def l0_post(t0):
            ti = t0 // 512
            p.wait("tensor", "VE")
            sto_marks = []
            for dc in range(NCH):
                sq = SQ0 if dc % 2 == 0 else SQ1
                p.ins("vector", "tensor_mul", sq,
                      XN(dc)[:, t0:t0 + 512], XN(dc)[:, t0:t0 + 512],
                      inc="VE")
                p.wait("tensor", "VE")
                mm(ONESB, sq, bank(2), dc == 0, dc == NCH - 1, inc="TE")
            p.wait("vector", "TE")
            p.ins("vector", "tensor_scalar", LNT[0], bank(2), 1.0 / D, 0.0,
                  ALU.mult, ALU.add, inc="VE")
            vgate()
            p.wait("scalar", "VE")
            p.ins("scalar", "activation", LNT[1], LNT[0], AF.Sqrt,
                  bias=sm(11), scale=1.0, inc="SE")
            p.wait("vector", "SE")
            p.ins("vector", "reciprocal", LNT[0], LNT[1], inc="VE")
            p.wait("vector", "VE")
            for dc in range(NCH):
                i = dc % 2
                if dc >= 2:
                    p.wait("vector", "STO", sto_marks[dc - 2])
                p.ins("vector", "scalar_tensor_tensor", LNT[1],
                      XN(dc)[:, t0:t0 + 512],
                      BIAS[:, B_G0 + dc:B_G0 + dc + 1], LNT[0],
                      ALU.mult, ALU.mult, inc="VE")
                p.wait("vector", "VE")
                p.ins("vector", "tensor_copy", STG(i), LNT[1])
                p.ins("vector", "reduce_max", SM[:, 64 + 16 * ti + dc:
                                                 65 + 16 * ti + dc],
                      LNT[1], axis=AX.X, apply_absolute_value=True,
                      inc="VE")
                p.wait("sync", "VE")
                dma("sync", xpre_d[128 * dc:128 * dc + 128, t0:t0 + 512],
                    STG(i), "STO")
                sto_marks.append(p.v("STO"))

        layernorm(load_x, lambda dc, t0: XN(dc)[:, t0:t0 + 512], 10,
                  post=l0_post)
        tap_chunks("xn", XN)

        p.ins("vector", "reduce_max", sm(28), SM[:, 64:96], axis=AX.X,
              inc="VE")
        p.ins("vector", "reduce_max", sm(29), SM[:, 96:128], axis=AX.X,
              inc="VE")
        vgate()
        p.ins("vector", "tensor_scalar", sm(30), sm(28), mlo, 0.0, ALU.mult,
              ALU.add)
        p.ins("vector", "tensor_scalar", sm(31), sm(29), mhi, 0.0, ALU.mult,
              ALU.add, inc="VE")
        vgate()
        p.ins("vector", "tensor_max", sm(28), sm(30), sm(31), inc="VE")
        vgate()
        p.ins("vector", "tensor_copy", TRS[:, 0:1], sm(28), inc="VE")
        p.wait("tensor", "VE")
        p.ins("tensor", "transpose", bank(4, 128), TRS[:, 0:128], IDN,
              inc="TE")
        p.wait("vector", "TE")
        p.ins("vector", "reduce_max", TRS[0:1, 130:131], bank(4, 128)[0:1, :],
              axis=AX.X, inc="VE")
        vgate()
        for j in range(16):
            p.ins("vector", "tensor_copy", TRS[0:1, 131 + j:132 + j],
                  TRS[0:1, 130:131])
        p.ins("vector", "nop", inc="VE")
        p.wait("sync", "VE")
        dma("sync", gmax_i[:, :], TRS[0:1, 131:147], "STO")
        if "lmax" in dbg:
            p.wait("gpsimd", "VE")
            dma("gpsimd", dbg["lmax"][:, 0:4], SM[:, 24:28], "STO")
            dma("gpsimd", dbg["lmax"][:, 4:5], sm(28), "STO",
                allow_slow_non_contiguous=True)
        if "gmi" in dbg:
            p.wait("gpsimd", "STO")
            dma("gpsimd", dbg["gmi"][:, :], gmax_i[:, :], "STO")
        p.wait("gpsimd", "STO")
        p.ins("gpsimd", "collective_compute", "AllGather", ALU.bypass,
              replica_groups=[list(range(8))],
              ins=[gmax_i.opt()], outs=[gmax_o.opt()], inc="CC")

        dma_tiles("gpsimd", bv(96, 16384), wv0[:, :, :], 16, 1024, "LW")
        lw_v = p.v("LW")
        bcast8(BIAS[:, B_QKV + 16:B_QKV + 24], VBB)
        p.wait("tensor", "LW", lw_v)
        for tc in range(NCH):
            for half in range(2):
                bk = bank(half)
                for dc in range(NCH):
                    mm(XN(dc)[:, 128 * tc:128 * tc + 128],
                       VSL(dc)[:, 512 * half:512 * half + 512],
                       bk, dc == 0, dc == NCH - 1,
                       inc="TE" if dc == NCH - 1 else None)
            p.wait("vector", "TE")
            p.ins("vector", "tensor_add", VT(tc)[:, 0:512], bank(0),
                  VBB[:, 0:512])
            p.ins("vector", "tensor_add", VT(tc)[:, 512:1024], bank(1),
                  VBB[:, 512:1024], inc="VE")
            p.wait("tensor", "VE")
        if "vt" in dbg:
            p.wait("gpsimd", "VE")
            for tc in range(NCH):
                dma("gpsimd", dbg["vt"][:, 1024 * tc:1024 * tc + 1024],
                    VT(tc), "STO")

        te_head_done = {}

        def qk_head(h, pr, wsrc, bcol_q, bcol_k, l1=False):
            if h >= 2:
                p.wait("gpsimd", "TE", te_head_done[h - 2])
            else:
                p.wait("gpsimd", "VE")
                p.wait("gpsimd", "TE")
            dma_tiles("gpsimd", QKS(pr), wsrc[h, :, :, :], 16, 256, "LW")
            lw0 = p.v("LW")
            p.wait("tensor", "LW", lw0)
            for tt in range(4):
                bk = bank(4 + (tt % 2))
                for dc in range(NCH):
                    mm(QKS(pr)[:, 256 * dc:256 * dc + 128],
                       XN(dc)[:, 512 * tt:512 * tt + 512],
                       bk, dc == 0, dc == NCH - 1,
                       inc="TE" if dc == NCH - 1 else None)
                p.wait("scalar", "TE")
                p.ins("scalar", "activation", QH(pr)[:, 512 * tt:512 * tt + 512],
                      bk, AF.Identity, bias=BIAS[:, bcol_q:bcol_q + 1],
                      scale=1.0, inc="SE")
                p.wait("tensor", "SE")
            kranges = (((0, 512), 0), ((512, 512), 512), ((1024, 512), 1024),
                       ((1536, 512), 1536)) if not l1 else \
                      (((0, 256), 0), ((1536, 512), 256))
            for (src0, w), dcol in kranges:
                bk2 = bank(6, w)
                for dc in range(NCH):
                    mm(QKS(pr)[:, 256 * dc + 128:256 * dc + 256],
                       XN(dc)[:, src0:src0 + w],
                       bk2, dc == 0, dc == NCH - 1,
                       inc="TE" if dc == NCH - 1 else None)
                p.wait("scalar", "TE")
                p.ins("scalar", "activation", KH(pr)[:, dcol:dcol + w],
                      bk2, AF.Identity, bias=BIAS[:, bcol_k:bcol_k + 1],
                      scale=1.0, inc="SE")
                p.wait("tensor", "SE")

        def mhat(pr, ktiles):
            for which, tiles in ((0, ((0, 512), (512, 512), (1024, 512),
                                      (1536, 512))), (1, ktiles)):
                src = QH(pr) if which == 0 else KH(pr)
                for i, (c0, w) in enumerate(tiles):
                    p.wait("vector", "SE")
                    p.wait("vector", "TE")
                    p.ins("vector", "tensor_mul", STG(0)[:, 0:w],
                          src[:, c0:c0 + w], src[:, c0:c0 + w], inc="VE")
                    p.wait("tensor", "VE")
                    mm(ONESB, STG(0)[:, 0:w], bank(6, w), True, True,
                       inc="TE")
                    p.wait("vector", "TE")
                    p.ins("vector", "reduce_max",
                          SM[:, 32 + 4 * which + i:33 + 4 * which + i],
                          bank(6, w), axis=AX.X, inc="VE")
                    vgate()
            nk = len(ktiles)
            p.ins("vector", "reduce_max", sm(0), SM[:, 32:36], axis=AX.X)
            p.ins("vector", "reduce_max", sm(1), SM[:, 36:36 + nk], axis=AX.X,
                  inc="VE")
            p.wait("scalar", "VE")
            p.ins("scalar", "activation", sm(0), sm(0), AF.Sqrt, bias=sm(13),
                  scale=1.0)
            p.ins("scalar", "activation", sm(1), sm(1), AF.Sqrt, bias=sm(13),
                  scale=1.0, inc="SE")
            p.wait("vector", "SE")
            p.ins("vector", "scalar_tensor_tensor", sm(2), sm(0), -ISQ,
                  sm(1), ALU.mult, ALU.mult, inc="VE")
            vgate()
            p.wait("scalar", "VE")

        def l0_attn(h, pr):
            for qt in range(4):
                depth = 4 * (qt + 1)
                for g in range((depth + 3) // 4):
                    c0, c1 = 4 * g, min(4 * g + 4, depth)
                    for c in range(c0, c1):
                        mm(KH(pr)[:, 128 * c:128 * c + 128],
                           QH(pr)[:, 512 * qt:512 * qt + 512],
                           bank(c - c0), True, True,
                           inc="TE" if c == c1 - 1 else None)
                    p.wait("scalar", "TE")
                    p.wait("scalar", "VE")
                    w = 512 * (c1 - c0)
                    p.ins("scalar", "activation", AB[:, 0:w], PS[0:128, 0:w],
                          AF.Exp, bias=sm(2), scale=ISQ, inc="SE")
                    p.wait("vector", "SE")
                    for c in range(c0, c1):
                        dd = c - 4 * qt
                        if 0 <= dd <= 3:
                            off = 512 * (c - c0)
                            p.ins("vector", "tensor_mul", AB[:, off:off + 512],
                                  AB[:, off:off + 512], MKD(dd))
                    p.ins("vector", "nop", inc="VE")
                    p.wait("tensor", "VE")
                    for c in range(c0, c1):
                        off = 512 * (c - c0)
                        first = (g == 0 and c == c0)
                        last = (c == depth - 1)
                        mm(ONESB, AB[:, off:off + 512], bank(6), first, last)
                        mm(VT(c)[:, 128 * h:128 * h + 128],
                           AB[:, off:off + 512], bank(7), first, last,
                           inc="TE" if c == c1 - 1 else None)
                p.wait("vector", "TE")
                p.ins("vector", "reciprocal", LNT[0], bank(6), inc="VE")
                vgate()
                p.ins("vector", "tensor_mul",
                      OH(h)[:, 512 * qt:512 * qt + 512], bank(7), LNT[0],
                      inc="VE")
                p.wait("tensor", "VE")

        for h in range(HPC):
            pr = h % 2
            qk_head(h, pr, wqkv, B_QKV + h, B_QKV + 8 + h, l1=False)
            mhat(pr, ((0, 512), (512, 512), (1024, 512), (1536, 512)))
            l0_attn(h, pr)
            te_head_done[h] = p.v("TE")
        if "q0" in dbg:
            tap_one("q0", QH(1))
        if "k0" in dbg:
            tap_one("k0", KH(1))
        if "oh" in dbg:
            p.wait("gpsimd", "VE")
            for h in range(HPC):
                dma("gpsimd", dbg["oh"][:, 2048 * h:2048 * h + 2048], OH(h),
                    "STO")

        def wo_pass(wsrc):
            p.wait("gpsimd", "TE")
            p.wait("gpsimd", "SE")
            marks = {}
            for oc in range(NCH):
                pr = oc % 2
                if oc >= 2:
                    p.wait("gpsimd", "TE", marks[oc - 2])
                dma_tiles("gpsimd", WOS(pr), wsrc[oc, :, :, :], 8, 128,
                          "LW")
                lw0 = p.v("LW")
                p.wait("tensor", "LW", lw0)
                for tt in range(4):
                    bk = bank(tt % 2)
                    for hc in range(HPC):
                        mm(WOS(pr)[:, 128 * hc:128 * hc + 128],
                           OH(hc)[:, 512 * tt:512 * tt + 512],
                           bk, hc == 0, hc == HPC - 1,
                           inc="TE" if hc == HPC - 1 else None)
                    p.wait("vector", "TE")
                    p.ins("vector", "tensor_copy",
                          CC0(oc)[:, 512 * tt:512 * tt + 512], bk, inc="VE")
                    p.wait("tensor", "VE")
                marks[oc] = p.v("TE")

        wo_pass(wo0)

        p.ins("vector", "memset", sm(3), 0.0, inc="VE")
        for oc in range(NCH):
            pr = oc % 2
            dma_tiles("sync", WFA(pr),
                      w0t[oc, :, :, :], 16, 128, "LX")
            lx0 = p.v("LX")
            p.wait("vector", "LX", lx0)
            p.ins("vector", "reduce_sum", SM[:, 48 + oc:49 + oc], WFA(pr),
                  axis=AX.X, apply_absolute_value=True, inc="VE")
            p.wait("sync", "VE")
        p.wait("vector", "VE")
        p.ins("vector", "reduce_sum", sm(3), SM[:, 48:64], axis=AX.X,
              inc="VE")
        vgate()
        p.ins("vector", "tensor_copy", TRS[:, 0:1], sm(3), inc="VE")
        p.wait("tensor", "VE")
        p.ins("tensor", "transpose", bank(4, 128), TRS[:, 0:128], IDN,
              inc="TE")
        p.wait("vector", "TE")
        p.ins("vector", "reduce_sum", TRS[0:1, 130:131], bank(4, 128)[0:1, :],
              axis=AX.X, inc="VE")
        vgate()
        p.ins("vector", "tensor_scalar", TRS[0:1, 130:131], TRS[0:1, 130:131],
              1.0 / (D * D), 1e-5, ALU.mult, ALU.max, inc="VE")
        pbcast(130, 5)
        p.ins("vector", "tensor_scalar", sm(6), sm(5), 0.5, 0.0, ALU.mult,
              ALU.add)
        p.ins("vector", "tensor_scalar", sm(7), sm(5), -0.5, 0.0, ALU.mult,
              ALU.add, inc="VE")
        vgate()
        p.wait("sync", "CC")
        if "gmo" in dbg:
            p.wait("gpsimd", "CC")
            dma("gpsimd", dbg["gmo"][:, :], gmax_o[:, :], "STO")
        dma("sync", TRS[0:1, 0:128], gmax_o[:, :], "LX")
        p.wait("vector", "LX")
        p.ins("vector", "reduce_max", TRS[0:1, 130:131], TRS[0:1, 0:128],
              axis=AX.X, inc="VE")
        vgate()
        p.ins("vector", "tensor_scalar", TRS[0:1, 130:131], TRS[0:1, 130:131],
              1e-5, 0.0, ALU.max, ALU.add, inc="VE")
        pbcast(130, 8)
        if "gmax" in dbg:
            p.wait("gpsimd", "VE")
            dma("gpsimd", dbg["gmax"][:, 0:1], sm(8), "STO",
                allow_slow_non_contiguous=True)
        if "wsc" in dbg:
            p.wait("gpsimd", "VE")
            dma("gpsimd", dbg["wsc"][:, 0:1], sm(5), "STO",
                allow_slow_non_contiguous=True)
        p.ins("vector", "reciprocal", sm(9), sm(8), inc="VE")
        p.wait("vector", "VE")
        p.ins("vector", "tensor_scalar", sm(9), sm(9), 127.0, 0.0, ALU.mult,
              ALU.add, inc="VE")
        vgate()
        p.ins("vector", "scalar_tensor_tensor", sm(14), sm(8), 1.0 / 127.0,
              sm(5), ALU.mult, ALU.mult, inc="VE")
        vgate()

        for ic in range(NCH):
            p.wait("sync", "VE")
            dma("sync", RSC(0), xpre_d[128 * ic:128 * ic + 128, 0:1024], "LX")
            dma("sync", RSC(1), xpre_d[128 * ic:128 * ic + 128, 1024:2048],
                "LX")
            lx0 = p.v("LX")
            p.wait("vector", "LX", lx0)
            for tt in range(2):
                p.ins("vector", "tensor_scalar", LNT[0],
                      RSC(0)[:, 512 * tt:512 * tt + 512], mlo, 0.0,
                      ALU.mult, ALU.add)
                p.ins("vector", "scalar_tensor_tensor", LNT[0],
                      RSC(1)[:, 512 * tt:512 * tt + 512], mhi, LNT[0],
                      ALU.mult, ALU.add)
                p.ins("vector", "tensor_scalar", LNT[0], LNT[0], sm(9), MAGIC,
                      ALU.mult, ALU.add)
                p.ins("vector", "tensor_scalar",
                      W0IQ(ic)[:, 512 * tt:512 * tt + 512], LNT[0], MAGIC,
                      0.0, ALU.subtract, ALU.add, inc="VE")
        if "iq" in dbg:
            p.wait("gpsimd", "VE")
            for ic in range(NCH):
                dma("gpsimd", dbg["iq"][128 * ic:128 * ic + 128, :],
                    W0IQ(ic), "STO")

        ve_marks = {}
        te_marks_f = {}
        for oc in range(NCH):
            pr = oc % 2
            if oc >= 2:
                p.wait("sync", "VE", ve_marks[oc - 2])
            dma_tiles("sync", WFA(pr),
                      w0t[oc, :, :, :], 16, 128, "LX")
            lx0 = p.v("LX")
            p.wait("vector", "LX", lx0)
            if oc >= 1:
                p.wait("vector", "TE", te_marks_f[oc - 1])
            p.ins("vector", "tensor_scalar", RSC(0), WFA(pr)[:, 0:1024],
                  sm(6), 0.0, ALU.is_gt, ALU.add)
            p.ins("vector", "tensor_scalar", RSC(1), WFA(pr)[:, 0:1024],
                  sm(7), 0.0, ALU.is_lt, ALU.add)
            p.ins("vector", "tensor_sub", RSC(2), RSC(0), RSC(1))
            p.ins("vector", "tensor_scalar", RSC(0), WFA(pr)[:, 1024:2048],
                  sm(6), 0.0, ALU.is_gt, ALU.add)
            p.ins("vector", "tensor_scalar", RSC(1), WFA(pr)[:, 1024:2048],
                  sm(7), 0.0, ALU.is_lt, ALU.add)
            p.ins("vector", "tensor_sub", RSC(3), RSC(0), RSC(1), inc="VE")
            ve_marks[oc] = p.v("VE")
            p.wait("tensor", "VE")
            for tt in range(2):
                bk = bank(4 + tt)
                for ic in range(NCH):
                    wqt = RSC(2) if ic < 8 else RSC(3)
                    mm(wqt[:, 128 * (ic % 8):128 * (ic % 8) + 128],
                       W0IQ(ic)[:, 512 * tt:512 * tt + 512],
                       bk, ic == 0, ic == NCH - 1,
                       inc="TE" if ic == NCH - 1 else None)
            te_marks_f[oc] = p.v("TE")
            if oc == 0 and "wq0" in dbg:
                p.wait("gpsimd", "TE")
                dma("gpsimd", dbg["wq0"][:, :], RSC(2), "STO")
            p.wait("vector", "TE")
            for tt in range(2):
                p.ins("vector", "tensor_scalar", LNT[tt], bank(4 + tt), sm(14),
                      0.0, ALU.mult, ALU.add, inc="VE")
            if oc == 0 and "fs" in dbg:
                p.wait("gpsimd", "VE")
                dma("gpsimd", dbg["fs"][:, :], LNT[0], "STO")
            if oc == 0 and "osc" in dbg:
                p.wait("gpsimd", "VE")
                dma("gpsimd", dbg["osc"][:, 0:1], sm(14), "STO",
                    allow_slow_non_contiguous=True)
            vgate()
            for tt in range(2):
                lo = CC0(oc)[:, 512 * tt:512 * tt + 512]
                hi = CC0(oc)[:, 1024 + 512 * tt:1024 + 512 * tt + 512]
                p.ins("vector", "scalar_tensor_tensor", lo, LNT[tt], mlo, lo,
                      ALU.mult, ALU.add)
                p.ins("vector", "scalar_tensor_tensor", hi, LNT[tt], mhi, hi,
                      ALU.mult, ALU.add)
            p.ins("vector", "nop", inc="VE")
            p.wait("tensor", "VE")
        tap_chunks("cb", CC0)

        p.wait("sync", "VE")
        for oc in range(NCH):
            dma("sync", cb_d[128 * oc:128 * oc + 128, :], CC0(oc), "STO")
        p.wait("gpsimd", "STO")
        p.ins("gpsimd", "collective_compute", "AllReduce", ALU.add,
              replica_groups=[[0, 1], [2, 3], [4, 5], [6, 7]],
              ins=[cb_d.opt()], outs=[ar_d.opt()], inc="CC")

        p.wait("sync", "CC")
        for oc in range(NCH):
            for ht in range(2):
                dma("sync", RSC(0)[:, 0:512], ar_d[128 * oc:128 * oc + 128,
                                         512 * ht:512 * ht + 512], "LX")
                dma("sync", RSC(1)[:, 0:512], ar_d[128 * oc:128 * oc + 128,
                                         1024 + 512 * ht:1536 + 512 * ht],
                    "LX")
                dma("sync", XO1, xT[128 * oc:128 * oc + 128,
                                    512 * ht:512 * ht + 512], "LX")
                dma("sync", XO2, xT[128 * oc:128 * oc + 128,
                                    1024 + 512 * ht:1536 + 512 * ht], "LX")
                lx0 = p.v("LX")
                p.wait("vector", "LX", lx0)
                if oc > 0 or ht > 0:
                    p.wait("vector", "STO")
                p.ins("vector", "scalar_tensor_tensor", LNT[0],
                      RSC(0)[:, 0:512], BIAS[:, B_BO0 + oc:B_BO0 + oc + 1],
                      XO1, ALU.add, ALU.add)
                p.ins("vector", "scalar_tensor_tensor", LNT[1],
                      RSC(1)[:, 0:512], BIAS[:, B_BO0 + oc:B_BO0 + oc + 1],
                      XO2, ALU.add, ALU.add, inc="VE")
                p.ins("vector", "tensor_copy", STG(0), LNT[0])
                p.ins("vector", "tensor_copy", STG(1), LNT[1], inc="VE")
                p.ins("vector", "tensor_scalar", LNT[2], LNT[0], mlo, 0.0,
                      ALU.mult, ALU.add)
                p.ins("vector", "scalar_tensor_tensor", LNT[2], LNT[1], mhi,
                      LNT[2], ALU.mult, ALU.add, inc="VE")
                p.wait("sync", "VE")
                dma("sync", x1_d[128 * oc:128 * oc + 128,
                                 512 * ht:512 * ht + 512], STG(0), "STO")
                dma("sync", x1_d[128 * oc:128 * oc + 128,
                                 1024 + 512 * ht:1536 + 512 * ht], STG(1),
                    "STO")
                dma("sync", x1o_d[128 * oc:128 * oc + 128,
                                  512 * ht:512 * ht + 512], LNT[2], "STO")
        if "x1" in dbg:
            p.wait("gpsimd", "STO")
            for oc in range(NCH):
                dma("gpsimd", dbg["x1"][128 * oc:128 * oc + 128, :],
                    x1_d[128 * oc:128 * oc + 128, :], "STO")
        if "arx" in dbg:
            p.wait("gpsimd", "STO")
            for oc in range(NCH):
                dma("gpsimd", dbg["arx"][128 * oc:128 * oc + 128, :],
                    ar_d[128 * oc:128 * oc + 128, :], "STO")

        def load_x1(dc, t0, dst):
            if dc == 0:
                dma_tiles("sync", bv(64, 8192),
                          x1_d[:, t0:t0 + 512].rearrange("(n p) w -> n p w",
                                                         p=128), 16, 512, "LX")

        def l1_post(t0):
            p.wait("sync", "VE")
            for dc in range(NCH):
                dma("sync", xn2_d[128 * dc:128 * dc + 128, t0:t0 + 512],
                    XN(dc)[:, t0:t0 + 512], "STO")

        layernorm(load_x1, lambda dc, t0: XN(dc)[:, t0:t0 + 512], 10,
                  post=l1_post)
        tap_chunks("xn2", XN)

        p.wait("gpsimd", "VE")
        p.ins("gpsimd", "iota", TRS[0:64, 0:1], pattern=[[0, 1]], base=0,
              channel_multiplier=1, allow_small_or_imprecise_dtypes=True,
              inc="GE")
        p.wait("scalar", "GE")
        p.ins("scalar", "activation", sm(21)[0:64, :], TRS[0:64, 0:1], AF.Exp,
              bias=sm(15)[0:64, :], scale=-math.log(10000.0) / 64.0,
              inc="SE")
        p.wait("gpsimd", "SE")
        p.ins("gpsimd", "iota", COSF[0:64, :], pattern=[[1, 2048]], base=0,
              channel_multiplier=0, allow_small_or_imprecise_dtypes=True,
              inc="GE")
        p.wait("vector", "GE")
        p.wait("vector", "SE")
        p.ins("vector", "tensor_scalar", COSF[0:64, :], COSF[0:64, :],
              sm(21)[0:64, :], 0.0, ALU.mult, ALU.add, inc="VE")
        p.wait("vector", "VE")
        SCBOTH = fv(8, 4096)
        TWO_PI = 2 * math.pi
        for seg in range(8):
            s0 = 512 * (seg % 4)
            half_turn = 0.25 if seg >= 4 else 0.0
            dst = (SINF if seg < 4 else COSF)[0:64, s0:s0 + 512]
            xs = COSF[0:64, s0:s0 + 512] if seg < 4 else dst
            if seg >= 4:
                pass
            p.ins("vector", "tensor_scalar", TRS[0:64, 0:512], xs,
                  1.0 / TWO_PI, half_turn, ALU.mult, ALU.add, inc="VE")
            p.wait("vector", "VE")
            p.ins("vector", "tensor_scalar", TRS[0:64, 0:512],
                  TRS[0:64, 0:512], MAGIC, MAGIC, ALU.add, ALU.subtract,
                  inc="VE")
            p.wait("vector", "VE")
            p.ins("vector", "scalar_tensor_tensor", dst,
                  TRS[0:64, 0:512], -TWO_PI, xs, ALU.mult, ALU.add,
                  inc="VE")
            if seg >= 4:
                p.wait("vector", "VE")
                p.ins("vector", "tensor_scalar", dst, dst,
                      half_turn * TWO_PI, 0.0, ALU.add, ALU.add, inc="VE")
        p.wait("scalar", "VE")
        p.ins("scalar", "activation", SCBOTH[0:64, :], SCBOTH[0:64, :],
              AF.Sin, bias=sm(15)[0:64, :], scale=1.0, inc="SE")
        if "sin" in dbg:
            p.wait("gpsimd", "SE")
            dma("gpsimd", dbg["sin"][:, :], SINF[0:64, :], "STO")
        if "cos" in dbg:
            p.wait("gpsimd", "SE")
            dma("gpsimd", dbg["cos"][:, :], COSF[0:64, :], "STO")
        p.wait("sync", "SE")
        dma("sync", SINF[64:128, :], SINF[0:64, :], "LX")
        dma("sync", COSF[64:128, :], COSF[0:64, :], "LX")
        lx_rt = p.v("LX")
        p.wait("vector", "LX", lx_rt)
        p.ins("vector", "tensor_scalar", SINF[0:64, :], SINF[0:64, :], -1.0,
              0.0, ALU.mult, ALU.add, inc="VE")
        p.wait("tensor", "VE")

        dma_tiles("gpsimd", bv(96, 16384), wv1[:, :, :], 16, 1024, "LW")
        lw_v1 = p.v("LW")
        bcast8(BIAS[:, B_BV1:B_BV1 + 8], VBB)
        p.wait("tensor", "LW", lw_v1)
        for kc in range(KCH):
            t0 = KTOK[kc]
            for half in range(2):
                bk = bank(half)
                for dc in range(NCH):
                    mm(XN(dc)[:, t0:t0 + 128],
                       VSL(dc)[:, 512 * half:512 * half + 512],
                       bk, dc == 0, dc == NCH - 1,
                       inc="TE" if dc == NCH - 1 else None)
            p.wait("vector", "TE")
            p.ins("vector", "tensor_add", VT(kc)[:, 0:512], bank(0),
                  VBB[:, 0:512])
            p.ins("vector", "tensor_add", VT(kc)[:, 512:1024], bank(1),
                  VBB[:, 512:1024], inc="VE")
            p.wait("tensor", "VE")

        def rope(dst, ncol, pos0):
            p.wait("tensor", "VE")
            p.wait("tensor", "SE")
            nmm = (ncol + 511) // 512
            for s in range(nmm):
                c0 = s * 512
                w = min(512, ncol - c0)
                mm(P64, dst[:, c0:c0 + w], bank(s, w), True, True,
                   inc="TE" if s == nmm - 1 else None)
            p.wait("vector", "TE")
            ca = COSF[:, pos0:pos0 + ncol]
            sa = SINF[:, pos0:pos0 + ncol]
            p.ins("vector", "tensor_mul", AB[:, 0:ncol], dst[:, 0:ncol], ca)
            p.ins("vector", "tensor_mul", dst[:, 0:ncol],
                  PS[0:128, 0:ncol], sa)
            p.ins("vector", "tensor_add", dst[:, 0:ncol], dst[:, 0:ncol],
                  AB[:, 0:ncol], inc="VE")
            p.wait("tensor", "VE")

        def l1_attn(h, pr):
            for qt in range(4):
                for grp in range(2):
                    cs = range(3 * grp, 3 * grp + 3)
                    for i, c in enumerate(cs):
                        mm(KH(pr)[:, 128 * c:128 * c + 128],
                           QH(pr)[:, 512 * qt:512 * qt + 512],
                           bank(i), True, True,
                           inc="TE" if i == 2 else None)
                    p.wait("scalar", "TE")
                    p.wait("scalar", "VE")
                    p.ins("scalar", "activation", ABW, PS[0:128, 0:1536],
                          AF.Exp, bias=sm(2), scale=ISQ, inc="SE")
                    p.wait("vector", "SE")
                    if grp == 0:
                        p.ins("vector", "tensor_scalar", ABW[:, 512:1024],
                              ABW[:, 512:1024], sm(20), 0.0, ALU.mult,
                              ALU.add)
                    p.ins("vector", "nop", inc="VE")
                    p.wait("tensor", "VE")
                    for i, c in enumerate(cs):
                        off = 512 * i
                        first = (grp == 0 and i == 0)
                        last = (grp == 1 and i == 2)
                        mm(ONESB, ABW[:, off:off + 512], bank(6), first,
                           last)
                        mm(VT(c)[:, 128 * h:128 * h + 128],
                           ABW[:, off:off + 512], bank(7), first, last,
                           inc="TE" if i == 2 else None)
                p.wait("vector", "TE")
                p.ins("vector", "reciprocal", LNT[0], bank(6), inc="VE")
                vgate()
                p.ins("vector", "tensor_mul",
                      OH(h)[:, 512 * qt:512 * qt + 512], bank(7), LNT[0],
                      inc="VE")
                p.wait("tensor", "VE")

        te_head_done = {}
        for h in range(HPC):
            pr = h % 2
            qk_head(h, pr, wqk1, B_QK1 + h, B_QK1 + 8 + h, l1=True)
            p.wait("vector", "SE")
            rope(QH(pr), 2048, 0)
            rope(KH(pr)[:, 0:256], 256, 0)
            rope(KH(pr)[:, 256:768], 512, 1536)
            mhat(pr, ((0, 512), (256, 512)))
            l1_attn(h, pr)
            te_head_done[h] = p.v("TE")
        if "q1" in dbg:
            tap_one("q1", QH(1))
        if "k1" in dbg:
            tap_one("k1", KH(1)[:, 0:1024])
        if "oh1" in dbg:
            p.wait("gpsimd", "VE")
            for h in range(HPC):
                dma("gpsimd", dbg["oh1"][:, 2048 * h:2048 * h + 2048], OH(h),
                    "STO")

        for t0 in (0, 512):
            for dc in range(NCH):
                p.wait("sync", "VE")
                dma("sync", RSC(0)[:, 0:512], xn2_d[128 * dc:128 * dc + 128,
                                          t0:t0 + 512], "LX")
                dma("sync", RSC(1)[:, 0:512], xn2_d[128 * dc:128 * dc + 128,
                                          1024 + t0:1024 + t0 + 512], "LX")
                lx0 = p.v("LX")
                p.wait("vector", "LX", lx0)
                p.ins("vector", "tensor_scalar", STG(0), RSC(0)[:, 0:512],
                      mlo, 0.0, ALU.mult, ALU.add)
                p.ins("vector", "scalar_tensor_tensor",
                      XR(dc)[:, t0:t0 + 512], RSC(1)[:, 0:512], mhi, STG(0),
                      ALU.mult, ALU.add, inc="VE")
            p.wait("tensor", "VE")
            for dc in range(NCH):
                mm(ONESB, XR(dc)[:, t0:t0 + 512], bank(0), dc == 0,
                   dc == NCH - 1, inc="TE" if dc == NCH - 1 else None)
            te_marks = []
            for dc in range(NCH):
                if dc >= 2:
                    p.wait("vector", "TE", te_marks[dc - 2])
                sq = STG(0) if dc % 2 == 0 else STG(1)
                p.ins("vector", "tensor_mul", sq, XR(dc)[:, t0:t0 + 512],
                      XR(dc)[:, t0:t0 + 512], inc="VE")
                p.wait("tensor", "VE")
                mm(ONESB, sq, bank(1), dc == 0, dc == NCH - 1, inc="TE")
                te_marks.append(p.v("TE"))
            p.wait("vector", "TE")
            p.ins("vector", "tensor_scalar", LNT[0], bank(0), 1.0 / D, 0.0,
                  ALU.mult, ALU.add)
            p.ins("vector", "tensor_scalar", LNT[1], bank(1), 1.0 / D, 0.0,
                  ALU.mult, ALU.add, inc="VE")
            vgate()
            p.ins("vector", "tensor_mul", LNT[2], LNT[0], LNT[0])
            p.ins("vector", "tensor_sub", LNT[1], LNT[1], LNT[2], inc="VE")
            p.wait("scalar", "VE")
            p.ins("scalar", "activation", LNT[2], LNT[1], AF.Sqrt,
                  bias=sm(12), scale=1.0, inc="SE")
            p.wait("vector", "SE")
            p.ins("vector", "reciprocal", LNT[1], LNT[2], inc="VE")
            p.wait("vector", "VE")
            p.ins("vector", "tensor_mul", LNT[2], LNT[0], LNT[1], inc="VE")
            for dc in range(NCH):
                sq = STG(0) if dc % 2 == 0 else STG(1)
                p.ins("vector", "tensor_mul", sq, XR(dc)[:, t0:t0 + 512],
                      LNT[1])
                p.ins("vector", "tensor_sub", XR(dc)[:, t0:t0 + 512], sq,
                      LNT[2], inc="VE")

        p.ins("vector", "memset", SM[:, 40:48], 0.0, inc="VE")
        for dc in range(NCH):
            for tch in range(8):
                p.wait("vector", "TE")
                p.ins("vector", "tensor_copy", TRS[:, 0:128],
                      XR(dc)[:, 128 * tch:128 * tch + 128], inc="VE")
                p.wait("tensor", "VE")
                p.ins("tensor", "transpose", bank(4, 128), TRS[:, 0:128], IDN,
                      inc="TE")
                p.wait("vector", "TE")
                p.ins("vector", "reduce_max", sm(28), bank(4, 128), axis=AX.X,
                      apply_absolute_value=True, inc="VE")
                vgate()
                p.ins("vector", "tensor_max", SM[:, 40 + tch:41 + tch],
                      SM[:, 40 + tch:41 + tch], sm(28), inc="VE")
        if "rmax" in dbg:
            p.wait("gpsimd", "VE")
            dma("gpsimd", dbg["rmax"][:, 0:8], SM[:, 40:48], "STO")
        bcast8(SM[:, 40:48], VBB)
        p.ins("vector", "tensor_copy", BCQ(0), VBB[:, 0:512])
        p.ins("vector", "tensor_copy", BCQ(1), VBB[:, 512:1024], inc="VE")
        p.wait("vector", "VE")
        p.ins("vector", "reciprocal", BCQ(2), BCQ(0))
        p.ins("vector", "reciprocal", BCQ(3), BCQ(1), inc="VE")
        p.wait("vector", "VE")
        for dc in range(NCH):
            for tt in range(2):
                p.ins("vector", "scalar_tensor_tensor", LNT[0],
                      XR(dc)[:, 512 * tt:512 * tt + 512], 127.0, BCQ(2 + tt),
                      ALU.mult, ALU.mult)
                p.ins("vector", "tensor_scalar", LNT[0], LNT[0], MAGIC, MAGIC,
                      ALU.add, ALU.subtract)
                p.ins("vector", "tensor_copy",
                      XR(dc)[:, 512 * tt:512 * tt + 512], LNT[0], inc="VE")
        p.ins("vector", "tensor_scalar", BCQ(0), BCQ(0), 2.0 / 127.0, 0.0,
              ALU.mult, ALU.add)
        p.ins("vector", "tensor_scalar", BCQ(1), BCQ(1), 2.0 / 127.0, 0.0,
              ALU.mult, ALU.add, inc="VE")
        if "i1" in dbg:
            p.wait("gpsimd", "VE")
            for dc in range(NCH):
                dma("gpsimd", dbg["i1"][128 * dc:128 * dc + 128, :], XR(dc),
                    "STO")

        wo_pass(wo1)

        te_marks_g = {}
        p.wait("gpsimd", "TE")
        ve_marks_g = {}
        for osb in range(16):
            pr = osb % 2
            if osb >= 2:
                p.wait("gpsimd", "TE", te_marks_g[osb - 2])
            dma_tiles("gpsimd", GSL(pr), wg1[osb, :, :, :], 16, 512, "LW")
            dma_tiles("gpsimd", USL(pr), wu1[osb, :, :, :], 16, 512, "LW")
            lwg = p.v("LW")
            p.wait("vector", "LW", lwg)
            if osb >= 2:
                p.wait("vector", "TE", te_marks_g[osb - 2])
            p.ins("vector", "tensor_scalar", GSL(pr), GSL(pr), 0.0, 0.5,
                  ALU.is_gt, ALU.subtract)
            p.ins("vector", "tensor_scalar", USL(pr), USL(pr), 0.0, 0.5,
                  ALU.is_gt, ALU.subtract, inc="VE")
            ve_marks_g[osb] = p.v("VE")
            p.wait("tensor", "VE", ve_marks_g[osb])
            for oc4 in range(4):
                oc = 4 * osb + oc4
                for tt in range(2):
                    bkg, bku = bank(4), bank(5)
                    for ic in range(NCH):
                        mm(GSL(pr)[:, 512 * ic + 128 * oc4:512 * ic + 128 * oc4 + 128],
                           XR(ic)[:, 512 * tt:512 * tt + 512],
                           bkg, ic == 0, ic == NCH - 1)
                    for ic in range(NCH):
                        mm(USL(pr)[:, 512 * ic + 128 * oc4:512 * ic + 128 * oc4 + 128],
                           XR(ic)[:, 512 * tt:512 * tt + 512],
                           bku, ic == 0, ic == NCH - 1,
                           inc="TE" if ic == NCH - 1 else None)
                    p.wait("vector", "TE")
                    if not (osb == 0 and oc4 == 0 and tt == 0):
                        p.wait("vector", "STO")
                    p.ins("vector", "tensor_mul", GUT, bkg,
                          BCQ(tt), inc="VE")
                    p.wait("scalar", "VE")
                    p.ins("scalar", "activation", GUT,
                          GUT, AF.Silu, bias=sm(15), scale=1.0,
                          inc="SE")
                    p.ins("vector", "tensor_mul", UUT, bku,
                          BCQ(tt), inc="VE")
                    p.wait("vector", "SE")
                    p.ins("vector", "tensor_mul", GUT,
                          GUT, UUT, inc="VE")
                    p.wait("sync", "VE")
                    dma("sync", p_d[128 * oc:128 * oc + 128,
                                    512 * tt:512 * tt + 512],
                        GUT, "STO")
                    p.wait("tensor", "VE")
            te_marks_g[osb] = p.v("TE")

        if "pd" in dbg:
            p.wait("gpsimd", "STO")
            for oc in range(8):
                dma("gpsimd", dbg["pd"][128 * oc:128 * oc + 128, :],
                    p_d[128 * oc:128 * oc + 128, :], "STO")

        mm_done = {}
        p.wait("gpsimd", "TE")
        p.wait("vector", "TE")
        for th in range(2):
            p.wait("sync", "TE")
            dma_tiles("sync", bv(64, 32768),
                      p_d[:, 512 * th:512 * th + 512]
                      .rearrange("(n p) w -> n p w", p=128), 64, 512, "LX")
            lxp = p.v("LX")
            p.wait("tensor", "LX", lxp)
            for d2c in range(NCH):
                for half in range(2):
                    pr = half
                    if d2c >= 2:
                        p.wait("gpsimd", "TE", mm_done[(th, d2c - 2)])
                    elif th == 1 and d2c == 0 and half == 0:
                        p.wait("gpsimd", "TE", mm_done[(0, NCH - 1)])
                        p.wait("gpsimd", "STO")
                    elif th == 1 and d2c == 1 and half == 0:
                        pass
                    if th == 0:
                        dma_tiles("gpsimd", WDS2(pr, d2c % 2),
                                  wd1[d2c, 32 * half:32 * half + 32, :, :],
                                  32, 128, "LW")
                        lw0 = p.v("LW")
                        p.wait("vector", "LW", lw0)
                        if d2c >= 2:
                            p.wait("vector", "TE", mm_done[(th, d2c - 2)])
                        p.ins("vector", "tensor_scalar", WDS2(pr, d2c % 2),
                              WDS2(pr, d2c % 2), 0.0, 0.5, ALU.is_gt,
                              ALU.subtract, inc="VE")
                        p.wait("sync", "VE")
                        dma("sync",
                            wdc_d[d2c, 32 * half:32 * half + 32, :, :]
                            .rearrange("n p w -> p n w"),
                            WDS2(pr, d2c % 2), "STO")
                    else:
                        dma_tiles("gpsimd", WDS2(pr, d2c % 2),
                                  wdc_d[d2c, 32 * half:32 * half + 32, :, :],
                                  32, 128, "LW")
                lw1, ve1 = p.v("LW"), p.v("VE")
                p.wait("tensor", "LW", lw1)
                p.wait("tensor", "VE", ve1)
                bk = bank(6 + (d2c % 2))
                for oc in range(64):
                    mm(WDS2(oc // 32, d2c % 2)[:, 128 * (oc % 32):
                                               128 * (oc % 32) + 128],
                       PQ(oc), bk, oc == 0, oc == 63,
                       inc="TE" if oc == 63 else None)
                mm_done[(th, d2c)] = p.v("TE")
                p.wait("vector", "TE")
                p.ins("vector", "tensor_scalar", LNT[0], bk, 2.0, 0.0,
                      ALU.mult, ALU.add, inc="VE")
                vgate()
                lo = CC0(d2c)[:, 512 * th:512 * th + 512]
                hi = CC0(d2c)[:, 1024 + 512 * th:1536 + 512 * th]
                p.ins("vector", "scalar_tensor_tensor", lo, LNT[0], mlo, lo,
                      ALU.mult, ALU.add)
                p.ins("vector", "scalar_tensor_tensor", hi, LNT[0], mhi, hi,
                      ALU.mult, ALU.add)
                p.ins("vector", "nop", inc="VE")

        tap_chunks("cb1", CC0)
        p.wait("sync", "VE")
        for oc in range(NCH):
            dma("sync", cb_d[128 * oc:128 * oc + 128, :], CC0(oc), "STO")
        p.wait("gpsimd", "STO")
        p.ins("gpsimd", "collective_compute", "AllReduce", ALU.add,
              replica_groups=[[0, 1], [2, 3], [4, 5], [6, 7]],
              ins=[cb_d.opt()], outs=[ar_d.opt()], inc="CC")

        p.wait("sync", "CC")
        if "ar1" in dbg:
            p.wait("gpsimd", "CC")
            for oc in range(NCH):
                dma("gpsimd", dbg["ar1"][128 * oc:128 * oc + 128, :],
                    ar_d[128 * oc:128 * oc + 128, :], "STO")
        for oc in range(NCH):
            for ht in range(2):
                dma("sync", RSC(0)[:, 0:512], ar_d[128 * oc:128 * oc + 128,
                                         512 * ht:512 * ht + 512], "LX")
                dma("sync", RSC(1)[:, 0:512], ar_d[128 * oc:128 * oc + 128,
                                         1024 + 512 * ht:1536 + 512 * ht],
                    "LX")
                dma("sync", XO1, x1o_d[128 * oc:128 * oc + 128,
                                       512 * ht:512 * ht + 512], "LX")
                lx0 = p.v("LX")
                p.wait("vector", "LX", lx0)
                if oc > 0 or ht > 0:
                    p.wait("vector", "STO")
                p.ins("vector", "tensor_scalar", LNT[0], RSC(0)[:, 0:512],
                      mlo, 0.0, ALU.mult, ALU.add)
                p.ins("vector", "scalar_tensor_tensor", LNT[0],
                      RSC(1)[:, 0:512], mhi, LNT[0], ALU.mult, ALU.add)
                p.ins("vector", "scalar_tensor_tensor", LNT[1], LNT[0],
                      BIAS[:, B_BO1 + oc:B_BO1 + oc + 1], XO1,
                      ALU.add, ALU.add, inc="VE")
                p.wait("sync", "VE")
                dma("sync", out_own[128 * oc:128 * oc + 128,
                                    512 * ht:512 * ht + 512], LNT[1], "STO")

        p.wait("sync", "STO")
        p.wait("sync", "LX")
        p.wait("sync", "LW")

        block = ctx.enter_context(nc.Block())

        @block.sync
        def _(e):
            for f in p.ops["sync"]:
                f(e)

        @block.gpsimd
        def _(e):
            for f in p.ops["gpsimd"]:
                f(e)

        @block.tensor
        def _(e):
            for f in p.ops["tensor"]:
                f(e)

        @block.vector
        def _(e):
            for f in p.ops["vector"]:
                f(e)

        @block.scalar
        def _(e):
            for f in p.ops["scalar"]:
                f(e)

    return nc


def _rope_perm():
    pm = np.zeros(128, np.int64)
    pm[0:64] = np.arange(0, 128, 2)
    pm[64:128] = np.arange(1, 128, 2)
    return pm


def _shard(inputs):
    f32 = np.float32
    x = np.asarray(inputs["L0_attn_x"], f32)
    Wqkv = np.asarray(inputs["L0_attn_Wqkv"], f32)
    bqkv = np.asarray(inputs["L0_attn_bqkv"], f32)
    Wo0 = np.asarray(inputs["L0_attn_Wo"], f32)
    bo0 = np.asarray(inputs["L0_attn_bo"], f32)
    W0 = np.asarray(inputs["L0_ffn_weight"], f32)
    g0 = np.asarray(inputs["L0_ffn_norm_weight"], f32)
    Wq1 = np.asarray(inputs["L1_attn_Wq"], f32)
    bq1 = np.asarray(inputs["L1_attn_bq"], f32)
    Wk1 = np.asarray(inputs["L1_attn_Wk"], f32)
    bk1 = np.asarray(inputs["L1_attn_bk"], f32)
    Wv1 = np.asarray(inputs["L1_attn_Wv"], f32)
    bv1 = np.asarray(inputs["L1_attn_bv"], f32)
    Wo1 = np.asarray(inputs["L1_attn_Wo"], f32)
    bo1 = np.asarray(inputs["L1_attn_bo"], f32)
    Wg = np.asarray(inputs["L1_ffn_W_g"], f32)
    Wu = np.asarray(inputs["L1_ffn_W_u"], f32)
    Wd = np.asarray(inputs["L1_ffn_W_d"], f32)

    pm = _rope_perm()
    W0T = np.ascontiguousarray(W0.T)
    w0t = np.ascontiguousarray(
        W0T.reshape(NCH, 128, NCH, 128).transpose(2, 0, 1, 3))
    wg1 = np.ascontiguousarray(
        Wg.T.reshape(NCH, 128, 16, 512).transpose(2, 0, 1, 3))
    wu1 = np.ascontiguousarray(
        Wu.T.reshape(NCH, 128, 16, 512).transpose(2, 0, 1, 3))
    wd1 = np.ascontiguousarray(
        Wd.T.reshape(64, 128, NCH, 128).transpose(2, 0, 1, 3))
    Wq1T, Wk1T, Wv1T, Wo1T = (np.ascontiguousarray(w.T)
                              for w in (Wq1, Wk1, Wv1, Wo1))

    shards = []
    for c in range(8):
        pair, m = divmod(c, 2)
        heads = list(range(8 * m, 8 * m + 8))
        wq_cols = np.stack([Wqkv[:, 128 * h:128 * h + 128] for h in heads])
        wk_cols = np.stack([Wqkv[:, 2048 + 128 * h:2048 + 128 * h + 128]
                            for h in heads])
        wqkv_h = np.ascontiguousarray(
            np.concatenate([wq_cols, wk_cols], axis=2)
            .reshape(HPC, NCH, 128, 256))
        vcols = Wqkv[:, 4096 + 1024 * m:4096 + 1024 * m + 1024]
        wv0_t = np.ascontiguousarray(vcols.reshape(NCH, 128, 1024))
        bq = bqkv[1024 * m:1024 * m + 1024]
        bk = bqkv[2048 + 1024 * m:2048 + 1024 * m + 1024]
        bvv = bqkv[4096 + 1024 * m:4096 + 1024 * m + 1024]
        bqkv_t = np.concatenate([bq.reshape(8, 128).T, bk.reshape(8, 128).T,
                                 bvv.reshape(8, 128).T], axis=1)
        wo0_m = Wo0[1024 * m:1024 * m + 1024, :]
        wo0_t = np.ascontiguousarray(
            wo0_m.reshape(HPC, 128, NCH, 128).transpose(2, 0, 1, 3))
        wq1l, wk1l, bq1l, bk1l = [], [], [], []
        for h in heads:
            cols = 128 * h + pm
            wq1l.append(Wq1T[:, cols])
            wk1l.append(Wk1T[:, cols])
            bq1l.append(bq1[cols])
            bk1l.append(bk1[cols])
        wqk1_t = np.ascontiguousarray(
            np.concatenate([np.stack(wq1l), np.stack(wk1l)], axis=2)
            .reshape(HPC, NCH, 128, 256))
        bqk1_t = np.concatenate(
            [np.stack(bq1l).reshape(8, 128).T,
             np.stack(bk1l).reshape(8, 128).T], axis=1)
        wv1_m = Wv1T[:, 1024 * m:1024 * m + 1024]
        wv1_t = np.ascontiguousarray(wv1_m.reshape(NCH, 128, 1024))
        bv1_t = np.ascontiguousarray(
            bv1[1024 * m:1024 * m + 1024].reshape(8, 128).T)
        wo1_m = Wo1T[1024 * m:1024 * m + 1024, :]
        wo1_t = np.ascontiguousarray(
            wo1_m.reshape(HPC, 128, NCH, 128).transpose(2, 0, 1, 3))
        msel_t = np.zeros((128, 2), f32)
        msel_t[:, m] = 1.0
        shards.append({
            "xT": np.ascontiguousarray(x[pair].T),
            "wqkv": wqkv_h, "wv0": wv0_t,
            "bqkv": np.ascontiguousarray(bqkv_t),
            "wo0": wo0_t, "bo0": np.ascontiguousarray(bo0.reshape(16, 128).T),
            "w0t": w0t, "g0": np.ascontiguousarray(g0.reshape(16, 128).T),
            "wqk1": wqk1_t, "wv1": wv1_t,
            "bqk1": np.ascontiguousarray(bqk1_t), "bv1": bv1_t,
            "wo1": wo1_t, "bo1": np.ascontiguousarray(bo1.reshape(16, 128).T),
            "wg1": wg1, "wu1": wu1, "wd1": wd1,
            "msel": msel_t,
        })
    return shards


def kernel(**inputs):
    if "nc" not in _CACHE:
        _CACHE["nc"] = _build()
    nc = _CACHE["nc"]
    shards = _shard(inputs)
    res = run_bass_kernel_spmd(nc, shards, core_ids=list(range(8)))
    out = np.zeros((4, TALL, D), np.float32)
    for c in range(8):
        pair, m = divmod(c, 2)
        out[pair, 1024 * m:1024 * m + 1024, :] = res.results[c]["out_own"].T
    if DBG:
        kernel.dbg = [{k: v for k, v in r.items() if k.startswith("dbg_")}
                      for r in res.results]
    return out


# version 05
# speedup vs baseline: 1.0908x; 1.0086x over previous
import contextlib
import math

import numpy as np
import ml_dtypes

import concourse.bass as bass
import concourse.mybir as mybir
from concourse.bass_utils import run_bass_kernel_spmd

F32 = mybir.dt.float32
BF16 = mybir.dt.bfloat16
AF = mybir.ActivationFunctionType
ALU = mybir.AluOpType
AX = mybir.AxisListType

MAGIC = float(np.float32(1.5 * 2 ** 23))
D = 2048
NCH = 16
TALL = 2048
TOWN = 1024
HPC = 8
ISQ = 1.0 / math.sqrt(128.0)
KCH = 6
KTOK = [0, 128, 1536, 1664, 1792, 1920]

ENGS = ("sync", "gpsimd", "tensor", "vector", "scalar")

DBG = []
_CACHE = {}


class P(object):
    def __init__(self, nc, ctx):
        self.nc = nc
        self.ops = {e: [] for e in ENGS}
        self.sems = {}
        self.ctx = ctx

    def sem(self, name):
        self.sems[name] = [self.ctx.enter_context(self.nc.semaphore(name)), 0]

    def v(self, name):
        return self.sems[name][1]

    def h(self, name):
        return self.sems[name][0]

    def wait(self, eng, sname, val=None):
        h = self.h(sname)
        v = self.v(sname) if val is None else val
        if v > 0:
            self.ops[eng].append(lambda e, h=h, v=v: e.wait_ge(h, v))

    def ins(self, eng, meth, *args, inc=None, incn=1, **kw):
        pair = None
        if inc is not None:
            self.sems[inc][1] += incn
            pair = (self.h(inc), incn)

        def f(e, meth=meth, args=args, kw=kw, pair=pair):
            i = getattr(e, meth)(*args, **kw)
            if pair is not None:
                i.then_inc(*pair)

        self.ops[eng].append(f)


def _build():
    nc = bass.Bass("TRN2", target_bir_lowering=False, debug=False)

    def dram(n, s, d, k=None):
        t = nc.dram_tensor(n, s, d, kind=k) if k else nc.dram_tensor(n, s, d)
        return t.ap()

    xT = dram("xT", [D, TALL], F32, "ExternalInput")
    wqkv = dram("wqkv", [HPC, NCH, 128, 256], F32, "ExternalInput")
    wv0 = dram("wv0", [NCH, 128, 1024], F32, "ExternalInput")
    bqkv = dram("bqkv", [128, 24], F32, "ExternalInput")
    wo0 = dram("wo0", [NCH, HPC, 128, 128], F32, "ExternalInput")
    bo0 = dram("bo0", [128, 16], F32, "ExternalInput")
    w0t = dram("w0t", [NCH, NCH, 128, 128], F32, "ExternalInput")
    g0 = dram("g0", [128, 16], F32, "ExternalInput")
    wqk1 = dram("wqk1", [HPC, NCH, 128, 256], F32, "ExternalInput")
    wv1 = dram("wv1", [NCH, 128, 1024], F32, "ExternalInput")
    bqk1 = dram("bqk1", [128, 16], F32, "ExternalInput")
    bv1 = dram("bv1", [128, 8], F32, "ExternalInput")
    wo1 = dram("wo1", [NCH, HPC, 128, 128], F32, "ExternalInput")
    bo1 = dram("bo1", [128, 16], F32, "ExternalInput")
    wg1 = dram("wg1", [16, NCH, 128, 512], F32, "ExternalInput")
    wu1 = dram("wu1", [16, NCH, 128, 512], F32, "ExternalInput")
    wd1 = dram("wd1", [NCH, 64, 128, 128], F32, "ExternalInput")
    msel = dram("msel", [128, 2], F32, "ExternalInput")

    out_own = dram("out_own", [D, TOWN], F32, "ExternalOutput")

    xpre_d = dram("xpre_d", [D, TALL], BF16)
    gmax_i = dram("gmax_i", [1, 16], F32)
    gmax_o = dram("gmax_o", [8, 16], F32)
    cb_d = dram("cb_d", [D, TALL], BF16)
    ar_d = dram("ar_d", [D, TALL], BF16)
    x1_d = dram("x1_d", [D, TALL], BF16)
    x1o_d = dram("x1o_d", [D, TOWN], F32)
    xn2_d = dram("xn2_d", [D, TALL], BF16)
    p_d = dram("p_d", [8192, TOWN], BF16)
    wdc_d = dram("wdc_d", [NCH, 64, 128, 128], BF16)

    dbg = {}
    for name in DBG:
        shp = {"xn": [D, TALL], "xpre": [D, TALL], "gmax": [128, 4],
               "vt": [128, 16384], "q0": [128, TALL], "k0": [128, TALL],
               "oh": [128, 16384], "cb": [D, TALL], "arx": [D, TALL],
               "x1": [D, TALL], "xn2": [D, TALL], "iq": [D, TOWN],
               "k1": [128, 1024], "q1": [128, TALL], "sin": [64, TALL],
               "cos": [64, TALL], "rmax": [128, 16], "wsc": [128, 4],
               "i1": [D, TOWN], "oh1": [128, 16384], "mh": [128, 4],
               "gmi": [1, 16], "gmo": [8, 16], "lmax": [128, 8],
               "cb1": [D, TALL], "ar1": [D, TALL], "pd": [1024, TOWN],
               "wq0": [128, 1024], "fs": [128, 512], "osc": [128, 4],
               }[name]
        dbg[name] = dram("dbg_" + name, shp, BF16, "ExternalOutput")

    ctx = contextlib.ExitStack()
    with ctx:
        ARB = ctx.enter_context(nc.sbuf_tensor("ARB", [128, 86528], BF16))
        ARF = ctx.enter_context(nc.sbuf_tensor("ARF", [128, 8960], F32))
        PS = ctx.enter_context(nc.psum_tensor("PS", [128, 4096], F32))

        p = P(nc, ctx)
        for s in ("LX", "LW", "TE", "VE", "SE", "GE", "STO", "CC"):
            p.sem(s)

        def bv(kb, ncol, part=128, frac=0):
            c = kb * 512 + frac
            return ARB[0:part, c:c + ncol]

        def fv(kb, ncol, part=128, frac=0):
            c = kb * 256 + frac
            return ARF[0:part, c:c + ncol]

        def bank(b, w=512, part=128):
            return PS[0:part, b * 512:b * 512 + w]

        XN = lambda dc: bv(4 * dc, 2048)
        CC0 = lambda oc: bv(4 * oc, 2048)
        XBT = lambda dc: bv(64 + dc, 512)
        SQ0 = bv(80, 512)
        SQ1 = bv(81, 512)
        VT = lambda tc: bv(64 + 2 * tc, 1024)
        W0IQ = lambda ic: bv(64 + 2 * ic, 1024)
        XR = W0IQ
        PQ = lambda oc: bv(64 + oc, 512)
        VSL = lambda dc: bv(96 + 2 * dc, 1024)
        OH = lambda h: bv(96 + 4 * h, 2048)
        GSL = lambda pr: bv(96 + 32 * pr, 8192)
        USL = lambda pr: bv(112 + 32 * pr, 8192)
        GUT = bv(161, 512)
        UUT = bv(163, 512)
        WDS = lambda pr: bv(128 + 8 * pr, 4096)
        WDS2 = lambda pr, par: bv(128 + 16 * par + 8 * pr, 4096)
        QH = lambda pr: bv(128 + 4 * pr, 2048)
        KH = lambda pr: bv(136 + 4 * pr, 2048)
        WOS = lambda pr: bv(128 + 2 * pr, 1024)
        AB = bv(144, 2048)
        ABW = bv(144, 1536)
        MKE = bv(148, 896)
        MKD = lambda d_: bv(148, 512, frac=384 - 128 * d_)
        ONESB = bv(150, 128)
        P64 = bv(150, 128, frac=128)
        STG = lambda i: bv(151 + i, 512)
        QKS = lambda pr: bv(153 + 8 * pr, 4096)
        RSC = lambda i: bv(153 + 2 * i, 1024)
        VBB = bv(153, 1024)

        LNT = [fv(0, 512), fv(2, 512), fv(4, 512)]
        SM = fv(6, 256)
        sm = lambda i: SM[:, i:i + 1]
        BIAS = fv(7, 256)
        WFA = lambda pr: fv(8, 2048, frac=pr * 2048)
        SINF = fv(8, 2048)
        COSF = fv(16, 2048)
        IDN = fv(24, 128)
        ONESF = fv(24, 128, frac=128)
        TRS = fv(25, 512)
        BCQ = lambda i: fv(27 + 2 * i, 512)
        XO1 = fv(27, 512)
        XO2 = fv(29, 512)

        B_QKV, B_BO0, B_G0, B_QK1, B_BV1, B_BO1, B_MS = 0, 24, 40, 56, 72, 80, 96
        mlo = BIAS[:, B_MS:B_MS + 1]
        mhi = BIAS[:, B_MS + 1:B_MS + 2]

        def dma(q, out, in_, sem_="LX", **kw):
            p.ins(q, "dma_start", inc=sem_, incn=16, out=out, in_=in_, **kw)

        def dma_tiles(q, dst2d, src3d, n, w, sem_="LW"):
            dma(q, dst2d, src3d.rearrange("n p w -> p n w"), sem_)

        def mm(lhsT, rhs, out, st, sp, inc=None):
            p.ins("tensor", "matmul", out, lhsT, rhs, start=st, stop=sp,
                  inc=inc)

        def vgate():
            p.ins("vector", "nop", inc="VE")
            p.wait("vector", "VE")

        p.ins("gpsimd", "memset", ONESB, 1.0)
        p.ins("gpsimd", "memset", ONESF, 1.0)
        p.ins("gpsimd", "memset", IDN, 0.0)
        p.ins("gpsimd", "affine_select", out=IDN, in_=IDN,
              compare_op=ALU.not_equal, fill=1.0, base=0,
              pattern=[[-1, 128]], channel_multiplier=1)
        p.ins("gpsimd", "memset", MKE, 1.0)
        p.ins("gpsimd", "affine_select", out=MKE, in_=MKE,
              compare_op=ALU.is_ge, fill=0.0, base=-384,
              pattern=[[1, 896]], channel_multiplier=-1)
        p.ins("gpsimd", "memset", P64, 0.0)
        p.ins("gpsimd", "affine_select", out=P64, in_=P64,
              compare_op=ALU.not_equal, fill=1.0, base=64,
              pattern=[[1, 128]], channel_multiplier=-1)
        p.ins("gpsimd", "affine_select", out=P64, in_=P64,
              compare_op=ALU.not_equal, fill=1.0, base=-64,
              pattern=[[1, 128]], channel_multiplier=-1)
        p.ins("gpsimd", "memset", sm(20), 1.0)
        p.ins("gpsimd", "affine_select", out=sm(20), in_=sm(20),
              compare_op=ALU.is_ge, fill=0.0, base=76,
              pattern=[[0, 1]], channel_multiplier=-1, inc="GE")

        dma("sync", BIAS[:, B_QKV:B_QKV + 24], bqkv[:, :])
        dma("sync", BIAS[:, B_BO0:B_BO0 + 16], bo0[:, :])
        dma("sync", BIAS[:, B_G0:B_G0 + 16], g0[:, :])
        dma("sync", BIAS[:, B_QK1:B_QK1 + 16], bqk1[:, :])
        dma("sync", BIAS[:, B_BV1:B_BV1 + 8], bv1[:, :])
        dma("sync", BIAS[:, B_BO1:B_BO1 + 16], bo1[:, :])
        dma("sync", BIAS[:, B_MS:B_MS + 2], msel[:, :])
        lx_bias = p.v("LX")
        p.wait("vector", "LX", lx_bias)
        p.wait("vector", "GE")
        for i, val in ((10, 1e-5), (11, 1e-6), (12, 1e-8), (13, 0.0),
                       (15, 0.0)):
            p.ins("vector", "memset", sm(i), val)
        p.ins("vector", "memset", SM[:, 24:28], 0.0, inc="VE")
        p.wait("tensor", "VE")
        p.wait("tensor", "GE")
        p.wait("scalar", "VE")
        p.wait("gpsimd", "VE")

        def tap_chunks(name, view, w=TALL):
            if name in dbg:
                for e in ("VE", "SE", "TE"):
                    p.wait("gpsimd", e)
                for dc in range(NCH):
                    dma("gpsimd", dbg[name][128 * dc:128 * dc + 128, 0:w],
                        view(dc), "STO")

        def tap_one(name, src, rows=128):
            if name in dbg:
                for e in ("VE", "SE", "TE"):
                    p.wait("gpsimd", e)
                dma("gpsimd", dbg[name][0:rows, 0:src.shape[1]], src, "STO")

        def bcast8(src8, dst):
            p.wait("vector", "TE")
            p.wait("vector", "SE")
            p.ins("vector", "memset", LNT[0][0:8, :], 0.0)
            p.ins("vector", "memset", LNT[1][0:8, :], 0.0, inc="VE")
            p.wait("sync", "VE")
            for j in range(8):
                t = LNT[0] if j < 4 else LNT[1]
                dma("sync", t[j:j + 1, (j % 4) * 128:(j % 4) * 128 + 128],
                    src8[:, j:j + 1], "LX")
            lxg = p.v("LX")
            p.wait("tensor", "LX", lxg)
            p.wait("tensor", "VE")
            mm(ONESF[0:8, :], LNT[0][0:8, :], bank(4), True, True)
            mm(ONESF[0:8, :], LNT[1][0:8, :], bank(5), True, True, inc="TE")
            p.wait("vector", "TE")
            p.ins("vector", "tensor_copy", dst[:, 0:512], bank(4))
            p.ins("vector", "tensor_copy", dst[:, 512:1024], bank(5), inc="VE")
            vgate()

        def pbcast(srccol, dstslot):
            p.wait("tensor", "VE")
            mm(ONESF[0:1, :], TRS[0:1, srccol:srccol + 1], bank(4, 1),
               True, True, inc="TE")
            p.wait("vector", "TE")
            p.ins("vector", "tensor_copy", sm(dstslot), bank(4, 1),
                  inc="VE")
            vgate()

        def layernorm(load, xdst, eps_slot, post=None):
            for t0 in (0, 512, 1024, 1536):
                p.wait("gpsimd", "VE")
                p.wait("sync", "VE")
                for dc in range(NCH):
                    load(dc, t0, XBT(dc))
                lx0, lw0 = p.v("LX"), p.v("LW")
                for e in ("tensor", "vector"):
                    p.wait(e, "LX", lx0)
                    p.wait(e, "LW", lw0)
                p.wait("tensor", "VE")
                for dc in range(NCH):
                    mm(ONESB, XBT(dc), bank(0), dc == 0, dc == NCH - 1,
                       inc="TE" if dc == NCH - 1 else None)
                te_marks = []
                for dc in range(NCH):
                    if dc >= 2:
                        p.wait("vector", "TE", te_marks[dc - 2])
                    sq = SQ0 if dc % 2 == 0 else SQ1
                    p.ins("vector", "tensor_mul", sq, XBT(dc), XBT(dc),
                          inc="VE")
                    p.wait("tensor", "VE")
                    mm(ONESB, sq, bank(1), dc == 0, dc == NCH - 1, inc="TE")
                    te_marks.append(p.v("TE"))
                p.wait("vector", "TE")
                p.ins("vector", "tensor_scalar", LNT[0], bank(0), 1.0 / D, 0.0,
                      ALU.mult, ALU.add)
                p.ins("vector", "tensor_scalar", LNT[1], bank(1), 1.0 / D, 0.0,
                      ALU.mult, ALU.add, inc="VE")
                vgate()
                p.ins("vector", "tensor_mul", LNT[2], LNT[0], LNT[0])
                p.ins("vector", "tensor_sub", LNT[1], LNT[1], LNT[2], inc="VE")
                p.wait("scalar", "VE")
                p.ins("scalar", "activation", LNT[2], LNT[1], AF.Sqrt,
                      bias=sm(eps_slot), scale=1.0, inc="SE")
                p.wait("vector", "SE")
                p.ins("vector", "reciprocal", LNT[1], LNT[2], inc="VE")
                p.wait("vector", "VE")
                p.ins("vector", "tensor_mul", LNT[2], LNT[0], LNT[1], inc="VE")
                for dc in range(NCH):
                    sq = SQ0 if dc % 2 == 0 else SQ1
                    p.ins("vector", "tensor_mul", sq, XBT(dc), LNT[1])
                    p.ins("vector", "tensor_sub", xdst(dc, t0), sq, LNT[2],
                          inc="VE")
                if post is not None:
                    post(t0)

        def load_x(dc, t0, dst):
            if dc == 0:
                dma_tiles("gpsimd", bv(64, 8192),
                          xT[:, t0:t0 + 512].rearrange("(n p) w -> n p w",
                                                       p=128), 16, 512, "LW")

        def l0_post(t0):
            ti = t0 // 512
            p.wait("tensor", "VE")
            sto_marks = []
            for dc in range(NCH):
                sq = SQ0 if dc % 2 == 0 else SQ1
                p.ins("vector", "tensor_mul", sq,
                      XN(dc)[:, t0:t0 + 512], XN(dc)[:, t0:t0 + 512],
                      inc="VE")
                p.wait("tensor", "VE")
                mm(ONESB, sq, bank(2), dc == 0, dc == NCH - 1, inc="TE")
            p.wait("vector", "TE")
            p.ins("vector", "tensor_scalar", LNT[0], bank(2), 1.0 / D, 0.0,
                  ALU.mult, ALU.add, inc="VE")
            vgate()
            p.wait("scalar", "VE")
            p.ins("scalar", "activation", LNT[1], LNT[0], AF.Sqrt,
                  bias=sm(11), scale=1.0, inc="SE")
            p.wait("vector", "SE")
            p.ins("vector", "reciprocal", LNT[0], LNT[1], inc="VE")
            p.wait("vector", "VE")
            for dc in range(NCH):
                i = dc % 2
                if dc >= 2:
                    p.wait("vector", "STO", sto_marks[dc - 2])
                p.ins("vector", "scalar_tensor_tensor", LNT[1],
                      XN(dc)[:, t0:t0 + 512],
                      BIAS[:, B_G0 + dc:B_G0 + dc + 1], LNT[0],
                      ALU.mult, ALU.mult, inc="VE")
                p.wait("vector", "VE")
                p.ins("vector", "tensor_copy", STG(i), LNT[1])
                p.ins("vector", "reduce_max", SM[:, 64 + 16 * ti + dc:
                                                 65 + 16 * ti + dc],
                      LNT[1], axis=AX.X, apply_absolute_value=True,
                      inc="VE")
                p.wait("sync", "VE")
                dma("sync", xpre_d[128 * dc:128 * dc + 128, t0:t0 + 512],
                    STG(i), "STO")
                sto_marks.append(p.v("STO"))

        layernorm(load_x, lambda dc, t0: XN(dc)[:, t0:t0 + 512], 10,
                  post=l0_post)
        tap_chunks("xn", XN)

        p.ins("vector", "reduce_max", sm(28), SM[:, 64:96], axis=AX.X,
              inc="VE")
        p.ins("vector", "reduce_max", sm(29), SM[:, 96:128], axis=AX.X,
              inc="VE")
        vgate()
        p.ins("vector", "tensor_scalar", sm(30), sm(28), mlo, 0.0, ALU.mult,
              ALU.add)
        p.ins("vector", "tensor_scalar", sm(31), sm(29), mhi, 0.0, ALU.mult,
              ALU.add, inc="VE")
        vgate()
        p.ins("vector", "tensor_max", sm(28), sm(30), sm(31), inc="VE")
        vgate()
        p.ins("vector", "tensor_copy", TRS[:, 0:1], sm(28), inc="VE")
        p.wait("tensor", "VE")
        p.ins("tensor", "transpose", bank(4, 128), TRS[:, 0:128], IDN,
              inc="TE")
        p.wait("vector", "TE")
        p.ins("vector", "reduce_max", TRS[0:1, 130:131], bank(4, 128)[0:1, :],
              axis=AX.X, inc="VE")
        vgate()
        for j in range(16):
            p.ins("vector", "tensor_copy", TRS[0:1, 131 + j:132 + j],
                  TRS[0:1, 130:131])
        p.ins("vector", "nop", inc="VE")
        p.wait("sync", "VE")
        dma("sync", gmax_i[:, :], TRS[0:1, 131:147], "STO")
        if "lmax" in dbg:
            p.wait("gpsimd", "VE")
            dma("gpsimd", dbg["lmax"][:, 0:4], SM[:, 24:28], "STO")
            dma("gpsimd", dbg["lmax"][:, 4:5], sm(28), "STO",
                allow_slow_non_contiguous=True)
        if "gmi" in dbg:
            p.wait("gpsimd", "STO")
            dma("gpsimd", dbg["gmi"][:, :], gmax_i[:, :], "STO")
        p.wait("gpsimd", "STO")
        p.ins("gpsimd", "collective_compute", "AllGather", ALU.bypass,
              replica_groups=[list(range(8))],
              ins=[gmax_i.opt()], outs=[gmax_o.opt()], inc="CC")

        dma_tiles("gpsimd", bv(96, 16384), wv0[:, :, :], 16, 1024, "LW")
        lw_v = p.v("LW")
        bcast8(BIAS[:, B_QKV + 16:B_QKV + 24], VBB)
        p.wait("tensor", "LW", lw_v)
        for tc in range(NCH):
            for half in range(2):
                bk = bank(half)
                for dc in range(NCH):
                    mm(XN(dc)[:, 128 * tc:128 * tc + 128],
                       VSL(dc)[:, 512 * half:512 * half + 512],
                       bk, dc == 0, dc == NCH - 1,
                       inc="TE" if dc == NCH - 1 else None)
            p.wait("vector", "TE")
            p.ins("vector", "tensor_add", VT(tc)[:, 0:512], bank(0),
                  VBB[:, 0:512])
            p.ins("vector", "tensor_add", VT(tc)[:, 512:1024], bank(1),
                  VBB[:, 512:1024], inc="VE")
            p.wait("tensor", "VE")
        if "vt" in dbg:
            p.wait("gpsimd", "VE")
            for tc in range(NCH):
                dma("gpsimd", dbg["vt"][:, 1024 * tc:1024 * tc + 1024],
                    VT(tc), "STO")

        te_head_done = {}

        def qk_head(h, pr, wsrc, bcol_q, bcol_k, l1=False):
            if h >= 2:
                p.wait("gpsimd", "TE", te_head_done[h - 2])
            else:
                p.wait("gpsimd", "VE")
                p.wait("gpsimd", "TE")
            dma_tiles("gpsimd", QKS(pr), wsrc[h, :, :, :], 16, 256, "LW")
            lw0 = p.v("LW")
            p.wait("tensor", "LW", lw0)
            for tt in range(4):
                bk = bank(4 + (tt % 2))
                for dc in range(NCH):
                    mm(QKS(pr)[:, 256 * dc:256 * dc + 128],
                       XN(dc)[:, 512 * tt:512 * tt + 512],
                       bk, dc == 0, dc == NCH - 1,
                       inc="TE" if dc == NCH - 1 else None)
                p.wait("scalar", "TE")
                p.ins("scalar", "activation", QH(pr)[:, 512 * tt:512 * tt + 512],
                      bk, AF.Identity, bias=BIAS[:, bcol_q:bcol_q + 1],
                      scale=1.0, inc="SE")
                p.wait("tensor", "SE")
            kranges = (((0, 512), 0), ((512, 512), 512), ((1024, 512), 1024),
                       ((1536, 512), 1536)) if not l1 else \
                      (((0, 256), 0), ((1536, 512), 256))
            for (src0, w), dcol in kranges:
                bk2 = bank(6, w)
                for dc in range(NCH):
                    mm(QKS(pr)[:, 256 * dc + 128:256 * dc + 256],
                       XN(dc)[:, src0:src0 + w],
                       bk2, dc == 0, dc == NCH - 1,
                       inc="TE" if dc == NCH - 1 else None)
                p.wait("scalar", "TE")
                p.ins("scalar", "activation", KH(pr)[:, dcol:dcol + w],
                      bk2, AF.Identity, bias=BIAS[:, bcol_k:bcol_k + 1],
                      scale=1.0, inc="SE")
                p.wait("tensor", "SE")

        def mhat(pr, ktiles):
            for which, tiles in ((0, ((0, 512), (512, 512), (1024, 512),
                                      (1536, 512))), (1, ktiles)):
                src = QH(pr) if which == 0 else KH(pr)
                for i, (c0, w) in enumerate(tiles):
                    p.wait("vector", "SE")
                    p.wait("vector", "TE")
                    p.ins("vector", "tensor_mul", STG(0)[:, 0:w],
                          src[:, c0:c0 + w], src[:, c0:c0 + w], inc="VE")
                    p.wait("tensor", "VE")
                    mm(ONESB, STG(0)[:, 0:w], bank(6, w), True, True,
                       inc="TE")
                    p.wait("vector", "TE")
                    p.ins("vector", "reduce_max",
                          SM[:, 32 + 4 * which + i:33 + 4 * which + i],
                          bank(6, w), axis=AX.X, inc="VE")
                    vgate()
            nk = len(ktiles)
            p.ins("vector", "reduce_max", sm(0), SM[:, 32:36], axis=AX.X)
            p.ins("vector", "reduce_max", sm(1), SM[:, 36:36 + nk], axis=AX.X,
                  inc="VE")
            p.wait("scalar", "VE")
            p.ins("scalar", "activation", sm(0), sm(0), AF.Sqrt, bias=sm(13),
                  scale=1.0)
            p.ins("scalar", "activation", sm(1), sm(1), AF.Sqrt, bias=sm(13),
                  scale=1.0, inc="SE")
            p.wait("vector", "SE")
            p.ins("vector", "scalar_tensor_tensor", sm(2), sm(0), -ISQ,
                  sm(1), ALU.mult, ALU.mult, inc="VE")
            vgate()
            p.wait("scalar", "VE")

        def l0_attn(h, pr):
            for qt in range(4):
                depth = 4 * (qt + 1)
                for g in range((depth + 3) // 4):
                    c0, c1 = 4 * g, min(4 * g + 4, depth)
                    for c in range(c0, c1):
                        mm(KH(pr)[:, 128 * c:128 * c + 128],
                           QH(pr)[:, 512 * qt:512 * qt + 512],
                           bank(c - c0), True, True,
                           inc="TE" if c == c1 - 1 else None)
                    p.wait("scalar", "TE")
                    p.wait("scalar", "VE")
                    w = 512 * (c1 - c0)
                    p.ins("scalar", "activation", AB[:, 0:w], PS[0:128, 0:w],
                          AF.Exp, bias=sm(2), scale=ISQ, inc="SE")
                    p.wait("vector", "SE")
                    for c in range(c0, c1):
                        dd = c - 4 * qt
                        if 0 <= dd <= 3:
                            off = 512 * (c - c0)
                            p.ins("vector", "tensor_mul", AB[:, off:off + 512],
                                  AB[:, off:off + 512], MKD(dd))
                    p.ins("vector", "nop", inc="VE")
                    p.wait("tensor", "VE")
                    for c in range(c0, c1):
                        off = 512 * (c - c0)
                        first = (g == 0 and c == c0)
                        last = (c == depth - 1)
                        mm(ONESB, AB[:, off:off + 512], bank(6), first, last)
                        mm(VT(c)[:, 128 * h:128 * h + 128],
                           AB[:, off:off + 512], bank(7), first, last,
                           inc="TE" if c == c1 - 1 else None)
                p.wait("vector", "TE")
                p.ins("vector", "reciprocal", LNT[0], bank(6), inc="VE")
                vgate()
                p.ins("vector", "tensor_mul",
                      OH(h)[:, 512 * qt:512 * qt + 512], bank(7), LNT[0],
                      inc="VE")
                p.wait("tensor", "VE")

        for h in range(HPC):
            pr = h % 2
            qk_head(h, pr, wqkv, B_QKV + h, B_QKV + 8 + h, l1=False)
            mhat(pr, ((0, 512), (512, 512), (1024, 512), (1536, 512)))
            l0_attn(h, pr)
            te_head_done[h] = p.v("TE")
        if "q0" in dbg:
            tap_one("q0", QH(1))
        if "k0" in dbg:
            tap_one("k0", KH(1))
        if "oh" in dbg:
            p.wait("gpsimd", "VE")
            for h in range(HPC):
                dma("gpsimd", dbg["oh"][:, 2048 * h:2048 * h + 2048], OH(h),
                    "STO")

        def wo_pass(wsrc):
            p.wait("gpsimd", "TE")
            p.wait("gpsimd", "SE")
            marks = {}
            for oc in range(NCH):
                pr = oc % 2
                if oc >= 2:
                    p.wait("gpsimd", "TE", marks[oc - 2])
                dma_tiles("gpsimd", WOS(pr), wsrc[oc, :, :, :], 8, 128,
                          "LW")
                lw0 = p.v("LW")
                p.wait("tensor", "LW", lw0)
                for tt in range(4):
                    bk = bank(tt % 2)
                    for hc in range(HPC):
                        mm(WOS(pr)[:, 128 * hc:128 * hc + 128],
                           OH(hc)[:, 512 * tt:512 * tt + 512],
                           bk, hc == 0, hc == HPC - 1,
                           inc="TE" if hc == HPC - 1 else None)
                    p.wait("vector", "TE")
                    p.ins("vector", "tensor_copy",
                          CC0(oc)[:, 512 * tt:512 * tt + 512], bk, inc="VE")
                    p.wait("tensor", "VE")
                marks[oc] = p.v("TE")

        wo_pass(wo0)

        p.ins("vector", "memset", sm(3), 0.0, inc="VE")
        for oc in range(NCH):
            pr = oc % 2
            dma_tiles("sync", WFA(pr),
                      w0t[oc, :, :, :], 16, 128, "LX")
            lx0 = p.v("LX")
            p.wait("vector", "LX", lx0)
            p.ins("vector", "reduce_sum", SM[:, 48 + oc:49 + oc], WFA(pr),
                  axis=AX.X, apply_absolute_value=True, inc="VE")
            p.wait("sync", "VE")
        p.wait("vector", "VE")
        p.ins("vector", "reduce_sum", sm(3), SM[:, 48:64], axis=AX.X,
              inc="VE")
        vgate()
        p.ins("vector", "tensor_copy", TRS[:, 0:1], sm(3), inc="VE")
        p.wait("tensor", "VE")
        p.ins("tensor", "transpose", bank(4, 128), TRS[:, 0:128], IDN,
              inc="TE")
        p.wait("vector", "TE")
        p.ins("vector", "reduce_sum", TRS[0:1, 130:131], bank(4, 128)[0:1, :],
              axis=AX.X, inc="VE")
        vgate()
        p.ins("vector", "tensor_scalar", TRS[0:1, 130:131], TRS[0:1, 130:131],
              1.0 / (D * D), 1e-5, ALU.mult, ALU.max, inc="VE")
        pbcast(130, 5)
        p.ins("vector", "tensor_scalar", sm(6), sm(5), 0.5, 0.0, ALU.mult,
              ALU.add)
        p.ins("vector", "tensor_scalar", sm(7), sm(5), -0.5, 0.0, ALU.mult,
              ALU.add, inc="VE")
        vgate()
        p.wait("sync", "CC")
        if "gmo" in dbg:
            p.wait("gpsimd", "CC")
            dma("gpsimd", dbg["gmo"][:, :], gmax_o[:, :], "STO")
        dma("sync", TRS[0:1, 0:128], gmax_o[:, :], "LX")
        p.wait("vector", "LX")
        p.ins("vector", "reduce_max", TRS[0:1, 130:131], TRS[0:1, 0:128],
              axis=AX.X, inc="VE")
        vgate()
        p.ins("vector", "tensor_scalar", TRS[0:1, 130:131], TRS[0:1, 130:131],
              1e-5, 0.0, ALU.max, ALU.add, inc="VE")
        pbcast(130, 8)
        if "gmax" in dbg:
            p.wait("gpsimd", "VE")
            dma("gpsimd", dbg["gmax"][:, 0:1], sm(8), "STO",
                allow_slow_non_contiguous=True)
        if "wsc" in dbg:
            p.wait("gpsimd", "VE")
            dma("gpsimd", dbg["wsc"][:, 0:1], sm(5), "STO",
                allow_slow_non_contiguous=True)
        p.ins("vector", "reciprocal", sm(9), sm(8), inc="VE")
        p.wait("vector", "VE")
        p.ins("vector", "tensor_scalar", sm(9), sm(9), 127.0, 0.0, ALU.mult,
              ALU.add, inc="VE")
        vgate()
        p.ins("vector", "scalar_tensor_tensor", sm(14), sm(8), 1.0 / 127.0,
              sm(5), ALU.mult, ALU.mult, inc="VE")
        vgate()

        ve_mk0 = {}
        for ic in range(NCH):
            par = ic % 2
            if ic >= 2:
                p.wait("sync", "VE", ve_mk0[ic - 2])
            r0, r1 = RSC(2 * par), RSC(2 * par + 1)
            dma("sync", r0, xpre_d[128 * ic:128 * ic + 128, 0:1024], "LX")
            dma("sync", r1, xpre_d[128 * ic:128 * ic + 128, 1024:2048], "LX")
            lx0 = p.v("LX")
            p.wait("vector", "LX", lx0)
            for tt in range(2):
                lt = LNT[par] if tt == 0 else LNT[2]
                p.ins("vector", "tensor_scalar", lt,
                      r0[:, 512 * tt:512 * tt + 512], mlo, 0.0,
                      ALU.mult, ALU.add)
                p.ins("vector", "scalar_tensor_tensor", lt,
                      r1[:, 512 * tt:512 * tt + 512], mhi, lt,
                      ALU.mult, ALU.add)
                p.ins("vector", "tensor_scalar", lt, lt, sm(9), MAGIC,
                      ALU.mult, ALU.add)
                p.ins("vector", "tensor_scalar",
                      W0IQ(ic)[:, 512 * tt:512 * tt + 512], lt, MAGIC,
                      0.0, ALU.subtract, ALU.add, inc="VE")
            ve_mk0[ic] = p.v("VE")
        if "iq" in dbg:
            p.wait("gpsimd", "VE")
            for ic in range(NCH):
                dma("gpsimd", dbg["iq"][128 * ic:128 * ic + 128, :],
                    W0IQ(ic), "STO")

        ve_marks = {}
        te_marks_f = {}
        for oc in range(NCH):
            pr = oc % 2
            if oc >= 2:
                p.wait("sync", "VE", ve_marks[oc - 2])
            dma_tiles("sync", WFA(pr),
                      w0t[oc, :, :, :], 16, 128, "LX")
            lx0 = p.v("LX")
            p.wait("vector", "LX", lx0)
            if oc >= 1:
                p.wait("vector", "TE", te_marks_f[oc - 1])
            p.ins("vector", "tensor_scalar", RSC(0), WFA(pr)[:, 0:1024],
                  sm(6), 0.0, ALU.is_gt, ALU.add)
            p.ins("vector", "tensor_scalar", RSC(1), WFA(pr)[:, 0:1024],
                  sm(7), 0.0, ALU.is_lt, ALU.add)
            p.ins("vector", "tensor_sub", RSC(2), RSC(0), RSC(1))
            p.ins("vector", "tensor_scalar", RSC(0), WFA(pr)[:, 1024:2048],
                  sm(6), 0.0, ALU.is_gt, ALU.add)
            p.ins("vector", "tensor_scalar", RSC(1), WFA(pr)[:, 1024:2048],
                  sm(7), 0.0, ALU.is_lt, ALU.add)
            p.ins("vector", "tensor_sub", RSC(3), RSC(0), RSC(1), inc="VE")
            ve_marks[oc] = p.v("VE")
            p.wait("tensor", "VE")
            for tt in range(2):
                bk = bank(4 + tt)
                for ic in range(NCH):
                    wqt = RSC(2) if ic < 8 else RSC(3)
                    mm(wqt[:, 128 * (ic % 8):128 * (ic % 8) + 128],
                       W0IQ(ic)[:, 512 * tt:512 * tt + 512],
                       bk, ic == 0, ic == NCH - 1,
                       inc="TE" if ic == NCH - 1 else None)
            te_marks_f[oc] = p.v("TE")
            if oc == 0 and "wq0" in dbg:
                p.wait("gpsimd", "TE")
                dma("gpsimd", dbg["wq0"][:, :], RSC(2), "STO")
            p.wait("vector", "TE")
            for tt in range(2):
                p.ins("vector", "tensor_scalar", LNT[tt], bank(4 + tt), sm(14),
                      0.0, ALU.mult, ALU.add, inc="VE")
            if oc == 0 and "fs" in dbg:
                p.wait("gpsimd", "VE")
                dma("gpsimd", dbg["fs"][:, :], LNT[0], "STO")
            if oc == 0 and "osc" in dbg:
                p.wait("gpsimd", "VE")
                dma("gpsimd", dbg["osc"][:, 0:1], sm(14), "STO",
                    allow_slow_non_contiguous=True)
            vgate()
            for tt in range(2):
                lo = CC0(oc)[:, 512 * tt:512 * tt + 512]
                hi = CC0(oc)[:, 1024 + 512 * tt:1024 + 512 * tt + 512]
                p.ins("vector", "scalar_tensor_tensor", lo, LNT[tt], mlo, lo,
                      ALU.mult, ALU.add)
                p.ins("vector", "scalar_tensor_tensor", hi, LNT[tt], mhi, hi,
                      ALU.mult, ALU.add)
            p.ins("vector", "nop", inc="VE")
            p.wait("tensor", "VE")
        tap_chunks("cb", CC0)

        p.wait("sync", "VE")
        for oc in range(NCH):
            dma("sync", cb_d[128 * oc:128 * oc + 128, :], CC0(oc), "STO")
        p.wait("gpsimd", "STO")
        p.ins("gpsimd", "collective_compute", "AllReduce", ALU.add,
              replica_groups=[[0, 1], [2, 3], [4, 5], [6, 7]],
              ins=[cb_d.opt()], outs=[ar_d.opt()], inc="CC")

        p.wait("sync", "CC")
        ASF = lambda par, i: fv(8 + 6 * par + 2 * i, 512)
        AXO = lambda par, i: fv(27 + 4 * par + 2 * i, 512)
        ve_mk, sto_mk = {}, {}
        it = 0
        for oc in range(NCH):
            for ht in range(2):
                par = it % 2
                if it >= 2:
                    p.wait("sync", "VE", ve_mk[it - 2])
                r0, r1 = RSC(2 * par), RSC(2 * par + 1)
                dma("sync", r0[:, 0:512], ar_d[128 * oc:128 * oc + 128,
                                               512 * ht:512 * ht + 512], "LX")
                dma("sync", r1[:, 0:512], ar_d[128 * oc:128 * oc + 128,
                                               1024 + 512 * ht:1536 + 512 * ht],
                    "LX")
                dma("sync", AXO(par, 0), xT[128 * oc:128 * oc + 128,
                                            512 * ht:512 * ht + 512], "LX")
                dma("sync", AXO(par, 1), xT[128 * oc:128 * oc + 128,
                                            1024 + 512 * ht:1536 + 512 * ht],
                    "LX")
                lx0 = p.v("LX")
                p.wait("vector", "LX", lx0)
                if it >= 2:
                    p.wait("vector", "STO", sto_mk[it - 2])
                p.ins("vector", "scalar_tensor_tensor", ASF(par, 0),
                      r0[:, 0:512], BIAS[:, B_BO0 + oc:B_BO0 + oc + 1],
                      AXO(par, 0), ALU.add, ALU.add)
                p.ins("vector", "scalar_tensor_tensor", ASF(par, 1),
                      r1[:, 0:512], BIAS[:, B_BO0 + oc:B_BO0 + oc + 1],
                      AXO(par, 1), ALU.add, ALU.add, inc="VE")
                p.ins("vector", "tensor_copy", STG(2 * par), ASF(par, 0))
                p.ins("vector", "tensor_copy", STG(2 * par + 1), ASF(par, 1),
                      inc="VE")
                p.ins("vector", "tensor_scalar", ASF(par, 2), ASF(par, 0),
                      mlo, 0.0, ALU.mult, ALU.add)
                p.ins("vector", "scalar_tensor_tensor", ASF(par, 2),
                      ASF(par, 1), mhi, ASF(par, 2), ALU.mult, ALU.add,
                      inc="VE")
                ve_mk[it] = p.v("VE")
                p.wait("sync", "VE", ve_mk[it])
                dma("sync", x1_d[128 * oc:128 * oc + 128,
                                 512 * ht:512 * ht + 512], STG(2 * par),
                    "STO")
                dma("sync", x1_d[128 * oc:128 * oc + 128,
                                 1024 + 512 * ht:1536 + 512 * ht],
                    STG(2 * par + 1), "STO")
                dma("sync", x1o_d[128 * oc:128 * oc + 128,
                                  512 * ht:512 * ht + 512], ASF(par, 2),
                    "STO")
                sto_mk[it] = p.v("STO")
                it += 1
        if "x1" in dbg:
            p.wait("gpsimd", "STO")
            for oc in range(NCH):
                dma("gpsimd", dbg["x1"][128 * oc:128 * oc + 128, :],
                    x1_d[128 * oc:128 * oc + 128, :], "STO")
        if "arx" in dbg:
            p.wait("gpsimd", "STO")
            for oc in range(NCH):
                dma("gpsimd", dbg["arx"][128 * oc:128 * oc + 128, :],
                    ar_d[128 * oc:128 * oc + 128, :], "STO")

        def load_x1(dc, t0, dst):
            if dc == 0:
                dma_tiles("sync", bv(64, 8192),
                          x1_d[:, t0:t0 + 512].rearrange("(n p) w -> n p w",
                                                         p=128), 16, 512, "LX")

        def l1_post(t0):
            p.wait("sync", "VE")
            for dc in range(NCH):
                dma("sync", xn2_d[128 * dc:128 * dc + 128, t0:t0 + 512],
                    XN(dc)[:, t0:t0 + 512], "STO")

        layernorm(load_x1, lambda dc, t0: XN(dc)[:, t0:t0 + 512], 10,
                  post=l1_post)
        tap_chunks("xn2", XN)

        p.wait("gpsimd", "VE")
        p.ins("gpsimd", "iota", TRS[0:64, 0:1], pattern=[[0, 1]], base=0,
              channel_multiplier=1, allow_small_or_imprecise_dtypes=True,
              inc="GE")
        p.wait("scalar", "GE")
        p.ins("scalar", "activation", sm(21)[0:64, :], TRS[0:64, 0:1], AF.Exp,
              bias=sm(15)[0:64, :], scale=-math.log(10000.0) / 64.0,
              inc="SE")
        p.wait("gpsimd", "SE")
        p.ins("gpsimd", "iota", COSF[0:64, :], pattern=[[1, 2048]], base=0,
              channel_multiplier=0, allow_small_or_imprecise_dtypes=True,
              inc="GE")
        p.wait("vector", "GE")
        p.wait("vector", "SE")
        p.ins("vector", "tensor_scalar", COSF[0:64, :], COSF[0:64, :],
              sm(21)[0:64, :], 0.0, ALU.mult, ALU.add, inc="VE")
        p.wait("vector", "VE")
        SCBOTH = fv(8, 4096)
        TWO_PI = 2 * math.pi
        for seg in range(8):
            s0 = 512 * (seg % 4)
            half_turn = 0.25 if seg >= 4 else 0.0
            dst = (SINF if seg < 4 else COSF)[0:64, s0:s0 + 512]
            xs = COSF[0:64, s0:s0 + 512] if seg < 4 else dst
            if seg >= 4:
                pass
            p.ins("vector", "tensor_scalar", TRS[0:64, 0:512], xs,
                  1.0 / TWO_PI, half_turn, ALU.mult, ALU.add, inc="VE")
            p.wait("vector", "VE")
            p.ins("vector", "tensor_scalar", TRS[0:64, 0:512],
                  TRS[0:64, 0:512], MAGIC, MAGIC, ALU.add, ALU.subtract,
                  inc="VE")
            p.wait("vector", "VE")
            p.ins("vector", "scalar_tensor_tensor", dst,
                  TRS[0:64, 0:512], -TWO_PI, xs, ALU.mult, ALU.add,
                  inc="VE")
            if seg >= 4:
                p.wait("vector", "VE")
                p.ins("vector", "tensor_scalar", dst, dst,
                      half_turn * TWO_PI, 0.0, ALU.add, ALU.add, inc="VE")
        p.wait("scalar", "VE")
        p.ins("scalar", "activation", SCBOTH[0:64, :], SCBOTH[0:64, :],
              AF.Sin, bias=sm(15)[0:64, :], scale=1.0, inc="SE")
        if "sin" in dbg:
            p.wait("gpsimd", "SE")
            dma("gpsimd", dbg["sin"][:, :], SINF[0:64, :], "STO")
        if "cos" in dbg:
            p.wait("gpsimd", "SE")
            dma("gpsimd", dbg["cos"][:, :], COSF[0:64, :], "STO")
        p.wait("sync", "SE")
        dma("sync", SINF[64:128, :], SINF[0:64, :], "LX")
        dma("sync", COSF[64:128, :], COSF[0:64, :], "LX")
        lx_rt = p.v("LX")
        p.wait("vector", "LX", lx_rt)
        p.ins("vector", "tensor_scalar", SINF[0:64, :], SINF[0:64, :], -1.0,
              0.0, ALU.mult, ALU.add, inc="VE")
        p.wait("tensor", "VE")

        dma_tiles("gpsimd", bv(96, 16384), wv1[:, :, :], 16, 1024, "LW")
        lw_v1 = p.v("LW")
        bcast8(BIAS[:, B_BV1:B_BV1 + 8], VBB)
        p.wait("tensor", "LW", lw_v1)
        for kc in range(KCH):
            t0 = KTOK[kc]
            for half in range(2):
                bk = bank(half)
                for dc in range(NCH):
                    mm(XN(dc)[:, t0:t0 + 128],
                       VSL(dc)[:, 512 * half:512 * half + 512],
                       bk, dc == 0, dc == NCH - 1,
                       inc="TE" if dc == NCH - 1 else None)
            p.wait("vector", "TE")
            p.ins("vector", "tensor_add", VT(kc)[:, 0:512], bank(0),
                  VBB[:, 0:512])
            p.ins("vector", "tensor_add", VT(kc)[:, 512:1024], bank(1),
                  VBB[:, 512:1024], inc="VE")
            p.wait("tensor", "VE")

        def rope(dst, ncol, pos0):
            p.wait("tensor", "VE")
            p.wait("tensor", "SE")
            nmm = (ncol + 511) // 512
            for s in range(nmm):
                c0 = s * 512
                w = min(512, ncol - c0)
                mm(P64, dst[:, c0:c0 + w], bank(s, w), True, True,
                   inc="TE" if s == nmm - 1 else None)
            p.wait("vector", "TE")
            ca = COSF[:, pos0:pos0 + ncol]
            sa = SINF[:, pos0:pos0 + ncol]
            p.ins("vector", "tensor_mul", AB[:, 0:ncol], dst[:, 0:ncol], ca)
            p.ins("vector", "tensor_mul", dst[:, 0:ncol],
                  PS[0:128, 0:ncol], sa)
            p.ins("vector", "tensor_add", dst[:, 0:ncol], dst[:, 0:ncol],
                  AB[:, 0:ncol], inc="VE")
            p.wait("tensor", "VE")

        def l1_attn(h, pr):
            for qt in range(4):
                for grp in range(2):
                    cs = range(3 * grp, 3 * grp + 3)
                    for i, c in enumerate(cs):
                        mm(KH(pr)[:, 128 * c:128 * c + 128],
                           QH(pr)[:, 512 * qt:512 * qt + 512],
                           bank(i), True, True,
                           inc="TE" if i == 2 else None)
                    p.wait("scalar", "TE")
                    p.wait("scalar", "VE")
                    p.ins("scalar", "activation", ABW, PS[0:128, 0:1536],
                          AF.Exp, bias=sm(2), scale=ISQ, inc="SE")
                    p.wait("vector", "SE")
                    if grp == 0:
                        p.ins("vector", "tensor_scalar", ABW[:, 512:1024],
                              ABW[:, 512:1024], sm(20), 0.0, ALU.mult,
                              ALU.add)
                    p.ins("vector", "nop", inc="VE")
                    p.wait("tensor", "VE")
                    for i, c in enumerate(cs):
                        off = 512 * i
                        first = (grp == 0 and i == 0)
                        last = (grp == 1 and i == 2)
                        mm(ONESB, ABW[:, off:off + 512], bank(6), first,
                           last)
                        mm(VT(c)[:, 128 * h:128 * h + 128],
                           ABW[:, off:off + 512], bank(7), first, last,
                           inc="TE" if i == 2 else None)
                p.wait("vector", "TE")
                p.ins("vector", "reciprocal", LNT[0], bank(6), inc="VE")
                vgate()
                p.ins("vector", "tensor_mul",
                      OH(h)[:, 512 * qt:512 * qt + 512], bank(7), LNT[0],
                      inc="VE")
                p.wait("tensor", "VE")

        te_head_done = {}
        for h in range(HPC):
            pr = h % 2
            qk_head(h, pr, wqk1, B_QK1 + h, B_QK1 + 8 + h, l1=True)
            p.wait("vector", "SE")
            rope(QH(pr), 2048, 0)
            rope(KH(pr)[:, 0:256], 256, 0)
            rope(KH(pr)[:, 256:768], 512, 1536)
            mhat(pr, ((0, 512), (256, 512)))
            l1_attn(h, pr)
            te_head_done[h] = p.v("TE")
        if "q1" in dbg:
            tap_one("q1", QH(1))
        if "k1" in dbg:
            tap_one("k1", KH(1)[:, 0:1024])
        if "oh1" in dbg:
            p.wait("gpsimd", "VE")
            for h in range(HPC):
                dma("gpsimd", dbg["oh1"][:, 2048 * h:2048 * h + 2048], OH(h),
                    "STO")

        for t0 in (0, 512):
            for dc in range(NCH):
                p.wait("sync", "VE")
                dma("sync", RSC(0)[:, 0:512], xn2_d[128 * dc:128 * dc + 128,
                                          t0:t0 + 512], "LX")
                dma("sync", RSC(1)[:, 0:512], xn2_d[128 * dc:128 * dc + 128,
                                          1024 + t0:1024 + t0 + 512], "LX")
                lx0 = p.v("LX")
                p.wait("vector", "LX", lx0)
                p.ins("vector", "tensor_scalar", STG(0), RSC(0)[:, 0:512],
                      mlo, 0.0, ALU.mult, ALU.add)
                p.ins("vector", "scalar_tensor_tensor",
                      XR(dc)[:, t0:t0 + 512], RSC(1)[:, 0:512], mhi, STG(0),
                      ALU.mult, ALU.add, inc="VE")
            p.wait("tensor", "VE")
            for dc in range(NCH):
                mm(ONESB, XR(dc)[:, t0:t0 + 512], bank(0), dc == 0,
                   dc == NCH - 1, inc="TE" if dc == NCH - 1 else None)
            te_marks = []
            for dc in range(NCH):
                if dc >= 2:
                    p.wait("vector", "TE", te_marks[dc - 2])
                sq = STG(0) if dc % 2 == 0 else STG(1)
                p.ins("vector", "tensor_mul", sq, XR(dc)[:, t0:t0 + 512],
                      XR(dc)[:, t0:t0 + 512], inc="VE")
                p.wait("tensor", "VE")
                mm(ONESB, sq, bank(1), dc == 0, dc == NCH - 1, inc="TE")
                te_marks.append(p.v("TE"))
            p.wait("vector", "TE")
            p.ins("vector", "tensor_scalar", LNT[0], bank(0), 1.0 / D, 0.0,
                  ALU.mult, ALU.add)
            p.ins("vector", "tensor_scalar", LNT[1], bank(1), 1.0 / D, 0.0,
                  ALU.mult, ALU.add, inc="VE")
            vgate()
            p.ins("vector", "tensor_mul", LNT[2], LNT[0], LNT[0])
            p.ins("vector", "tensor_sub", LNT[1], LNT[1], LNT[2], inc="VE")
            p.wait("scalar", "VE")
            p.ins("scalar", "activation", LNT[2], LNT[1], AF.Sqrt,
                  bias=sm(12), scale=1.0, inc="SE")
            p.wait("vector", "SE")
            p.ins("vector", "reciprocal", LNT[1], LNT[2], inc="VE")
            p.wait("vector", "VE")
            p.ins("vector", "tensor_mul", LNT[2], LNT[0], LNT[1], inc="VE")
            for dc in range(NCH):
                sq = STG(0) if dc % 2 == 0 else STG(1)
                p.ins("vector", "tensor_mul", sq, XR(dc)[:, t0:t0 + 512],
                      LNT[1])
                p.ins("vector", "tensor_sub", XR(dc)[:, t0:t0 + 512], sq,
                      LNT[2], inc="VE")

        p.ins("vector", "memset", SM[:, 40:48], 0.0, inc="VE")
        for dc in range(NCH):
            for tch in range(8):
                p.wait("vector", "TE")
                p.ins("vector", "tensor_copy", TRS[:, 0:128],
                      XR(dc)[:, 128 * tch:128 * tch + 128], inc="VE")
                p.wait("tensor", "VE")
                p.ins("tensor", "transpose", bank(4, 128), TRS[:, 0:128], IDN,
                      inc="TE")
                p.wait("vector", "TE")
                p.ins("vector", "reduce_max", sm(28), bank(4, 128), axis=AX.X,
                      apply_absolute_value=True, inc="VE")
                vgate()
                p.ins("vector", "tensor_max", SM[:, 40 + tch:41 + tch],
                      SM[:, 40 + tch:41 + tch], sm(28), inc="VE")
        if "rmax" in dbg:
            p.wait("gpsimd", "VE")
            dma("gpsimd", dbg["rmax"][:, 0:8], SM[:, 40:48], "STO")
        bcast8(SM[:, 40:48], VBB)
        p.ins("vector", "tensor_copy", BCQ(0), VBB[:, 0:512])
        p.ins("vector", "tensor_copy", BCQ(1), VBB[:, 512:1024], inc="VE")
        p.wait("vector", "VE")
        p.ins("vector", "reciprocal", BCQ(2), BCQ(0))
        p.ins("vector", "reciprocal", BCQ(3), BCQ(1), inc="VE")
        p.wait("vector", "VE")
        for dc in range(NCH):
            for tt in range(2):
                p.ins("vector", "scalar_tensor_tensor", LNT[0],
                      XR(dc)[:, 512 * tt:512 * tt + 512], 127.0, BCQ(2 + tt),
                      ALU.mult, ALU.mult)
                p.ins("vector", "tensor_scalar", LNT[0], LNT[0], MAGIC, MAGIC,
                      ALU.add, ALU.subtract)
                p.ins("vector", "tensor_copy",
                      XR(dc)[:, 512 * tt:512 * tt + 512], LNT[0], inc="VE")
        p.ins("vector", "tensor_scalar", BCQ(0), BCQ(0), 2.0 / 127.0, 0.0,
              ALU.mult, ALU.add)
        p.ins("vector", "tensor_scalar", BCQ(1), BCQ(1), 2.0 / 127.0, 0.0,
              ALU.mult, ALU.add, inc="VE")
        if "i1" in dbg:
            p.wait("gpsimd", "VE")
            for dc in range(NCH):
                dma("gpsimd", dbg["i1"][128 * dc:128 * dc + 128, :], XR(dc),
                    "STO")

        wo_pass(wo1)

        te_marks_g = {}
        p.wait("gpsimd", "TE")
        ve_marks_g = {}
        for osb in range(16):
            pr = osb % 2
            if osb >= 2:
                p.wait("gpsimd", "TE", te_marks_g[osb - 2])
            dma_tiles("gpsimd", GSL(pr), wg1[osb, :, :, :], 16, 512, "LW")
            dma_tiles("gpsimd", USL(pr), wu1[osb, :, :, :], 16, 512, "LW")
            lwg = p.v("LW")
            p.wait("vector", "LW", lwg)
            if osb >= 2:
                p.wait("vector", "TE", te_marks_g[osb - 2])
            p.ins("vector", "tensor_scalar", GSL(pr), GSL(pr), 0.0, 0.5,
                  ALU.is_gt, ALU.subtract)
            p.ins("vector", "tensor_scalar", USL(pr), USL(pr), 0.0, 0.5,
                  ALU.is_gt, ALU.subtract, inc="VE")
            ve_marks_g[osb] = p.v("VE")
            p.wait("tensor", "VE", ve_marks_g[osb])
            for oc4 in range(4):
                oc = 4 * osb + oc4
                for tt in range(2):
                    bkg, bku = bank(4), bank(5)
                    for ic in range(NCH):
                        mm(GSL(pr)[:, 512 * ic + 128 * oc4:512 * ic + 128 * oc4 + 128],
                           XR(ic)[:, 512 * tt:512 * tt + 512],
                           bkg, ic == 0, ic == NCH - 1)
                    for ic in range(NCH):
                        mm(USL(pr)[:, 512 * ic + 128 * oc4:512 * ic + 128 * oc4 + 128],
                           XR(ic)[:, 512 * tt:512 * tt + 512],
                           bku, ic == 0, ic == NCH - 1,
                           inc="TE" if ic == NCH - 1 else None)
                    p.wait("vector", "TE")
                    if not (osb == 0 and oc4 == 0 and tt == 0):
                        p.wait("vector", "STO")
                    p.ins("vector", "tensor_mul", GUT, bkg,
                          BCQ(tt), inc="VE")
                    p.wait("scalar", "VE")
                    p.ins("scalar", "activation", GUT,
                          GUT, AF.Silu, bias=sm(15), scale=1.0,
                          inc="SE")
                    p.ins("vector", "tensor_mul", UUT, bku,
                          BCQ(tt), inc="VE")
                    p.wait("vector", "SE")
                    p.ins("vector", "tensor_mul", GUT,
                          GUT, UUT, inc="VE")
                    p.wait("sync", "VE")
                    dma("sync", p_d[128 * oc:128 * oc + 128,
                                    512 * tt:512 * tt + 512],
                        GUT, "STO")
                    p.wait("tensor", "VE")
            te_marks_g[osb] = p.v("TE")

        if "pd" in dbg:
            p.wait("gpsimd", "STO")
            for oc in range(8):
                dma("gpsimd", dbg["pd"][128 * oc:128 * oc + 128, :],
                    p_d[128 * oc:128 * oc + 128, :], "STO")

        mm_done = {}
        p.wait("gpsimd", "TE")
        p.wait("vector", "TE")
        for th in range(2):
            p.wait("sync", "TE")
            dma_tiles("sync", bv(64, 32768),
                      p_d[:, 512 * th:512 * th + 512]
                      .rearrange("(n p) w -> n p w", p=128), 64, 512, "LX")
            lxp = p.v("LX")
            p.wait("tensor", "LX", lxp)
            for d2c in range(NCH):
                for half in range(2):
                    pr = half
                    if d2c >= 2:
                        p.wait("gpsimd", "TE", mm_done[(th, d2c - 2)])
                    elif th == 1 and d2c == 0 and half == 0:
                        p.wait("gpsimd", "TE", mm_done[(0, NCH - 1)])
                        p.wait("gpsimd", "STO")
                    elif th == 1 and d2c == 1 and half == 0:
                        pass
                    if th == 0:
                        dma_tiles("gpsimd", WDS2(pr, d2c % 2),
                                  wd1[d2c, 32 * half:32 * half + 32, :, :],
                                  32, 128, "LW")
                        lw0 = p.v("LW")
                        p.wait("vector", "LW", lw0)
                        if d2c >= 2:
                            p.wait("vector", "TE", mm_done[(th, d2c - 2)])
                        p.ins("vector", "tensor_scalar", WDS2(pr, d2c % 2),
                              WDS2(pr, d2c % 2), 0.0, 0.5, ALU.is_gt,
                              ALU.subtract, inc="VE")
                        p.wait("sync", "VE")
                        dma("sync",
                            wdc_d[d2c, 32 * half:32 * half + 32, :, :]
                            .rearrange("n p w -> p n w"),
                            WDS2(pr, d2c % 2), "STO")
                    else:
                        dma_tiles("gpsimd", WDS2(pr, d2c % 2),
                                  wdc_d[d2c, 32 * half:32 * half + 32, :, :],
                                  32, 128, "LW")
                lw1, ve1 = p.v("LW"), p.v("VE")
                p.wait("tensor", "LW", lw1)
                p.wait("tensor", "VE", ve1)
                bk = bank(6 + (d2c % 2))
                for oc in range(64):
                    mm(WDS2(oc // 32, d2c % 2)[:, 128 * (oc % 32):
                                               128 * (oc % 32) + 128],
                       PQ(oc), bk, oc == 0, oc == 63,
                       inc="TE" if oc == 63 else None)
                mm_done[(th, d2c)] = p.v("TE")
                p.wait("vector", "TE")
                p.ins("vector", "tensor_scalar", LNT[0], bk, 2.0, 0.0,
                      ALU.mult, ALU.add, inc="VE")
                vgate()
                lo = CC0(d2c)[:, 512 * th:512 * th + 512]
                hi = CC0(d2c)[:, 1024 + 512 * th:1536 + 512 * th]
                p.ins("vector", "scalar_tensor_tensor", lo, LNT[0], mlo, lo,
                      ALU.mult, ALU.add)
                p.ins("vector", "scalar_tensor_tensor", hi, LNT[0], mhi, hi,
                      ALU.mult, ALU.add)
                p.ins("vector", "nop", inc="VE")

        tap_chunks("cb1", CC0)
        p.wait("sync", "VE")
        for oc in range(NCH):
            dma("sync", cb_d[128 * oc:128 * oc + 128, :], CC0(oc), "STO")
        p.wait("gpsimd", "STO")
        p.ins("gpsimd", "collective_compute", "AllReduce", ALU.add,
              replica_groups=[[0, 1], [2, 3], [4, 5], [6, 7]],
              ins=[cb_d.opt()], outs=[ar_d.opt()], inc="CC")

        p.wait("sync", "CC")
        if "ar1" in dbg:
            p.wait("gpsimd", "CC")
            for oc in range(NCH):
                dma("gpsimd", dbg["ar1"][128 * oc:128 * oc + 128, :],
                    ar_d[128 * oc:128 * oc + 128, :], "STO")
        FSF = lambda par, i: fv(8 + 6 * par + 2 * i, 512)
        FXO = lambda par: fv(27 + 4 * par, 512)
        ve_mk2, sto_mk2 = {}, {}
        it = 0
        for oc in range(NCH):
            for ht in range(2):
                par = it % 2
                if it >= 2:
                    p.wait("sync", "VE", ve_mk2[it - 2])
                r0, r1 = RSC(2 * par), RSC(2 * par + 1)
                dma("sync", r0[:, 0:512], ar_d[128 * oc:128 * oc + 128,
                                               512 * ht:512 * ht + 512], "LX")
                dma("sync", r1[:, 0:512], ar_d[128 * oc:128 * oc + 128,
                                               1024 + 512 * ht:1536 + 512 * ht],
                    "LX")
                dma("sync", FXO(par), x1o_d[128 * oc:128 * oc + 128,
                                            512 * ht:512 * ht + 512], "LX")
                lx0 = p.v("LX")
                p.wait("vector", "LX", lx0)
                if it >= 2:
                    p.wait("vector", "STO", sto_mk2[it - 2])
                p.ins("vector", "tensor_scalar", FSF(par, 1), r0[:, 0:512],
                      mlo, 0.0, ALU.mult, ALU.add)
                p.ins("vector", "scalar_tensor_tensor", FSF(par, 1),
                      r1[:, 0:512], mhi, FSF(par, 1), ALU.mult, ALU.add)
                p.ins("vector", "scalar_tensor_tensor", FSF(par, 2),
                      FSF(par, 1), BIAS[:, B_BO1 + oc:B_BO1 + oc + 1],
                      FXO(par), ALU.add, ALU.add, inc="VE")
                ve_mk2[it] = p.v("VE")
                p.wait("sync", "VE", ve_mk2[it])
                dma("sync", out_own[128 * oc:128 * oc + 128,
                                    512 * ht:512 * ht + 512], FSF(par, 2),
                    "STO")
                sto_mk2[it] = p.v("STO")
                it += 1

        p.wait("sync", "STO")
        p.wait("sync", "LX")
        p.wait("sync", "LW")

        block = ctx.enter_context(nc.Block())

        @block.sync
        def _(e):
            for f in p.ops["sync"]:
                f(e)

        @block.gpsimd
        def _(e):
            for f in p.ops["gpsimd"]:
                f(e)

        @block.tensor
        def _(e):
            for f in p.ops["tensor"]:
                f(e)

        @block.vector
        def _(e):
            for f in p.ops["vector"]:
                f(e)

        @block.scalar
        def _(e):
            for f in p.ops["scalar"]:
                f(e)

    return nc


def _rope_perm():
    pm = np.zeros(128, np.int64)
    pm[0:64] = np.arange(0, 128, 2)
    pm[64:128] = np.arange(1, 128, 2)
    return pm


def _shard(inputs):
    f32 = np.float32
    x = np.asarray(inputs["L0_attn_x"], f32)
    Wqkv = np.asarray(inputs["L0_attn_Wqkv"], f32)
    bqkv = np.asarray(inputs["L0_attn_bqkv"], f32)
    Wo0 = np.asarray(inputs["L0_attn_Wo"], f32)
    bo0 = np.asarray(inputs["L0_attn_bo"], f32)
    W0 = np.asarray(inputs["L0_ffn_weight"], f32)
    g0 = np.asarray(inputs["L0_ffn_norm_weight"], f32)
    Wq1 = np.asarray(inputs["L1_attn_Wq"], f32)
    bq1 = np.asarray(inputs["L1_attn_bq"], f32)
    Wk1 = np.asarray(inputs["L1_attn_Wk"], f32)
    bk1 = np.asarray(inputs["L1_attn_bk"], f32)
    Wv1 = np.asarray(inputs["L1_attn_Wv"], f32)
    bv1 = np.asarray(inputs["L1_attn_bv"], f32)
    Wo1 = np.asarray(inputs["L1_attn_Wo"], f32)
    bo1 = np.asarray(inputs["L1_attn_bo"], f32)
    Wg = np.asarray(inputs["L1_ffn_W_g"], f32)
    Wu = np.asarray(inputs["L1_ffn_W_u"], f32)
    Wd = np.asarray(inputs["L1_ffn_W_d"], f32)

    pm = _rope_perm()
    W0T = np.ascontiguousarray(W0.T)
    w0t = np.ascontiguousarray(
        W0T.reshape(NCH, 128, NCH, 128).transpose(2, 0, 1, 3))
    wg1 = np.ascontiguousarray(
        Wg.T.reshape(NCH, 128, 16, 512).transpose(2, 0, 1, 3))
    wu1 = np.ascontiguousarray(
        Wu.T.reshape(NCH, 128, 16, 512).transpose(2, 0, 1, 3))
    wd1 = np.ascontiguousarray(
        Wd.T.reshape(64, 128, NCH, 128).transpose(2, 0, 1, 3))
    Wq1T, Wk1T, Wv1T, Wo1T = (np.ascontiguousarray(w.T)
                              for w in (Wq1, Wk1, Wv1, Wo1))

    shards = []
    for c in range(8):
        pair, m = divmod(c, 2)
        heads = list(range(8 * m, 8 * m + 8))
        wq_cols = np.stack([Wqkv[:, 128 * h:128 * h + 128] for h in heads])
        wk_cols = np.stack([Wqkv[:, 2048 + 128 * h:2048 + 128 * h + 128]
                            for h in heads])
        wqkv_h = np.ascontiguousarray(
            np.concatenate([wq_cols, wk_cols], axis=2)
            .reshape(HPC, NCH, 128, 256))
        vcols = Wqkv[:, 4096 + 1024 * m:4096 + 1024 * m + 1024]
        wv0_t = np.ascontiguousarray(vcols.reshape(NCH, 128, 1024))
        bq = bqkv[1024 * m:1024 * m + 1024]
        bk = bqkv[2048 + 1024 * m:2048 + 1024 * m + 1024]
        bvv = bqkv[4096 + 1024 * m:4096 + 1024 * m + 1024]
        bqkv_t = np.concatenate([bq.reshape(8, 128).T, bk.reshape(8, 128).T,
                                 bvv.reshape(8, 128).T], axis=1)
        wo0_m = Wo0[1024 * m:1024 * m + 1024, :]
        wo0_t = np.ascontiguousarray(
            wo0_m.reshape(HPC, 128, NCH, 128).transpose(2, 0, 1, 3))
        wq1l, wk1l, bq1l, bk1l = [], [], [], []
        for h in heads:
            cols = 128 * h + pm
            wq1l.append(Wq1T[:, cols])
            wk1l.append(Wk1T[:, cols])
            bq1l.append(bq1[cols])
            bk1l.append(bk1[cols])
        wqk1_t = np.ascontiguousarray(
            np.concatenate([np.stack(wq1l), np.stack(wk1l)], axis=2)
            .reshape(HPC, NCH, 128, 256))
        bqk1_t = np.concatenate(
            [np.stack(bq1l).reshape(8, 128).T,
             np.stack(bk1l).reshape(8, 128).T], axis=1)
        wv1_m = Wv1T[:, 1024 * m:1024 * m + 1024]
        wv1_t = np.ascontiguousarray(wv1_m.reshape(NCH, 128, 1024))
        bv1_t = np.ascontiguousarray(
            bv1[1024 * m:1024 * m + 1024].reshape(8, 128).T)
        wo1_m = Wo1T[1024 * m:1024 * m + 1024, :]
        wo1_t = np.ascontiguousarray(
            wo1_m.reshape(HPC, 128, NCH, 128).transpose(2, 0, 1, 3))
        msel_t = np.zeros((128, 2), f32)
        msel_t[:, m] = 1.0
        shards.append({
            "xT": np.ascontiguousarray(x[pair].T),
            "wqkv": wqkv_h, "wv0": wv0_t,
            "bqkv": np.ascontiguousarray(bqkv_t),
            "wo0": wo0_t, "bo0": np.ascontiguousarray(bo0.reshape(16, 128).T),
            "w0t": w0t, "g0": np.ascontiguousarray(g0.reshape(16, 128).T),
            "wqk1": wqk1_t, "wv1": wv1_t,
            "bqk1": np.ascontiguousarray(bqk1_t), "bv1": bv1_t,
            "wo1": wo1_t, "bo1": np.ascontiguousarray(bo1.reshape(16, 128).T),
            "wg1": wg1, "wu1": wu1, "wd1": wd1,
            "msel": msel_t,
        })
    return shards


def kernel(**inputs):
    if "nc" not in _CACHE:
        _CACHE["nc"] = _build()
    nc = _CACHE["nc"]
    shards = _shard(inputs)
    res = run_bass_kernel_spmd(nc, shards, core_ids=list(range(8)))
    out = np.zeros((4, TALL, D), np.float32)
    for c in range(8):
        pair, m = divmod(c, 2)
        out[pair, 1024 * m:1024 * m + 1024, :] = res.results[c]["out_own"].T
    if DBG:
        kernel.dbg = [{k: v for k, v in r.items() if k.startswith("dbg_")}
                      for r in res.results]
    return out


# version 08 (final)
# speedup vs baseline: 1.1087x; 1.0164x over previous
import contextlib
import math

import numpy as np
import ml_dtypes

import concourse.bass as bass
import concourse.mybir as mybir
from concourse.bass_utils import run_bass_kernel_spmd

F32 = mybir.dt.float32
BF16 = mybir.dt.bfloat16
AF = mybir.ActivationFunctionType
ALU = mybir.AluOpType
AX = mybir.AxisListType

MAGIC = float(np.float32(1.5 * 2 ** 23))
D = 2048
NCH = 16
TALL = 2048
TOWN = 1024
HPC = 8
ISQ = 1.0 / math.sqrt(128.0)
KCH = 6
KTOK = [0, 128, 1536, 1664, 1792, 1920]

ENGS = ("sync", "gpsimd", "tensor", "vector", "scalar")

DBG = []
_CACHE = {}


class P(object):
    def __init__(self, nc, ctx):
        self.nc = nc
        self.ops = {e: [] for e in ENGS}
        self.sems = {}
        self.ctx = ctx

    def sem(self, name):
        self.sems[name] = [self.ctx.enter_context(self.nc.semaphore(name)), 0]

    def v(self, name):
        return self.sems[name][1]

    def h(self, name):
        return self.sems[name][0]

    def wait(self, eng, sname, val=None):
        h = self.h(sname)
        v = self.v(sname) if val is None else val
        if v > 0:
            self.ops[eng].append(lambda e, h=h, v=v: e.wait_ge(h, v))

    def ins(self, eng, meth, *args, inc=None, incn=1, **kw):
        pair = None
        if inc is not None:
            self.sems[inc][1] += incn
            pair = (self.h(inc), incn)

        def f(e, meth=meth, args=args, kw=kw, pair=pair):
            i = getattr(e, meth)(*args, **kw)
            if pair is not None:
                i.then_inc(*pair)

        self.ops[eng].append(f)


def _build():
    nc = bass.Bass("TRN2", target_bir_lowering=False, debug=False)

    def dram(n, s, d, k=None):
        t = nc.dram_tensor(n, s, d, kind=k) if k else nc.dram_tensor(n, s, d)
        return t.ap()

    xT = dram("xT", [D, TALL], F32, "ExternalInput")
    wqkv = dram("wqkv", [HPC, NCH, 128, 256], F32, "ExternalInput")
    wv0 = dram("wv0", [NCH, 128, 1024], F32, "ExternalInput")
    bqkv = dram("bqkv", [128, 24], F32, "ExternalInput")
    wo0 = dram("wo0", [NCH, HPC, 128, 128], F32, "ExternalInput")
    bo0 = dram("bo0", [128, 16], F32, "ExternalInput")
    w0t = dram("w0t", [NCH, NCH, 128, 128], F32, "ExternalInput")
    g0 = dram("g0", [128, 16], F32, "ExternalInput")
    wqk1 = dram("wqk1", [HPC, NCH, 128, 256], F32, "ExternalInput")
    wv1 = dram("wv1", [NCH, 128, 1024], F32, "ExternalInput")
    bqk1 = dram("bqk1", [128, 16], F32, "ExternalInput")
    bv1 = dram("bv1", [128, 8], F32, "ExternalInput")
    wo1 = dram("wo1", [NCH, HPC, 128, 128], F32, "ExternalInput")
    bo1 = dram("bo1", [128, 16], F32, "ExternalInput")
    wg1 = dram("wg1", [16, NCH, 128, 512], F32, "ExternalInput")
    wu1 = dram("wu1", [16, NCH, 128, 512], F32, "ExternalInput")
    wd1 = dram("wd1", [NCH, 64, 128, 128], F32, "ExternalInput")
    msel = dram("msel", [128, 2], F32, "ExternalInput")

    out_own = dram("out_own", [D, TOWN], F32, "ExternalOutput")

    xpre_d = dram("xpre_d", [D, TALL], BF16)
    gmax_i = dram("gmax_i", [1, 16], F32)
    gmax_o = dram("gmax_o", [8, 16], F32)
    cb_d = dram("cb_d", [D, TALL], BF16)
    ar_d = dram("ar_d", [D, TALL], BF16)
    x1_d = dram("x1_d", [D, TALL], BF16)
    x1o_d = dram("x1o_d", [D, TOWN], F32)
    xn2_d = dram("xn2_d", [D, TALL], BF16)
    p_d = dram("p_d", [8192, TOWN], BF16)
    wdc_d = dram("wdc_d", [NCH, 64, 128, 128], BF16)

    dbg = {}
    for name in DBG:
        shp = {"xn": [D, TALL], "xpre": [D, TALL], "gmax": [128, 4],
               "vt": [128, 16384], "q0": [128, TALL], "k0": [128, TALL],
               "oh": [128, 16384], "cb": [D, TALL], "arx": [D, TALL],
               "x1": [D, TALL], "xn2": [D, TALL], "iq": [D, TOWN],
               "k1": [128, 1024], "q1": [128, TALL], "sin": [64, TALL],
               "cos": [64, TALL], "rmax": [128, 16], "wsc": [128, 4],
               "i1": [D, TOWN], "oh1": [128, 16384], "mh": [128, 4],
               "gmi": [1, 16], "gmo": [8, 16], "lmax": [128, 8],
               "cb1": [D, TALL], "ar1": [D, TALL], "pd": [1024, TOWN],
               "wq0": [128, 1024], "fs": [128, 512], "osc": [128, 4],
               }[name]
        dbg[name] = dram("dbg_" + name, shp, BF16, "ExternalOutput")

    ctx = contextlib.ExitStack()
    with ctx:
        ARB = ctx.enter_context(nc.sbuf_tensor("ARB", [128, 86528], BF16))
        ARF = ctx.enter_context(nc.sbuf_tensor("ARF", [128, 8960], F32))
        PS = ctx.enter_context(nc.psum_tensor("PS", [128, 4096], F32))

        p = P(nc, ctx)
        for s in ("LX", "LW", "TE", "VE", "SE", "GE", "STO", "CC"):
            p.sem(s)

        def bv(kb, ncol, part=128, frac=0):
            c = kb * 512 + frac
            return ARB[0:part, c:c + ncol]

        def fv(kb, ncol, part=128, frac=0):
            c = kb * 256 + frac
            return ARF[0:part, c:c + ncol]

        def bank(b, w=512, part=128):
            return PS[0:part, b * 512:b * 512 + w]

        XN = lambda dc: bv(4 * dc, 2048)
        CC0 = lambda oc: bv(4 * oc, 2048)
        XBT = lambda dc: bv(64 + dc, 512)
        SQ0 = bv(80, 512)
        SQ1 = bv(81, 512)
        VT = lambda tc: bv(64 + 2 * tc, 1024)
        W0IQ = lambda ic: bv(64 + 2 * ic, 1024)
        XR = W0IQ
        PQ = lambda oc: bv(64 + oc, 512)
        VSL = lambda dc: bv(96 + 2 * dc, 1024)
        OH = lambda h: bv(96 + 4 * h, 2048)
        GSL = lambda pr: bv(96 + 32 * pr, 8192)
        USL = lambda pr: bv(112 + 32 * pr, 8192)
        GUT = bv(161, 512)
        UUT = bv(163, 512)
        WDS = lambda pr: bv(128 + 8 * pr, 4096)
        WDS2 = lambda pr, par: bv(128 + 16 * par + 8 * pr, 4096)
        QH = lambda pr: bv(128 + 4 * pr, 2048)
        KH = lambda pr: bv(136 + 4 * pr, 2048)
        WOS = lambda pr: bv(128 + 2 * pr, 1024)
        AB = bv(144, 2048)
        ABW = bv(144, 1536)
        MKE = bv(148, 896)
        MKD = lambda d_: bv(148, 512, frac=384 - 128 * d_)
        ONESB = bv(150, 128)
        P64 = bv(150, 128, frac=128)
        STG = lambda i: bv(151 + i, 512)
        QKS = lambda pr: bv(153 + 8 * pr, 4096)
        RSC = lambda i: bv(153 + 2 * i, 1024)
        VBB = bv(153, 1024)

        LNT = [fv(0, 512), fv(2, 512), fv(4, 512)]
        SM = fv(6, 256)
        sm = lambda i: SM[:, i:i + 1]
        BIAS = fv(7, 256)
        WFA = lambda pr: fv(8, 2048, frac=pr * 2048)
        SINF = fv(8, 2048)
        COSF = fv(16, 2048)
        IDN = fv(24, 128)
        ONESF = fv(24, 128, frac=128)
        TRS = fv(25, 512)
        BCQ = lambda i: fv(27 + 2 * i, 512)
        XO1 = fv(27, 512)
        XO2 = fv(29, 512)

        B_QKV, B_BO0, B_G0, B_QK1, B_BV1, B_BO1, B_MS = 0, 24, 40, 56, 72, 80, 96
        mlo = BIAS[:, B_MS:B_MS + 1]
        mhi = BIAS[:, B_MS + 1:B_MS + 2]

        def dma(q, out, in_, sem_="LX", **kw):
            p.ins(q, "dma_start", inc=sem_, incn=16, out=out, in_=in_, **kw)

        def dma_tiles(q, dst2d, src3d, n, w, sem_="LW"):
            dma(q, dst2d, src3d.rearrange("n p w -> p n w"), sem_)

        def mm(lhsT, rhs, out, st, sp, inc=None):
            p.ins("tensor", "matmul", out, lhsT, rhs, start=st, stop=sp,
                  inc=inc)

        def vgate():
            p.ins("vector", "nop", inc="VE")
            p.wait("vector", "VE")

        p.ins("gpsimd", "memset", ONESB, 1.0)
        p.ins("gpsimd", "memset", ONESF, 1.0)
        p.ins("gpsimd", "memset", IDN, 0.0, inc="GE")
        p.wait("gpsimd", "GE")
        p.ins("gpsimd", "affine_select", out=IDN, in_=IDN,
              compare_op=ALU.not_equal, fill=1.0, base=0,
              pattern=[[-1, 128]], channel_multiplier=1)
        p.ins("gpsimd", "memset", MKE, 1.0, inc="GE")
        p.wait("gpsimd", "GE")
        p.ins("gpsimd", "affine_select", out=MKE, in_=MKE,
              compare_op=ALU.is_ge, fill=0.0, base=-384,
              pattern=[[1, 896]], channel_multiplier=-1)
        p.ins("gpsimd", "memset", P64, 0.0, inc="GE")
        p.wait("gpsimd", "GE")
        p.ins("gpsimd", "affine_select", out=P64, in_=P64,
              compare_op=ALU.not_equal, fill=1.0, base=64,
              pattern=[[1, 128]], channel_multiplier=-1, inc="GE")
        p.wait("gpsimd", "GE")
        p.ins("gpsimd", "affine_select", out=P64, in_=P64,
              compare_op=ALU.not_equal, fill=1.0, base=-64,
              pattern=[[1, 128]], channel_multiplier=-1)
        p.ins("gpsimd", "memset", sm(20), 1.0, inc="GE")
        p.wait("gpsimd", "GE")
        p.ins("gpsimd", "affine_select", out=sm(20), in_=sm(20),
              compare_op=ALU.is_ge, fill=0.0, base=76,
              pattern=[[0, 1]], channel_multiplier=-1, inc="GE")

        dma("sync", BIAS[:, B_QKV:B_QKV + 24], bqkv[:, :])
        dma("sync", BIAS[:, B_BO0:B_BO0 + 16], bo0[:, :])
        dma("sync", BIAS[:, B_G0:B_G0 + 16], g0[:, :])
        dma("sync", BIAS[:, B_QK1:B_QK1 + 16], bqk1[:, :])
        dma("sync", BIAS[:, B_BV1:B_BV1 + 8], bv1[:, :])
        dma("sync", BIAS[:, B_BO1:B_BO1 + 16], bo1[:, :])
        dma("sync", BIAS[:, B_MS:B_MS + 2], msel[:, :])
        lx_bias = p.v("LX")
        p.wait("vector", "LX", lx_bias)
        p.wait("vector", "GE")
        for i, val in ((10, 1e-5), (11, 1e-6), (12, 1e-8), (13, 0.0),
                       (15, 0.0)):
            p.ins("vector", "memset", sm(i), val)
        p.ins("vector", "memset", SM[:, 24:28], 0.0, inc="VE")
        p.wait("tensor", "VE")
        p.wait("tensor", "GE")
        p.wait("scalar", "VE")
        p.wait("gpsimd", "VE")

        def tap_chunks(name, view, w=TALL):
            if name in dbg:
                for e in ("VE", "SE", "TE"):
                    p.wait("gpsimd", e)
                for dc in range(NCH):
                    dma("gpsimd", dbg[name][128 * dc:128 * dc + 128, 0:w],
                        view(dc), "STO")

        def tap_one(name, src, rows=128):
            if name in dbg:
                for e in ("VE", "SE", "TE"):
                    p.wait("gpsimd", e)
                dma("gpsimd", dbg[name][0:rows, 0:src.shape[1]], src, "STO")

        def bcast8(src8, dst):
            p.wait("vector", "TE")
            p.wait("vector", "SE")
            p.ins("vector", "memset", LNT[0][0:8, :], 0.0)
            p.ins("vector", "memset", LNT[1][0:8, :], 0.0, inc="VE")
            p.wait("sync", "VE")
            for j in range(8):
                t = LNT[0] if j < 4 else LNT[1]
                dma("sync", t[j:j + 1, (j % 4) * 128:(j % 4) * 128 + 128],
                    src8[:, j:j + 1], "LX")
            lxg = p.v("LX")
            p.wait("tensor", "LX", lxg)
            p.wait("tensor", "VE")
            mm(ONESF[0:8, :], LNT[0][0:8, :], bank(4), True, True)
            mm(ONESF[0:8, :], LNT[1][0:8, :], bank(5), True, True, inc="TE")
            p.wait("vector", "TE")
            p.ins("vector", "tensor_copy", dst[:, 0:512], bank(4))
            p.ins("vector", "tensor_copy", dst[:, 512:1024], bank(5), inc="VE")
            vgate()

        def pbcast(srccol, dstslot):
            p.wait("tensor", "VE")
            mm(ONESF[0:1, :], TRS[0:1, srccol:srccol + 1], bank(4, 1),
               True, True, inc="TE")
            p.wait("vector", "TE")
            p.ins("vector", "tensor_copy", sm(dstslot), bank(4, 1),
                  inc="VE")
            vgate()

        def layernorm(load, xdst, eps_slot, post=None):
            for t0 in (0, 512, 1024, 1536):
                p.wait("gpsimd", "VE")
                p.wait("sync", "VE")
                for dc in range(NCH):
                    load(dc, t0, XBT(dc))
                lx0, lw0 = p.v("LX"), p.v("LW")
                for e in ("tensor", "vector"):
                    p.wait(e, "LX", lx0)
                    p.wait(e, "LW", lw0)
                p.wait("tensor", "VE")
                for dc in range(NCH):
                    mm(ONESB, XBT(dc), bank(0), dc == 0, dc == NCH - 1,
                       inc="TE" if dc == NCH - 1 else None)
                te_marks = []
                for dc in range(NCH):
                    if dc >= 2:
                        p.wait("vector", "TE", te_marks[dc - 2])
                    sq = SQ0 if dc % 2 == 0 else SQ1
                    p.ins("vector", "tensor_mul", sq, XBT(dc), XBT(dc),
                          inc="VE")
                    p.wait("tensor", "VE")
                    mm(ONESB, sq, bank(1), dc == 0, dc == NCH - 1, inc="TE")
                    te_marks.append(p.v("TE"))
                p.wait("vector", "TE")
                p.ins("vector", "tensor_scalar", LNT[0], bank(0), 1.0 / D, 0.0,
                      ALU.mult, ALU.add)
                p.ins("vector", "tensor_scalar", LNT[1], bank(1), 1.0 / D, 0.0,
                      ALU.mult, ALU.add, inc="VE")
                vgate()
                p.ins("vector", "tensor_mul", LNT[2], LNT[0], LNT[0])
                p.ins("vector", "tensor_sub", LNT[1], LNT[1], LNT[2], inc="VE")
                p.wait("scalar", "VE")
                p.ins("scalar", "activation", LNT[2], LNT[1], AF.Sqrt,
                      bias=sm(eps_slot), scale=1.0, inc="SE")
                p.wait("vector", "SE")
                p.ins("vector", "reciprocal", LNT[1], LNT[2], inc="VE")
                p.wait("vector", "VE")
                p.ins("vector", "tensor_mul", LNT[2], LNT[0], LNT[1], inc="VE")
                for dc in range(NCH):
                    sq = SQ0 if dc % 2 == 0 else SQ1
                    p.ins("vector", "tensor_mul", sq, XBT(dc), LNT[1])
                    p.ins("vector", "tensor_sub", xdst(dc, t0), sq, LNT[2],
                          inc="VE")
                if post is not None:
                    post(t0)

        def load_x(dc, t0, dst):
            if dc == 0:
                dma_tiles("gpsimd", bv(64, 8192),
                          xT[:, t0:t0 + 512].rearrange("(n p) w -> n p w",
                                                       p=128), 16, 512, "LW")

        def l0_post(t0):
            ti = t0 // 512
            p.wait("tensor", "VE")
            sto_marks = []
            for dc in range(NCH):
                sq = SQ0 if dc % 2 == 0 else SQ1
                p.ins("vector", "tensor_mul", sq,
                      XN(dc)[:, t0:t0 + 512], XN(dc)[:, t0:t0 + 512],
                      inc="VE")
                p.wait("tensor", "VE")
                mm(ONESB, sq, bank(2), dc == 0, dc == NCH - 1, inc="TE")
            p.wait("vector", "TE")
            p.ins("vector", "tensor_scalar", LNT[0], bank(2), 1.0 / D, 0.0,
                  ALU.mult, ALU.add, inc="VE")
            vgate()
            p.wait("scalar", "VE")
            p.ins("scalar", "activation", LNT[1], LNT[0], AF.Sqrt,
                  bias=sm(11), scale=1.0, inc="SE")
            p.wait("vector", "SE")
            p.ins("vector", "reciprocal", LNT[0], LNT[1], inc="VE")
            p.wait("vector", "VE")
            for dc in range(NCH):
                i = dc % 2
                if dc >= 2:
                    p.wait("vector", "STO", sto_marks[dc - 2])
                p.ins("vector", "scalar_tensor_tensor", LNT[1],
                      XN(dc)[:, t0:t0 + 512],
                      BIAS[:, B_G0 + dc:B_G0 + dc + 1], LNT[0],
                      ALU.mult, ALU.mult, inc="VE")
                p.wait("vector", "VE")
                p.ins("vector", "tensor_copy", STG(i), LNT[1])
                p.ins("vector", "reduce_max", SM[:, 64 + 16 * ti + dc:
                                                 65 + 16 * ti + dc],
                      LNT[1], axis=AX.X, apply_absolute_value=True,
                      inc="VE")
                p.wait("sync", "VE")
                dma("sync", xpre_d[128 * dc:128 * dc + 128, t0:t0 + 512],
                    STG(i), "STO")
                sto_marks.append(p.v("STO"))

        layernorm(load_x, lambda dc, t0: XN(dc)[:, t0:t0 + 512], 10,
                  post=l0_post)
        tap_chunks("xn", XN)

        p.ins("vector", "reduce_max", sm(28), SM[:, 64:96], axis=AX.X,
              inc="VE")
        p.ins("vector", "reduce_max", sm(29), SM[:, 96:128], axis=AX.X,
              inc="VE")
        vgate()
        p.ins("vector", "tensor_scalar", sm(30), sm(28), mlo, 0.0, ALU.mult,
              ALU.add)
        p.ins("vector", "tensor_scalar", sm(31), sm(29), mhi, 0.0, ALU.mult,
              ALU.add, inc="VE")
        vgate()
        p.ins("vector", "tensor_max", sm(28), sm(30), sm(31), inc="VE")
        vgate()
        p.ins("vector", "tensor_copy", TRS[:, 0:1], sm(28), inc="VE")
        p.wait("tensor", "VE")
        p.ins("tensor", "transpose", bank(4, 128), TRS[:, 0:128], IDN,
              inc="TE")
        p.wait("vector", "TE")
        p.ins("vector", "reduce_max", TRS[0:1, 130:131], bank(4, 128)[0:1, :],
              axis=AX.X, inc="VE")
        vgate()
        for j in range(16):
            p.ins("vector", "tensor_copy", TRS[0:1, 131 + j:132 + j],
                  TRS[0:1, 130:131])
        p.ins("vector", "nop", inc="VE")
        p.wait("sync", "VE")
        dma("sync", gmax_i[:, :], TRS[0:1, 131:147], "STO")
        if "lmax" in dbg:
            p.wait("gpsimd", "VE")
            dma("gpsimd", dbg["lmax"][:, 0:4], SM[:, 24:28], "STO")
            dma("gpsimd", dbg["lmax"][:, 4:5], sm(28), "STO",
                allow_slow_non_contiguous=True)
        if "gmi" in dbg:
            p.wait("gpsimd", "STO")
            dma("gpsimd", dbg["gmi"][:, :], gmax_i[:, :], "STO")
        p.wait("gpsimd", "STO")
        p.ins("gpsimd", "collective_compute", "AllGather", ALU.bypass,
              replica_groups=[list(range(8))],
              ins=[gmax_i.opt()], outs=[gmax_o.opt()], inc="CC")

        dma_tiles("gpsimd", bv(96, 16384), wv0[:, :, :], 16, 1024, "LW")
        lw_v = p.v("LW")
        bcast8(BIAS[:, B_QKV + 16:B_QKV + 24], VBB)
        p.wait("tensor", "LW", lw_v)
        for tc in range(NCH):
            for half in range(2):
                bk = bank(half)
                for dc in range(NCH):
                    mm(XN(dc)[:, 128 * tc:128 * tc + 128],
                       VSL(dc)[:, 512 * half:512 * half + 512],
                       bk, dc == 0, dc == NCH - 1,
                       inc="TE" if dc == NCH - 1 else None)
            p.wait("vector", "TE")
            p.ins("vector", "tensor_add", VT(tc)[:, 0:512], bank(0),
                  VBB[:, 0:512])
            p.ins("vector", "tensor_add", VT(tc)[:, 512:1024], bank(1),
                  VBB[:, 512:1024], inc="VE")
            p.wait("tensor", "VE")
        if "vt" in dbg:
            p.wait("gpsimd", "VE")
            for tc in range(NCH):
                dma("gpsimd", dbg["vt"][:, 1024 * tc:1024 * tc + 1024],
                    VT(tc), "STO")

        te_head_done = {}

        def qk_head(h, pr, wsrc, bcol_q, bcol_k, l1=False):
            if h >= 2:
                p.wait("gpsimd", "TE", te_head_done[h - 2])
            else:
                p.wait("gpsimd", "VE")
                p.wait("gpsimd", "TE")
            dma_tiles("gpsimd", QKS(pr), wsrc[h, :, :, :], 16, 256, "LW")
            lw0 = p.v("LW")
            p.wait("tensor", "LW", lw0)
            for tt in range(4):
                bk = bank(4 + (tt % 2))
                for dc in range(NCH):
                    mm(QKS(pr)[:, 256 * dc:256 * dc + 128],
                       XN(dc)[:, 512 * tt:512 * tt + 512],
                       bk, dc == 0, dc == NCH - 1,
                       inc="TE" if dc == NCH - 1 else None)
                p.wait("scalar", "TE")
                p.ins("scalar", "activation", QH(pr)[:, 512 * tt:512 * tt + 512],
                      bk, AF.Identity, bias=BIAS[:, bcol_q:bcol_q + 1],
                      scale=1.0, inc="SE")
                p.wait("tensor", "SE")
            kranges = (((0, 512), 0), ((512, 512), 512), ((1024, 512), 1024),
                       ((1536, 512), 1536)) if not l1 else \
                      (((0, 256), 0), ((1536, 512), 256))
            for (src0, w), dcol in kranges:
                bk2 = bank(6, w)
                for dc in range(NCH):
                    mm(QKS(pr)[:, 256 * dc + 128:256 * dc + 256],
                       XN(dc)[:, src0:src0 + w],
                       bk2, dc == 0, dc == NCH - 1,
                       inc="TE" if dc == NCH - 1 else None)
                p.wait("scalar", "TE")
                p.ins("scalar", "activation", KH(pr)[:, dcol:dcol + w],
                      bk2, AF.Identity, bias=BIAS[:, bcol_k:bcol_k + 1],
                      scale=1.0, inc="SE")
                p.wait("tensor", "SE")

        def mhat(pr, ktiles):
            for which, tiles in ((0, ((0, 512), (512, 512), (1024, 512),
                                      (1536, 512))), (1, ktiles)):
                src = QH(pr) if which == 0 else KH(pr)
                for i, (c0, w) in enumerate(tiles):
                    p.wait("vector", "SE")
                    p.wait("vector", "TE")
                    p.ins("vector", "tensor_mul", STG(0)[:, 0:w],
                          src[:, c0:c0 + w], src[:, c0:c0 + w], inc="VE")
                    p.wait("tensor", "VE")
                    mm(ONESB, STG(0)[:, 0:w], bank(6, w), True, True,
                       inc="TE")
                    p.wait("vector", "TE")
                    p.ins("vector", "reduce_max",
                          SM[:, 32 + 4 * which + i:33 + 4 * which + i],
                          bank(6, w), axis=AX.X, inc="VE")
                    vgate()
            nk = len(ktiles)
            p.ins("vector", "reduce_max", sm(0), SM[:, 32:36], axis=AX.X)
            p.ins("vector", "reduce_max", sm(1), SM[:, 36:36 + nk], axis=AX.X,
                  inc="VE")
            p.wait("scalar", "VE")
            p.ins("scalar", "activation", sm(0), sm(0), AF.Sqrt, bias=sm(13),
                  scale=1.0)
            p.ins("scalar", "activation", sm(1), sm(1), AF.Sqrt, bias=sm(13),
                  scale=1.0, inc="SE")
            p.wait("vector", "SE")
            p.ins("vector", "scalar_tensor_tensor", sm(2), sm(0), -ISQ,
                  sm(1), ALU.mult, ALU.mult, inc="VE")
            vgate()
            p.wait("scalar", "VE")

        def l0_attn(h, pr):
            for qt in range(4):
                depth = 4 * (qt + 1)
                for g in range((depth + 3) // 4):
                    c0, c1 = 4 * g, min(4 * g + 4, depth)
                    for c in range(c0, c1):
                        mm(KH(pr)[:, 128 * c:128 * c + 128],
                           QH(pr)[:, 512 * qt:512 * qt + 512],
                           bank(c - c0), True, True,
                           inc="TE" if c == c1 - 1 else None)
                    p.wait("scalar", "TE")
                    p.wait("scalar", "VE")
                    w = 512 * (c1 - c0)
                    p.ins("scalar", "activation", AB[:, 0:w], PS[0:128, 0:w],
                          AF.Exp, bias=sm(2), scale=ISQ, inc="SE")
                    p.wait("vector", "SE")
                    for c in range(c0, c1):
                        dd = c - 4 * qt
                        if 0 <= dd <= 3:
                            off = 512 * (c - c0)
                            p.ins("vector", "tensor_mul", AB[:, off:off + 512],
                                  AB[:, off:off + 512], MKD(dd))
                    p.ins("vector", "nop", inc="VE")
                    p.wait("tensor", "VE")
                    for c in range(c0, c1):
                        off = 512 * (c - c0)
                        first = (g == 0 and c == c0)
                        last = (c == depth - 1)
                        mm(ONESB, AB[:, off:off + 512], bank(6), first, last)
                        mm(VT(c)[:, 128 * h:128 * h + 128],
                           AB[:, off:off + 512], bank(7), first, last,
                           inc="TE" if c == c1 - 1 else None)
                p.wait("vector", "TE")
                p.ins("vector", "reciprocal", LNT[0], bank(6), inc="VE")
                vgate()
                p.ins("vector", "tensor_mul",
                      OH(h)[:, 512 * qt:512 * qt + 512], bank(7), LNT[0],
                      inc="VE")
                p.wait("tensor", "VE")

        for h in range(HPC):
            pr = h % 2
            qk_head(h, pr, wqkv, B_QKV + h, B_QKV + 8 + h, l1=False)
            mhat(pr, ((0, 512), (512, 512), (1024, 512), (1536, 512)))
            l0_attn(h, pr)
            te_head_done[h] = p.v("TE")
        if "q0" in dbg:
            tap_one("q0", QH(1))
        if "k0" in dbg:
            tap_one("k0", KH(1))
        if "oh" in dbg:
            p.wait("gpsimd", "VE")
            for h in range(HPC):
                dma("gpsimd", dbg["oh"][:, 2048 * h:2048 * h + 2048], OH(h),
                    "STO")

        def wo_pass(wsrc):
            p.wait("gpsimd", "TE")
            p.wait("gpsimd", "SE")
            marks = {}
            for oc in range(NCH):
                pr = oc % 2
                if oc >= 2:
                    p.wait("gpsimd", "TE", marks[oc - 2])
                dma_tiles("gpsimd", WOS(pr), wsrc[oc, :, :, :], 8, 128,
                          "LW")
                lw0 = p.v("LW")
                p.wait("tensor", "LW", lw0)
                for tt in range(4):
                    bk = bank(tt % 2)
                    for hc in range(HPC):
                        mm(WOS(pr)[:, 128 * hc:128 * hc + 128],
                           OH(hc)[:, 512 * tt:512 * tt + 512],
                           bk, hc == 0, hc == HPC - 1,
                           inc="TE" if hc == HPC - 1 else None)
                    p.wait("vector", "TE")
                    p.ins("vector", "tensor_copy",
                          CC0(oc)[:, 512 * tt:512 * tt + 512], bk, inc="VE")
                    p.wait("tensor", "VE")
                marks[oc] = p.v("TE")

        wo_pass(wo0)

        p.ins("vector", "memset", sm(3), 0.0, inc="VE")
        for oc in range(NCH):
            pr = oc % 2
            dma_tiles("sync", WFA(pr),
                      w0t[oc, :, :, :], 16, 128, "LX")
            lx0 = p.v("LX")
            p.wait("vector", "LX", lx0)
            p.ins("vector", "reduce_sum", SM[:, 48 + oc:49 + oc], WFA(pr),
                  axis=AX.X, apply_absolute_value=True, inc="VE")
            p.wait("sync", "VE")
        p.wait("vector", "VE")
        p.ins("vector", "reduce_sum", sm(3), SM[:, 48:64], axis=AX.X,
              inc="VE")
        vgate()
        p.ins("vector", "tensor_copy", TRS[:, 0:1], sm(3), inc="VE")
        p.wait("tensor", "VE")
        p.ins("tensor", "transpose", bank(4, 128), TRS[:, 0:128], IDN,
              inc="TE")
        p.wait("vector", "TE")
        p.ins("vector", "reduce_sum", TRS[0:1, 130:131], bank(4, 128)[0:1, :],
              axis=AX.X, inc="VE")
        vgate()
        p.ins("vector", "tensor_scalar", TRS[0:1, 130:131], TRS[0:1, 130:131],
              1.0 / (D * D), 1e-5, ALU.mult, ALU.max, inc="VE")
        pbcast(130, 5)
        p.ins("vector", "tensor_scalar", sm(6), sm(5), 0.5, 0.0, ALU.mult,
              ALU.add)
        p.ins("vector", "tensor_scalar", sm(7), sm(5), -0.5, 0.0, ALU.mult,
              ALU.add, inc="VE")
        vgate()
        p.wait("sync", "CC")
        if "gmo" in dbg:
            p.wait("gpsimd", "CC")
            dma("gpsimd", dbg["gmo"][:, :], gmax_o[:, :], "STO")
        dma("sync", TRS[0:1, 0:128], gmax_o[:, :], "LX")
        p.wait("vector", "LX")
        p.ins("vector", "reduce_max", TRS[0:1, 130:131], TRS[0:1, 0:128],
              axis=AX.X, inc="VE")
        vgate()
        p.ins("vector", "tensor_scalar", TRS[0:1, 130:131], TRS[0:1, 130:131],
              1e-5, 0.0, ALU.max, ALU.add, inc="VE")
        pbcast(130, 8)
        if "gmax" in dbg:
            p.wait("gpsimd", "VE")
            dma("gpsimd", dbg["gmax"][:, 0:1], sm(8), "STO",
                allow_slow_non_contiguous=True)
        if "wsc" in dbg:
            p.wait("gpsimd", "VE")
            dma("gpsimd", dbg["wsc"][:, 0:1], sm(5), "STO",
                allow_slow_non_contiguous=True)
        p.ins("vector", "reciprocal", sm(9), sm(8), inc="VE")
        p.wait("vector", "VE")
        p.ins("vector", "tensor_scalar", sm(9), sm(9), 127.0, 0.0, ALU.mult,
              ALU.add, inc="VE")
        vgate()
        p.ins("vector", "scalar_tensor_tensor", sm(14), sm(8), 1.0 / 127.0,
              sm(5), ALU.mult, ALU.mult, inc="VE")
        vgate()

        ve_mk0 = {}
        for ic in range(NCH):
            par = ic % 2
            if ic >= 2:
                p.wait("sync", "VE", ve_mk0[ic - 2])
            r0, r1 = RSC(2 * par), RSC(2 * par + 1)
            dma("sync", r0, xpre_d[128 * ic:128 * ic + 128, 0:1024], "LX")
            dma("sync", r1, xpre_d[128 * ic:128 * ic + 128, 1024:2048], "LX")
            lx0 = p.v("LX")
            p.wait("vector", "LX", lx0)
            for tt in range(2):
                lt = LNT[par] if tt == 0 else LNT[2]
                p.ins("vector", "tensor_scalar", lt,
                      r0[:, 512 * tt:512 * tt + 512], mlo, 0.0,
                      ALU.mult, ALU.add)
                p.ins("vector", "scalar_tensor_tensor", lt,
                      r1[:, 512 * tt:512 * tt + 512], mhi, lt,
                      ALU.mult, ALU.add)
                p.ins("vector", "tensor_scalar", lt, lt, sm(9), MAGIC,
                      ALU.mult, ALU.add)
                p.ins("vector", "tensor_scalar",
                      W0IQ(ic)[:, 512 * tt:512 * tt + 512], lt, MAGIC,
                      0.0, ALU.subtract, ALU.add, inc="VE")
            ve_mk0[ic] = p.v("VE")
        if "iq" in dbg:
            p.wait("gpsimd", "VE")
            for ic in range(NCH):
                dma("gpsimd", dbg["iq"][128 * ic:128 * ic + 128, :],
                    W0IQ(ic), "STO")

        ve_marks = {}
        te_marks_f = {}
        for oc in range(NCH):
            pr = oc % 2
            if oc >= 2:
                p.wait("sync", "VE", ve_marks[oc - 2])
            dma_tiles("sync", WFA(pr),
                      w0t[oc, :, :, :], 16, 128, "LX")
            lx0 = p.v("LX")
            p.wait("vector", "LX", lx0)
            if oc >= 1:
                p.wait("vector", "TE", te_marks_f[oc - 1])
            p.ins("vector", "tensor_scalar", RSC(0), WFA(pr)[:, 0:1024],
                  sm(6), 0.0, ALU.is_gt, ALU.add)
            p.ins("vector", "tensor_scalar", RSC(1), WFA(pr)[:, 0:1024],
                  sm(7), 0.0, ALU.is_lt, ALU.add)
            p.ins("vector", "tensor_sub", RSC(2), RSC(0), RSC(1))
            p.ins("vector", "tensor_scalar", RSC(0), WFA(pr)[:, 1024:2048],
                  sm(6), 0.0, ALU.is_gt, ALU.add)
            p.ins("vector", "tensor_scalar", RSC(1), WFA(pr)[:, 1024:2048],
                  sm(7), 0.0, ALU.is_lt, ALU.add)
            p.ins("vector", "tensor_sub", RSC(3), RSC(0), RSC(1), inc="VE")
            ve_marks[oc] = p.v("VE")
            p.wait("tensor", "VE")
            for tt in range(2):
                bk = bank(4 + tt)
                for ic in range(NCH):
                    wqt = RSC(2) if ic < 8 else RSC(3)
                    mm(wqt[:, 128 * (ic % 8):128 * (ic % 8) + 128],
                       W0IQ(ic)[:, 512 * tt:512 * tt + 512],
                       bk, ic == 0, ic == NCH - 1,
                       inc="TE" if ic == NCH - 1 else None)
            te_marks_f[oc] = p.v("TE")
            if oc == 0 and "wq0" in dbg:
                p.wait("gpsimd", "TE")
                dma("gpsimd", dbg["wq0"][:, :], RSC(2), "STO")
            p.wait("vector", "TE")
            for tt in range(2):
                p.ins("vector", "tensor_scalar", LNT[tt], bank(4 + tt), sm(14),
                      0.0, ALU.mult, ALU.add, inc="VE")
            if oc == 0 and "fs" in dbg:
                p.wait("gpsimd", "VE")
                dma("gpsimd", dbg["fs"][:, :], LNT[0], "STO")
            if oc == 0 and "osc" in dbg:
                p.wait("gpsimd", "VE")
                dma("gpsimd", dbg["osc"][:, 0:1], sm(14), "STO",
                    allow_slow_non_contiguous=True)
            vgate()
            for tt in range(2):
                lo = CC0(oc)[:, 512 * tt:512 * tt + 512]
                hi = CC0(oc)[:, 1024 + 512 * tt:1024 + 512 * tt + 512]
                p.ins("vector", "scalar_tensor_tensor", lo, LNT[tt], mlo, lo,
                      ALU.mult, ALU.add)
                p.ins("vector", "scalar_tensor_tensor", hi, LNT[tt], mhi, hi,
                      ALU.mult, ALU.add)
            p.ins("vector", "nop", inc="VE")
            p.wait("tensor", "VE")
        tap_chunks("cb", CC0)

        p.wait("sync", "VE")
        for oc in range(NCH):
            dma("sync", cb_d[128 * oc:128 * oc + 128, :], CC0(oc), "STO")
        p.wait("gpsimd", "STO")
        p.ins("gpsimd", "collective_compute", "AllReduce", ALU.add,
              replica_groups=[[0, 1], [2, 3], [4, 5], [6, 7]],
              ins=[cb_d.opt()], outs=[ar_d.opt()], inc="CC")

        p.wait("sync", "CC")
        ASF = lambda par, i: fv(8 + 6 * par + 2 * i, 512)
        AXO = lambda par, i: fv(27 + 4 * par + 2 * i, 512)
        ve_mk, sto_mk = {}, {}
        it = 0
        for oc in range(NCH):
            for ht in range(2):
                par = it % 2
                if it >= 2:
                    p.wait("sync", "VE", ve_mk[it - 2])
                r0, r1 = RSC(2 * par), RSC(2 * par + 1)
                dma("sync", r0[:, 0:512], ar_d[128 * oc:128 * oc + 128,
                                               512 * ht:512 * ht + 512], "LX")
                dma("sync", r1[:, 0:512], ar_d[128 * oc:128 * oc + 128,
                                               1024 + 512 * ht:1536 + 512 * ht],
                    "LX")
                dma("sync", AXO(par, 0), xT[128 * oc:128 * oc + 128,
                                            512 * ht:512 * ht + 512], "LX")
                dma("sync", AXO(par, 1), xT[128 * oc:128 * oc + 128,
                                            1024 + 512 * ht:1536 + 512 * ht],
                    "LX")
                lx0 = p.v("LX")
                p.wait("vector", "LX", lx0)
                if it >= 2:
                    p.wait("vector", "STO", sto_mk[it - 2])
                p.ins("vector", "scalar_tensor_tensor", ASF(par, 0),
                      r0[:, 0:512], BIAS[:, B_BO0 + oc:B_BO0 + oc + 1],
                      AXO(par, 0), ALU.add, ALU.add)
                p.ins("vector", "scalar_tensor_tensor", ASF(par, 1),
                      r1[:, 0:512], BIAS[:, B_BO0 + oc:B_BO0 + oc + 1],
                      AXO(par, 1), ALU.add, ALU.add, inc="VE")
                p.ins("vector", "tensor_copy", STG(2 * par), ASF(par, 0))
                p.ins("vector", "tensor_copy", STG(2 * par + 1), ASF(par, 1),
                      inc="VE")
                p.ins("vector", "tensor_scalar", ASF(par, 2), ASF(par, 0),
                      mlo, 0.0, ALU.mult, ALU.add)
                p.ins("vector", "scalar_tensor_tensor", ASF(par, 2),
                      ASF(par, 1), mhi, ASF(par, 2), ALU.mult, ALU.add,
                      inc="VE")
                ve_mk[it] = p.v("VE")
                p.wait("sync", "VE", ve_mk[it])
                dma("sync", x1_d[128 * oc:128 * oc + 128,
                                 512 * ht:512 * ht + 512], STG(2 * par),
                    "STO")
                dma("sync", x1_d[128 * oc:128 * oc + 128,
                                 1024 + 512 * ht:1536 + 512 * ht],
                    STG(2 * par + 1), "STO")
                dma("sync", x1o_d[128 * oc:128 * oc + 128,
                                  512 * ht:512 * ht + 512], ASF(par, 2),
                    "STO")
                sto_mk[it] = p.v("STO")
                it += 1
        if "x1" in dbg:
            p.wait("gpsimd", "STO")
            for oc in range(NCH):
                dma("gpsimd", dbg["x1"][128 * oc:128 * oc + 128, :],
                    x1_d[128 * oc:128 * oc + 128, :], "STO")
        if "arx" in dbg:
            p.wait("gpsimd", "STO")
            for oc in range(NCH):
                dma("gpsimd", dbg["arx"][128 * oc:128 * oc + 128, :],
                    ar_d[128 * oc:128 * oc + 128, :], "STO")

        def load_x1(dc, t0, dst):
            if dc == 0:
                dma_tiles("sync", bv(64, 8192),
                          x1_d[:, t0:t0 + 512].rearrange("(n p) w -> n p w",
                                                         p=128), 16, 512, "LX")

        def l1_post(t0):
            p.wait("sync", "VE")
            for dc in range(NCH):
                dma("sync", xn2_d[128 * dc:128 * dc + 128, t0:t0 + 512],
                    XN(dc)[:, t0:t0 + 512], "STO")

        layernorm(load_x1, lambda dc, t0: XN(dc)[:, t0:t0 + 512], 10,
                  post=l1_post)
        tap_chunks("xn2", XN)

        p.wait("gpsimd", "VE")
        p.ins("gpsimd", "iota", TRS[0:64, 0:1], pattern=[[0, 1]], base=0,
              channel_multiplier=1, allow_small_or_imprecise_dtypes=True,
              inc="GE")
        p.wait("scalar", "GE")
        p.ins("scalar", "activation", sm(21)[0:64, :], TRS[0:64, 0:1], AF.Exp,
              bias=sm(15)[0:64, :], scale=-math.log(10000.0) / 64.0,
              inc="SE")
        p.wait("gpsimd", "SE")
        p.ins("gpsimd", "iota", COSF[0:64, :], pattern=[[1, 2048]], base=0,
              channel_multiplier=0, allow_small_or_imprecise_dtypes=True,
              inc="GE")
        p.wait("vector", "GE")
        p.wait("vector", "SE")
        p.ins("vector", "tensor_scalar", COSF[0:64, :], COSF[0:64, :],
              sm(21)[0:64, :], 0.0, ALU.mult, ALU.add, inc="VE")
        p.wait("vector", "VE")
        SCBOTH = fv(8, 4096)
        TWO_PI = 2 * math.pi
        for seg in range(8):
            s0 = 512 * (seg % 4)
            half_turn = 0.25 if seg >= 4 else 0.0
            dst = (SINF if seg < 4 else COSF)[0:64, s0:s0 + 512]
            xs = COSF[0:64, s0:s0 + 512] if seg < 4 else dst
            if seg >= 4:
                pass
            p.ins("vector", "tensor_scalar", TRS[0:64, 0:512], xs,
                  1.0 / TWO_PI, half_turn, ALU.mult, ALU.add, inc="VE")
            p.wait("vector", "VE")
            p.ins("vector", "tensor_scalar", TRS[0:64, 0:512],
                  TRS[0:64, 0:512], MAGIC, MAGIC, ALU.add, ALU.subtract,
                  inc="VE")
            p.wait("vector", "VE")
            p.ins("vector", "scalar_tensor_tensor", dst,
                  TRS[0:64, 0:512], -TWO_PI, xs, ALU.mult, ALU.add,
                  inc="VE")
            if seg >= 4:
                p.wait("vector", "VE")
                p.ins("vector", "tensor_scalar", dst, dst,
                      half_turn * TWO_PI, 0.0, ALU.add, ALU.add, inc="VE")
        p.wait("scalar", "VE")
        p.ins("scalar", "activation", SCBOTH[0:64, :], SCBOTH[0:64, :],
              AF.Sin, bias=sm(15)[0:64, :], scale=1.0, inc="SE")
        if "sin" in dbg:
            p.wait("gpsimd", "SE")
            dma("gpsimd", dbg["sin"][:, :], SINF[0:64, :], "STO")
        if "cos" in dbg:
            p.wait("gpsimd", "SE")
            dma("gpsimd", dbg["cos"][:, :], COSF[0:64, :], "STO")
        p.wait("sync", "SE")
        dma("sync", SINF[64:128, :], SINF[0:64, :], "LX")
        dma("sync", COSF[64:128, :], COSF[0:64, :], "LX")
        lx_rt = p.v("LX")
        p.wait("vector", "LX", lx_rt)
        p.ins("vector", "tensor_scalar", SINF[0:64, :], SINF[0:64, :], -1.0,
              0.0, ALU.mult, ALU.add, inc="VE")
        p.wait("tensor", "VE")

        dma_tiles("gpsimd", bv(96, 16384), wv1[:, :, :], 16, 1024, "LW")
        lw_v1 = p.v("LW")
        bcast8(BIAS[:, B_BV1:B_BV1 + 8], VBB)
        p.wait("tensor", "LW", lw_v1)
        for kc in range(KCH):
            t0 = KTOK[kc]
            for half in range(2):
                bk = bank(half)
                for dc in range(NCH):
                    mm(XN(dc)[:, t0:t0 + 128],
                       VSL(dc)[:, 512 * half:512 * half + 512],
                       bk, dc == 0, dc == NCH - 1,
                       inc="TE" if dc == NCH - 1 else None)
            p.wait("vector", "TE")
            p.ins("vector", "tensor_add", VT(kc)[:, 0:512], bank(0),
                  VBB[:, 0:512])
            p.ins("vector", "tensor_add", VT(kc)[:, 512:1024], bank(1),
                  VBB[:, 512:1024], inc="VE")
            p.wait("tensor", "VE")

        def rope(dst, ncol, pos0):
            p.wait("tensor", "VE")
            p.wait("tensor", "SE")
            nmm = (ncol + 511) // 512
            for s in range(nmm):
                c0 = s * 512
                w = min(512, ncol - c0)
                mm(P64, dst[:, c0:c0 + w], bank(s, w), True, True,
                   inc="TE" if s == nmm - 1 else None)
            p.wait("vector", "TE")
            ca = COSF[:, pos0:pos0 + ncol]
            sa = SINF[:, pos0:pos0 + ncol]
            p.ins("vector", "tensor_mul", AB[:, 0:ncol], dst[:, 0:ncol], ca)
            p.ins("vector", "tensor_mul", dst[:, 0:ncol],
                  PS[0:128, 0:ncol], sa)
            p.ins("vector", "tensor_add", dst[:, 0:ncol], dst[:, 0:ncol],
                  AB[:, 0:ncol], inc="VE")
            p.wait("tensor", "VE")

        def l1_attn(h, pr):
            for qt in range(4):
                for grp in range(2):
                    cs = range(3 * grp, 3 * grp + 3)
                    for i, c in enumerate(cs):
                        mm(KH(pr)[:, 128 * c:128 * c + 128],
                           QH(pr)[:, 512 * qt:512 * qt + 512],
                           bank(i), True, True,
                           inc="TE" if i == 2 else None)
                    p.wait("scalar", "TE")
                    p.wait("scalar", "VE")
                    p.ins("scalar", "activation", ABW, PS[0:128, 0:1536],
                          AF.Exp, bias=sm(2), scale=ISQ, inc="SE")
                    p.wait("vector", "SE")
                    if grp == 0:
                        p.ins("vector", "tensor_scalar", ABW[:, 512:1024],
                              ABW[:, 512:1024], sm(20), 0.0, ALU.mult,
                              ALU.add)
                    p.ins("vector", "nop", inc="VE")
                    p.wait("tensor", "VE")
                    for i, c in enumerate(cs):
                        off = 512 * i
                        first = (grp == 0 and i == 0)
                        last = (grp == 1 and i == 2)
                        mm(ONESB, ABW[:, off:off + 512], bank(6), first,
                           last)
                        mm(VT(c)[:, 128 * h:128 * h + 128],
                           ABW[:, off:off + 512], bank(7), first, last,
                           inc="TE" if i == 2 else None)
                p.wait("vector", "TE")
                p.ins("vector", "reciprocal", LNT[0], bank(6), inc="VE")
                vgate()
                p.ins("vector", "tensor_mul",
                      OH(h)[:, 512 * qt:512 * qt + 512], bank(7), LNT[0],
                      inc="VE")
                p.wait("tensor", "VE")

        te_head_done = {}
        for h in range(HPC):
            pr = h % 2
            qk_head(h, pr, wqk1, B_QK1 + h, B_QK1 + 8 + h, l1=True)
            p.wait("vector", "SE")
            rope(QH(pr), 2048, 0)
            rope(KH(pr)[:, 0:256], 256, 0)
            rope(KH(pr)[:, 256:768], 512, 1536)
            mhat(pr, ((0, 512), (256, 512)))
            l1_attn(h, pr)
            te_head_done[h] = p.v("TE")
        if "q1" in dbg:
            tap_one("q1", QH(1))
        if "k1" in dbg:
            tap_one("k1", KH(1)[:, 0:1024])
        if "oh1" in dbg:
            p.wait("gpsimd", "VE")
            for h in range(HPC):
                dma("gpsimd", dbg["oh1"][:, 2048 * h:2048 * h + 2048], OH(h),
                    "STO")

        for t0 in (0, 512):
            for dc in range(NCH):
                p.wait("sync", "VE")
                dma("sync", RSC(0)[:, 0:512], xn2_d[128 * dc:128 * dc + 128,
                                          t0:t0 + 512], "LX")
                dma("sync", RSC(1)[:, 0:512], xn2_d[128 * dc:128 * dc + 128,
                                          1024 + t0:1024 + t0 + 512], "LX")
                lx0 = p.v("LX")
                p.wait("vector", "LX", lx0)
                p.ins("vector", "tensor_scalar", STG(0), RSC(0)[:, 0:512],
                      mlo, 0.0, ALU.mult, ALU.add)
                p.ins("vector", "scalar_tensor_tensor",
                      XR(dc)[:, t0:t0 + 512], RSC(1)[:, 0:512], mhi, STG(0),
                      ALU.mult, ALU.add, inc="VE")
            p.wait("tensor", "VE")
            for dc in range(NCH):
                mm(ONESB, XR(dc)[:, t0:t0 + 512], bank(0), dc == 0,
                   dc == NCH - 1, inc="TE" if dc == NCH - 1 else None)
            te_marks = []
            for dc in range(NCH):
                if dc >= 2:
                    p.wait("vector", "TE", te_marks[dc - 2])
                sq = STG(0) if dc % 2 == 0 else STG(1)
                p.ins("vector", "tensor_mul", sq, XR(dc)[:, t0:t0 + 512],
                      XR(dc)[:, t0:t0 + 512], inc="VE")
                p.wait("tensor", "VE")
                mm(ONESB, sq, bank(1), dc == 0, dc == NCH - 1, inc="TE")
                te_marks.append(p.v("TE"))
            p.wait("vector", "TE")
            p.ins("vector", "tensor_scalar", LNT[0], bank(0), 1.0 / D, 0.0,
                  ALU.mult, ALU.add)
            p.ins("vector", "tensor_scalar", LNT[1], bank(1), 1.0 / D, 0.0,
                  ALU.mult, ALU.add, inc="VE")
            vgate()
            p.ins("vector", "tensor_mul", LNT[2], LNT[0], LNT[0])
            p.ins("vector", "tensor_sub", LNT[1], LNT[1], LNT[2], inc="VE")
            p.wait("scalar", "VE")
            p.ins("scalar", "activation", LNT[2], LNT[1], AF.Sqrt,
                  bias=sm(12), scale=1.0, inc="SE")
            p.wait("vector", "SE")
            p.ins("vector", "reciprocal", LNT[1], LNT[2], inc="VE")
            p.wait("vector", "VE")
            p.ins("vector", "tensor_mul", LNT[2], LNT[0], LNT[1], inc="VE")
            for dc in range(NCH):
                sq = STG(0) if dc % 2 == 0 else STG(1)
                p.ins("vector", "tensor_mul", sq, XR(dc)[:, t0:t0 + 512],
                      LNT[1])
                p.ins("vector", "tensor_sub", XR(dc)[:, t0:t0 + 512], sq,
                      LNT[2], inc="VE")

        p.ins("vector", "memset", SM[:, 40:48], 0.0, inc="VE")
        for dc in range(NCH):
            for tch in range(8):
                p.wait("vector", "TE")
                p.ins("vector", "tensor_copy", TRS[:, 0:128],
                      XR(dc)[:, 128 * tch:128 * tch + 128], inc="VE")
                p.wait("tensor", "VE")
                p.ins("tensor", "transpose", bank(4, 128), TRS[:, 0:128], IDN,
                      inc="TE")
                p.wait("vector", "TE")
                p.ins("vector", "reduce_max", sm(28), bank(4, 128), axis=AX.X,
                      apply_absolute_value=True, inc="VE")
                vgate()
                p.ins("vector", "tensor_max", SM[:, 40 + tch:41 + tch],
                      SM[:, 40 + tch:41 + tch], sm(28), inc="VE")
        if "rmax" in dbg:
            p.wait("gpsimd", "VE")
            dma("gpsimd", dbg["rmax"][:, 0:8], SM[:, 40:48], "STO")
        bcast8(SM[:, 40:48], VBB)
        p.ins("vector", "tensor_copy", BCQ(0), VBB[:, 0:512])
        p.ins("vector", "tensor_copy", BCQ(1), VBB[:, 512:1024], inc="VE")
        p.wait("vector", "VE")
        p.ins("vector", "reciprocal", BCQ(2), BCQ(0))
        p.ins("vector", "reciprocal", BCQ(3), BCQ(1), inc="VE")
        p.wait("vector", "VE")
        for dc in range(NCH):
            for tt in range(2):
                p.ins("vector", "scalar_tensor_tensor", LNT[0],
                      XR(dc)[:, 512 * tt:512 * tt + 512], 127.0, BCQ(2 + tt),
                      ALU.mult, ALU.mult)
                p.ins("vector", "tensor_scalar", LNT[0], LNT[0], MAGIC, MAGIC,
                      ALU.add, ALU.subtract)
                p.ins("vector", "tensor_copy",
                      XR(dc)[:, 512 * tt:512 * tt + 512], LNT[0], inc="VE")
        p.ins("vector", "tensor_scalar", BCQ(0), BCQ(0), 2.0 / 127.0, 0.0,
              ALU.mult, ALU.add)
        p.ins("vector", "tensor_scalar", BCQ(1), BCQ(1), 2.0 / 127.0, 0.0,
              ALU.mult, ALU.add, inc="VE")
        if "i1" in dbg:
            p.wait("gpsimd", "VE")
            for dc in range(NCH):
                dma("gpsimd", dbg["i1"][128 * dc:128 * dc + 128, :], XR(dc),
                    "STO")

        wo_pass(wo1)

        te_marks_g = {}
        p.wait("gpsimd", "TE")
        ve_marks_g = {}
        for osb in range(16):
            pr = osb % 2
            if osb >= 2:
                p.wait("gpsimd", "TE", te_marks_g[osb - 2])
            dma_tiles("gpsimd", GSL(pr), wg1[osb, :, :, :], 16, 512, "LW")
            dma_tiles("gpsimd", USL(pr), wu1[osb, :, :, :], 16, 512, "LW")
            lwg = p.v("LW")
            p.wait("vector", "LW", lwg)
            if osb >= 2:
                p.wait("vector", "TE", te_marks_g[osb - 2])
            p.ins("vector", "tensor_scalar", GSL(pr), GSL(pr), 0.0, 0.5,
                  ALU.is_gt, ALU.subtract)
            p.ins("vector", "tensor_scalar", USL(pr), USL(pr), 0.0, 0.5,
                  ALU.is_gt, ALU.subtract, inc="VE")
            ve_marks_g[osb] = p.v("VE")
            p.wait("tensor", "VE", ve_marks_g[osb])
            for oc4 in range(4):
                oc = 4 * osb + oc4
                for tt in range(2):
                    bkg, bku = bank(4), bank(5)
                    for ic in range(NCH):
                        mm(GSL(pr)[:, 512 * ic + 128 * oc4:512 * ic + 128 * oc4 + 128],
                           XR(ic)[:, 512 * tt:512 * tt + 512],
                           bkg, ic == 0, ic == NCH - 1)
                    for ic in range(NCH):
                        mm(USL(pr)[:, 512 * ic + 128 * oc4:512 * ic + 128 * oc4 + 128],
                           XR(ic)[:, 512 * tt:512 * tt + 512],
                           bku, ic == 0, ic == NCH - 1,
                           inc="TE" if ic == NCH - 1 else None)
                    p.wait("vector", "TE")
                    if not (osb == 0 and oc4 == 0 and tt == 0):
                        p.wait("vector", "STO")
                    p.ins("vector", "tensor_mul", GUT, bkg,
                          BCQ(tt), inc="VE")
                    p.wait("scalar", "VE")
                    p.ins("scalar", "activation", GUT,
                          GUT, AF.Silu, bias=sm(15), scale=1.0,
                          inc="SE")
                    p.ins("vector", "tensor_mul", UUT, bku,
                          BCQ(tt), inc="VE")
                    p.wait("vector", "SE")
                    p.ins("vector", "tensor_mul", GUT,
                          GUT, UUT, inc="VE")
                    p.wait("sync", "VE")
                    dma("sync", p_d[128 * oc:128 * oc + 128,
                                    512 * tt:512 * tt + 512],
                        GUT, "STO")
                    p.wait("tensor", "VE")
            te_marks_g[osb] = p.v("TE")

        if "pd" in dbg:
            p.wait("gpsimd", "STO")
            for oc in range(8):
                dma("gpsimd", dbg["pd"][128 * oc:128 * oc + 128, :],
                    p_d[128 * oc:128 * oc + 128, :], "STO")

        mm_done = {}
        p.wait("gpsimd", "TE")
        p.wait("vector", "TE")
        for th in range(2):
            p.wait("sync", "TE")
            dma_tiles("sync", bv(64, 32768),
                      p_d[:, 512 * th:512 * th + 512]
                      .rearrange("(n p) w -> n p w", p=128), 64, 512, "LX")
            lxp = p.v("LX")
            p.wait("tensor", "LX", lxp)
            for d2c in range(NCH):
                for half in range(2):
                    pr = half
                    if d2c >= 2:
                        p.wait("gpsimd", "TE", mm_done[(th, d2c - 2)])
                    elif th == 1 and d2c == 0 and half == 0:
                        p.wait("gpsimd", "TE", mm_done[(0, NCH - 1)])
                        p.wait("gpsimd", "STO")
                    elif th == 1 and d2c == 1 and half == 0:
                        pass
                    if th == 0:
                        dma_tiles("gpsimd", WDS2(pr, d2c % 2),
                                  wd1[d2c, 32 * half:32 * half + 32, :, :],
                                  32, 128, "LW")
                        lw0 = p.v("LW")
                        p.wait("vector", "LW", lw0)
                        if d2c >= 2:
                            p.wait("vector", "TE", mm_done[(th, d2c - 2)])
                        p.ins("vector", "tensor_scalar", WDS2(pr, d2c % 2),
                              WDS2(pr, d2c % 2), 0.0, 0.5, ALU.is_gt,
                              ALU.subtract, inc="VE")
                        p.wait("sync", "VE")
                        dma("sync",
                            wdc_d[d2c, 32 * half:32 * half + 32, :, :]
                            .rearrange("n p w -> p n w"),
                            WDS2(pr, d2c % 2), "STO")
                    else:
                        dma_tiles("gpsimd", WDS2(pr, d2c % 2),
                                  wdc_d[d2c, 32 * half:32 * half + 32, :, :],
                                  32, 128, "LW")
                lw1, ve1 = p.v("LW"), p.v("VE")
                p.wait("tensor", "LW", lw1)
                p.wait("tensor", "VE", ve1)
                bk = bank(6 + (d2c % 2))
                for oc in range(64):
                    mm(WDS2(oc // 32, d2c % 2)[:, 128 * (oc % 32):
                                               128 * (oc % 32) + 128],
                       PQ(oc), bk, oc == 0, oc == 63,
                       inc="TE" if oc == 63 else None)
                mm_done[(th, d2c)] = p.v("TE")
                p.wait("vector", "TE")
                p.ins("vector", "tensor_scalar", LNT[0], bk, 2.0, 0.0,
                      ALU.mult, ALU.add, inc="VE")
                vgate()
                lo = CC0(d2c)[:, 512 * th:512 * th + 512]
                hi = CC0(d2c)[:, 1024 + 512 * th:1536 + 512 * th]
                p.ins("vector", "scalar_tensor_tensor", lo, LNT[0], mlo, lo,
                      ALU.mult, ALU.add)
                p.ins("vector", "scalar_tensor_tensor", hi, LNT[0], mhi, hi,
                      ALU.mult, ALU.add)
                p.ins("vector", "nop", inc="VE")

        tap_chunks("cb1", CC0)
        p.wait("sync", "VE")
        for oc in range(NCH):
            dma("sync", cb_d[128 * oc:128 * oc + 128, :], CC0(oc), "STO")
        p.wait("gpsimd", "STO")
        p.ins("gpsimd", "collective_compute", "AllReduce", ALU.add,
              replica_groups=[[0, 1], [2, 3], [4, 5], [6, 7]],
              ins=[cb_d.opt()], outs=[ar_d.opt()], inc="CC")

        p.wait("sync", "CC")
        if "ar1" in dbg:
            p.wait("gpsimd", "CC")
            for oc in range(NCH):
                dma("gpsimd", dbg["ar1"][128 * oc:128 * oc + 128, :],
                    ar_d[128 * oc:128 * oc + 128, :], "STO")
        FSF = lambda par, i: fv(8 + 6 * par + 2 * i, 512)
        FXO = lambda par: fv(27 + 4 * par, 512)
        ve_mk2, sto_mk2 = {}, {}
        it = 0
        for oc in range(NCH):
            for ht in range(2):
                par = it % 2
                if it >= 2:
                    p.wait("sync", "VE", ve_mk2[it - 2])
                r0, r1 = RSC(2 * par), RSC(2 * par + 1)
                dma("sync", r0[:, 0:512], ar_d[128 * oc:128 * oc + 128,
                                               512 * ht:512 * ht + 512], "LX")
                dma("sync", r1[:, 0:512], ar_d[128 * oc:128 * oc + 128,
                                               1024 + 512 * ht:1536 + 512 * ht],
                    "LX")
                dma("sync", FXO(par), x1o_d[128 * oc:128 * oc + 128,
                                            512 * ht:512 * ht + 512], "LX")
                lx0 = p.v("LX")
                p.wait("vector", "LX", lx0)
                if it >= 2:
                    p.wait("vector", "STO", sto_mk2[it - 2])
                p.ins("vector", "tensor_scalar", FSF(par, 1), r0[:, 0:512],
                      mlo, 0.0, ALU.mult, ALU.add)
                p.ins("vector", "scalar_tensor_tensor", FSF(par, 1),
                      r1[:, 0:512], mhi, FSF(par, 1), ALU.mult, ALU.add)
                p.ins("vector", "scalar_tensor_tensor", FSF(par, 2),
                      FSF(par, 1), BIAS[:, B_BO1 + oc:B_BO1 + oc + 1],
                      FXO(par), ALU.add, ALU.add, inc="VE")
                ve_mk2[it] = p.v("VE")
                p.wait("sync", "VE", ve_mk2[it])
                dma("sync", out_own[128 * oc:128 * oc + 128,
                                    512 * ht:512 * ht + 512], FSF(par, 2),
                    "STO")
                sto_mk2[it] = p.v("STO")
                it += 1

        p.wait("sync", "STO")
        p.wait("sync", "LX")
        p.wait("sync", "LW")

        block = ctx.enter_context(nc.Block())

        @block.sync
        def _(e):
            for f in p.ops["sync"]:
                f(e)

        @block.gpsimd
        def _(e):
            for f in p.ops["gpsimd"]:
                f(e)

        @block.tensor
        def _(e):
            for f in p.ops["tensor"]:
                f(e)

        @block.vector
        def _(e):
            for f in p.ops["vector"]:
                f(e)

        @block.scalar
        def _(e):
            for f in p.ops["scalar"]:
                f(e)

    return nc


def _rope_perm():
    pm = np.zeros(128, np.int64)
    pm[0:64] = np.arange(0, 128, 2)
    pm[64:128] = np.arange(1, 128, 2)
    return pm


def _shard(inputs):
    f32 = np.float32
    x = np.asarray(inputs["L0_attn_x"], f32)
    Wqkv = np.asarray(inputs["L0_attn_Wqkv"], f32)
    bqkv = np.asarray(inputs["L0_attn_bqkv"], f32)
    Wo0 = np.asarray(inputs["L0_attn_Wo"], f32)
    bo0 = np.asarray(inputs["L0_attn_bo"], f32)
    W0 = np.asarray(inputs["L0_ffn_weight"], f32)
    g0 = np.asarray(inputs["L0_ffn_norm_weight"], f32)
    Wq1 = np.asarray(inputs["L1_attn_Wq"], f32)
    bq1 = np.asarray(inputs["L1_attn_bq"], f32)
    Wk1 = np.asarray(inputs["L1_attn_Wk"], f32)
    bk1 = np.asarray(inputs["L1_attn_bk"], f32)
    Wv1 = np.asarray(inputs["L1_attn_Wv"], f32)
    bv1 = np.asarray(inputs["L1_attn_bv"], f32)
    Wo1 = np.asarray(inputs["L1_attn_Wo"], f32)
    bo1 = np.asarray(inputs["L1_attn_bo"], f32)
    Wg = np.asarray(inputs["L1_ffn_W_g"], f32)
    Wu = np.asarray(inputs["L1_ffn_W_u"], f32)
    Wd = np.asarray(inputs["L1_ffn_W_d"], f32)

    pm = _rope_perm()
    W0T = np.ascontiguousarray(W0.T)
    w0t = np.ascontiguousarray(
        W0T.reshape(NCH, 128, NCH, 128).transpose(2, 0, 1, 3))
    wg1 = np.ascontiguousarray(
        Wg.T.reshape(NCH, 128, 16, 512).transpose(2, 0, 1, 3))
    wu1 = np.ascontiguousarray(
        Wu.T.reshape(NCH, 128, 16, 512).transpose(2, 0, 1, 3))
    wd1 = np.ascontiguousarray(
        Wd.T.reshape(64, 128, NCH, 128).transpose(2, 0, 1, 3))
    Wq1T, Wk1T, Wv1T, Wo1T = (np.ascontiguousarray(w.T)
                              for w in (Wq1, Wk1, Wv1, Wo1))

    shards = []
    for c in range(8):
        pair, m = divmod(c, 2)
        heads = list(range(8 * m, 8 * m + 8))
        wq_cols = np.stack([Wqkv[:, 128 * h:128 * h + 128] for h in heads])
        wk_cols = np.stack([Wqkv[:, 2048 + 128 * h:2048 + 128 * h + 128]
                            for h in heads])
        wqkv_h = np.ascontiguousarray(
            np.concatenate([wq_cols, wk_cols], axis=2)
            .reshape(HPC, NCH, 128, 256))
        vcols = Wqkv[:, 4096 + 1024 * m:4096 + 1024 * m + 1024]
        wv0_t = np.ascontiguousarray(vcols.reshape(NCH, 128, 1024))
        bq = bqkv[1024 * m:1024 * m + 1024]
        bk = bqkv[2048 + 1024 * m:2048 + 1024 * m + 1024]
        bvv = bqkv[4096 + 1024 * m:4096 + 1024 * m + 1024]
        bqkv_t = np.concatenate([bq.reshape(8, 128).T, bk.reshape(8, 128).T,
                                 bvv.reshape(8, 128).T], axis=1)
        wo0_m = Wo0[1024 * m:1024 * m + 1024, :]
        wo0_t = np.ascontiguousarray(
            wo0_m.reshape(HPC, 128, NCH, 128).transpose(2, 0, 1, 3))
        wq1l, wk1l, bq1l, bk1l = [], [], [], []
        for h in heads:
            cols = 128 * h + pm
            wq1l.append(Wq1T[:, cols])
            wk1l.append(Wk1T[:, cols])
            bq1l.append(bq1[cols])
            bk1l.append(bk1[cols])
        wqk1_t = np.ascontiguousarray(
            np.concatenate([np.stack(wq1l), np.stack(wk1l)], axis=2)
            .reshape(HPC, NCH, 128, 256))
        bqk1_t = np.concatenate(
            [np.stack(bq1l).reshape(8, 128).T,
             np.stack(bk1l).reshape(8, 128).T], axis=1)
        wv1_m = Wv1T[:, 1024 * m:1024 * m + 1024]
        wv1_t = np.ascontiguousarray(wv1_m.reshape(NCH, 128, 1024))
        bv1_t = np.ascontiguousarray(
            bv1[1024 * m:1024 * m + 1024].reshape(8, 128).T)
        wo1_m = Wo1T[1024 * m:1024 * m + 1024, :]
        wo1_t = np.ascontiguousarray(
            wo1_m.reshape(HPC, 128, NCH, 128).transpose(2, 0, 1, 3))
        msel_t = np.zeros((128, 2), f32)
        msel_t[:, m] = 1.0
        shards.append({
            "xT": np.ascontiguousarray(x[pair].T),
            "wqkv": wqkv_h, "wv0": wv0_t,
            "bqkv": np.ascontiguousarray(bqkv_t),
            "wo0": wo0_t, "bo0": np.ascontiguousarray(bo0.reshape(16, 128).T),
            "w0t": w0t, "g0": np.ascontiguousarray(g0.reshape(16, 128).T),
            "wqk1": wqk1_t, "wv1": wv1_t,
            "bqk1": np.ascontiguousarray(bqk1_t), "bv1": bv1_t,
            "wo1": wo1_t, "bo1": np.ascontiguousarray(bo1.reshape(16, 128).T),
            "wg1": wg1, "wu1": wu1, "wd1": wd1,
            "msel": msel_t,
        })
    return shards


def kernel(**inputs):
    if "nc" not in _CACHE:
        _CACHE["nc"] = _build()
    nc = _CACHE["nc"]
    shards = _shard(inputs)
    res = run_bass_kernel_spmd(nc, shards, core_ids=list(range(8)))
    out = np.zeros((4, TALL, D), np.float32)
    for c in range(8):
        pair, m = divmod(c, 2)
        out[pair, 1024 * m:1024 * m + 1024, :] = res.results[c]["out_own"].T
    if DBG:
        kernel.dbg = [{k: v for k, v in r.items() if k.startswith("dbg_")}
                      for r in res.results]
    return out
